# version 15
# baseline (speedup 1.0000x reference)
import numpy as np
import ml_dtypes
from contextlib import ExitStack

import concourse.bass as bass
import concourse.mybir as mybir
from concourse.bass_utils import run_bass_kernel_spmd
from concourse.alu_op_type import AluOpType as Op

F32 = mybir.dt.float32
BF16 = mybir.dt.bfloat16
AF = mybir.ActivationFunctionType
AX = mybir.AxisListType
BF = ml_dtypes.bfloat16

P = 128
D = 1024
B = 4
SEQ = 4096
NCORES = 8
SLOC = SEQ // NCORES
NT = B * (SLOC // P)
NH = 16
HD = 64
WINLEN = 512
KTOK = SLOC + WINLEN
EPS = 1e-5



class Sems:

    def __init__(self, nc, st, names):
        self.h = {}
        self.n = {}
        for nm in names:
            self.h[nm] = st.enter_context(nc.semaphore(nm))
            self.n[nm] = 0

    def inc(self, instr, nm, k=1):
        instr.then_inc(self.h[nm], k)
        self.n[nm] += k
        return self.n[nm]

    def dma(self, instr, nm="dma"):
        return self.inc(instr, nm, 16)

    def val(self, nm):
        return self.n[nm]

    def wait(self, eng, nm, v):
        eng.wait_ge(self.h[nm], v)


class VP:

    _PASS = {"wait_ge", "drain", "engine_nop"}

    def __init__(self, v):
        self._v = v

    def __getattr__(self, name):
        fn = getattr(self._v, name)
        if name in VP._PASS or not callable(fn):
            return fn

        def wrapped(*a, **k):
            r = fn(*a, **k)
            self._v.drain()
            return r
        return wrapped


def emit_ln_stats(nc, blk, sems, src_tiles, nfeat, stats, sqdump, epsc, tag):
    n = len(src_tiles)
    sums, sumsq = stats[:, 0, :n], stats[:, 1, :n]
    mean, var, rstd = stats[:, 2, :n], stats[:, 3, :n], stats[:, 4, :n]
    sv, ss = f"v_{tag}", f"s_{tag}"
    v0, s0 = sems.val(sv), sems.val(ss)

    @blk.vector
    def _(v):
        v = VP(v)
        for i, srct in enumerate(src_tiles):
            instr = v.tensor_reduce(sums[:, i:i + 1], srct, AX.X, Op.add)
        sems.inc(instr, sv)

    @blk.scalar
    def _(sc):
        for i, srct in enumerate(src_tiles):
            instr = sc.activation(sqdump, srct, AF.Square,
                                  accum_out=sumsq[:, i:i + 1])
            sc.drain()
        sems.inc(instr, ss)

    @blk.vector
    def _(v):
        v = VP(v)
        sems.wait(v, ss, s0 + 1)
        v.tensor_scalar(mean, sums, 1.0 / nfeat, None, Op.mult)
        v.tensor_scalar(var, sumsq, 1.0 / nfeat, None, Op.mult)
        v.tensor_tensor(sums, mean, mean, Op.mult)
        instr = v.tensor_tensor(var, var, sums, Op.subtract)
        sems.inc(instr, sv)

    @blk.scalar
    def _(sc):
        sems.wait(sc, sv, v0 + 2)
        sc.activation(rstd, var, AF.Ln, bias=epsc)
        instr = sc.activation(rstd, rstd, AF.Exp, scale=-0.5)
        sems.inc(instr, ss)

    return mean, rstd



def emit_phase1(nc, st, env, xh_ap):
    xbuf, bufA, bufB = env["xbuf"], env["bufA"], env["bufB"]
    ones_row, id_bf = env["ones_row"], env["id_bf"]
    epsc = env["epsc"][:, :]
    w = env["w1"]
    sems = Sems(nc, st, ["p1dma", "p1ds", "p1w", "v_p1", "s_p1", "t_p1"])
    SV, SS, ST, SD = "v_p1", "s_p1", "t_p1", "p1dma"

    mask_sb = st.enter_context(nc.sbuf_tensor("sb_p1_mask", [P, 4, 640], BF16))
    cosq = st.enter_context(nc.sbuf_tensor("sb_p1_cq", [P, 4, 64], F32))
    sinq = st.enter_context(nc.sbuf_tensor("sb_p1_sq", [P, 4, 64], F32))
    cosk = st.enter_context(nc.sbuf_tensor("sb_p1_ck", [P, 8, 64], F32))
    sink = st.enter_context(nc.sbuf_tensor("sb_p1_sk", [P, 8, 64], F32))
    bprojr = st.enter_context(nc.sbuf_tensor("sb_p1_bpr", [1, D], BF16))
    wq2 = [st.enter_context(nc.sbuf_tensor(f"sb_p1_wq{i}", [P, 8, 512], BF16))
           for i in range(2)]
    qT = st.enter_context(nc.sbuf_tensor("sb_p1_qT", [P, 8, 512], BF16))
    yfm = st.enter_context(nc.sbuf_tensor("sb_p1_yfm", [P, 8, 512], BF16))
    hstage = st.enter_context(nc.sbuf_tensor("sb_p1_hst", [P, 4, D], F32))
    qrope = st.enter_context(nc.sbuf_tensor("sb_p1_qr", [P, 2, 512], BF16))
    scrA = st.enter_context(nc.sbuf_tensor("sb_p1_sA", [P, 512], F32))
    scrB = st.enter_context(nc.sbuf_tensor("sb_p1_sB", [P, 512], F32))
    Pn = st.enter_context(nc.sbuf_tensor("sb_p1_Pn", [P, 4, 640], BF16))
    PTs = st.enter_context(nc.sbuf_tensor("sb_p1_PT", [P, 2, 640], BF16))
    stats = st.enter_context(nc.sbuf_tensor("sb_p1_stats", [P, 5, 8], F32))
    cols = st.enter_context(nc.sbuf_tensor("sb_p1_cols", [P, 8], F32))
    fmb = bufB[:, :, 0:1024]
    kT = bufB[:, :, 1024:2048]
    vbv = bufA[:, 8:16, :]

    with nc.Block() as blk:
        @blk.gpsimd
        def _(g):
            sems.inc(g.dma_start(mask_sb[:, :, :],
                                 w["mask1"].rearrange("g p n -> p g n")), SD, 16)
            sems.inc(g.dma_start(bprojr[:, :], w["bprojr"]), SD, 16)
            g.wait_ge(sems.h[SD], sems.val(SD))

        @blk.vector
        def _(v):
            v.memset(cols[:, 6:7], -16.0)

        @blk.sync
        def _(s):
            sems.inc(s.dma_start(cosq[:, :, :], w["ropeq"][0].rearrange("g p n -> p g n")), "p1ds", 16)
            sems.inc(s.dma_start(sinq[:, :, :], w["ropeq"][1].rearrange("g p n -> p g n")), "p1ds", 16)
            sems.inc(s.dma_start(cosk[:, :, :], w["ropek"][0].rearrange("g p n -> p g n")), "p1ds", 16)
            sems.inc(s.dma_start(sink[:, :, :], w["ropek"][1].rearrange("g p n -> p g n")), "p1ds", 16)
            s.wait_ge(sems.h["p1ds"], sems.val("p1ds"))

    for b in range(B):
        with nc.Block() as blk:
            x0 = sems.val("p1ds")

            @blk.sync
            def _(s):
                sems.inc(s.dma_start(
                    hstage[:, :, :],
                    xh_ap.rearrange("t p d -> p t d")[:, 4 * b:4 * b + 4, :]), "p1ds", 16)

            @blk.vector
            def _(v):
                v.wait_ge(sems.h["p1ds"], x0 + 16)

            @blk.scalar
            def _(sc):
                sc.wait_ge(sems.h["p1ds"], x0 + 16)

            tiles8 = [xbuf[:, 4 * b + i, :] for i in range(4)] + \
                     [hstage[:, j, :] for j in range(4)]
            meanl, rstdl = emit_ln_stats(nc, blk, sems, tiles8, D,
                                         stats, bufA[:, 8, :], epsc, "p1")

        with nc.Block() as blk:
            @blk.vector
            def _(v):
                v = VP(v)
                for i in range(8):
                    srct = (xbuf[:, 4 * b + i, :] if i < 4 else hstage[:, i - 4, :])
                    v.tensor_scalar(bufA[:, i, :], srct, meanl[:, i:i + 1],
                                    rstdl[:, i:i + 1], Op.subtract, Op.mult)

        with ExitStack() as pst, nc.Block() as blk:
            psT = [pst.enter_context(nc.psum_tensor(f"ps_p1_T{i}_b{b}", [P, D], BF16))
                   for i in range(2)]
            t0, v0 = sems.val(ST), sems.val(SV)

            @blk.tensor
            def _(te):
                for i in range(8):
                    if i >= 2:
                        sems.wait(te, SV, v0 + i - 1)
                    for c in range(8):
                        instr = te.transpose(psT[i % 2][:, c * P:(c + 1) * P],
                                             bufA[:, i, c * P:(c + 1) * P], id_bf[:, :])
                    sems.inc(instr, ST)

            @blk.vector
            def _(v):
                v = VP(v)
                for i in range(8):
                    col = 512 + 128 * i if i < 4 else 128 * (i - 4)
                    sems.wait(v, ST, t0 + i + 1)
                    instr = v.tensor_copy(
                        fmb[:, :, col:col + 128],
                        psT[i % 2][:, :].rearrange("p (c n) -> p c n", n=P))
                    sems.inc(instr, SV)

        with ExitStack() as pst, nc.Block() as blk:
            psq = [pst.enter_context(nc.psum_tensor(f"ps_p1_q{i}_b{b}", [P, 512], F32))
                   for i in range(2)]
            pstr = [pst.enter_context(nc.psum_tensor(f"ps_p1_r{i}_b{b}", [P, 1024], BF16))
                    for i in range(2)]
            t0, v0, w0 = sems.val(ST), sems.val(SV), sems.val("p1w")
            sched = []
            for ns in range(6):
                for ti in range(4 if ns < 2 else 8):
                    sched.append((ns, ti))
            NJ = len(sched)
            tmm, ttr = {}, {}
            tc = t0 + 1
            tmm[0] = tc
            for j in range(NJ):
                if j + 1 < NJ:
                    tc += 1
                    tmm[j + 1] = tc
                if sched[j][0] < 4:
                    tc += 1
                    ttr[j] = tc
            vps, vrp, vtc = {}, {}, {}
            vc = v0
            for j in range(NJ):
                if sched[j][0] < 4:
                    vc += 1; vps[j] = vc
                    vc += 1; vrp[j] = vc
                else:
                    vc += 1; vps[j] = vc
                if j >= 1 and sched[j - 1][0] < 4:
                    vc += 1; vtc[j - 1] = vc
            if sched[NJ - 1][0] < 4:
                vc += 1; vtc[NJ - 1] = vc

            @blk.gpsimd
            def _(g):
                for ns in range(6):
                    if ns >= 2:
                        last = max(j for j, (n2, _) in enumerate(sched) if n2 == ns - 2)
                        sems.wait(g, ST, tmm[last])
                    src = w["WqkvT"][:, ns * 512:(ns + 1) * 512]
                    sems.inc(g.dma_start(wq2[ns % 2][:, :, :],
                                         src.rearrange("(c p) n -> p c n", p=P)),
                             "p1w", 16)

            def emit_mm(te, j):
                ns, ti = sched[j]
                sems.wait(te, "p1w", w0 + 16 * (ns + 1))
                if j >= 2:
                    sems.wait(te, SV, vps[j - 2])
                col = 512 + 128 * ti if ns < 2 else 128 * ti
                for c in range(8):
                    instr = te.matmul(psq[j % 2][:, :], fmb[:, c, col:col + 128],
                                      wq2[ns % 2][:, c, :],
                                      start=(c == 0), stop=(c == 7))
                sems.inc(instr, ST)

            @blk.tensor
            def _(te):
                emit_mm(te, 0)
                for j, (ns, ti) in enumerate(sched):
                    if j + 1 < NJ:
                        emit_mm(te, j + 1)
                    if ns < 4:
                        sems.wait(te, SV, vrp[j])
                        if j >= 2 and sched[j - 2][0] < 4:
                            sems.wait(te, SV, vtc[j - 2])
                        for i in range(8):
                            par, slot = i % 2, i // 2
                            instr = te.transpose(
                                pstr[j % 2][64 * par:64 * par + 64, slot * 128:(slot + 1) * 128],
                                qrope[:, j % 2, i * 64:(i + 1) * 64], id_bf[:, :])
                        sems.inc(instr, ST)

            def emit_copy(v, j):
                ns, ti = sched[j]
                sems.wait(v, ST, ttr[j])
                if ns < 2:
                    dst = qT[:, 4 * ns:4 * ns + 4, ti * 128:(ti + 1) * 128]
                else:
                    dst = kT[:, 4 * (ns - 2):4 * (ns - 2) + 4, ti * 128:(ti + 1) * 128]
                instr = v.tensor_copy(
                    dst, pstr[j % 2][:, 0:512].rearrange("p (s n) -> p s n", n=P))
                sems.inc(instr, SV)

            @blk.vector
            def _(v):
                v = VP(v)
                for j, (ns, ti) in enumerate(sched):
                    sems.wait(v, ST, tmm[j])
                    if ns < 4:
                        if ns < 2:
                            cosv, sinv, gidx = cosq, sinq, ti
                        else:
                            cosv, sinv, gidx = cosk, sink, ti
                        pv = psq[j % 2][:, :].rearrange("p (h d) -> p h d", d=64)
                        pvs = psq[j % 2][:, :].rearrange("p (h a c) -> p h a c", a=32, c=2)[
                            :, :, :, ::-1]
                        cb = cosv[:, gidx, :].unsqueeze(1).broadcast_to([P, 8, 64])
                        sb = sinv[:, gidx, :].rearrange("p (a c) -> p a c", c=2).unsqueeze(
                            1).broadcast_to([P, 8, 32, 2])
                        v.tensor_tensor(scrA[:, :].rearrange("p (h d) -> p h d", d=64),
                                        pv, cb, Op.mult)
                        instr = v.tensor_tensor(
                            scrB[:, :].rearrange("p (h a c) -> p h a c", a=32, c=2),
                            pvs, sb, Op.mult)
                        sems.inc(instr, SV)
                        if j >= 2 and sched[j - 2][0] < 4:
                            sems.wait(v, ST, ttr[j - 2])
                        instr = v.tensor_tensor(qrope[:, j % 2, :], scrA[:, :],
                                                scrB[:, :], Op.add)
                        sems.inc(instr, SV)
                    else:
                        instr = v.tensor_copy(vbv[:, ti, (ns - 4) * 512:(ns - 3) * 512],
                                              psq[j % 2][:, :])
                        sems.inc(instr, SV)
                    if j >= 1 and sched[j - 1][0] < 4:
                        emit_copy(v, j - 1)
                if sched[NJ - 1][0] < 4:
                    emit_copy(v, NJ - 1)

        with ExitStack() as pst, nc.Block() as blk:
            pss = [pst.enter_context(nc.psum_tensor(f"ps_p1_s{i}_b{b}", [P, D], F32))
                   for i in range(2)]
            pstr2 = [pst.enter_context(nc.psum_tensor(f"ps_p1_t{i}_b{b}", [P, 1024], BF16))
                     for i in range(2)]
            psyp = [pst.enter_context(nc.psum_tensor(f"ps_p1_y{i}_b{b}", [P, 512], F32))
                    for i in range(2)]
            t0, v0, s0 = sems.val(ST), sems.val(SV), sems.val(SS)
            NI = 64
            NSLOT = NI + 3
            _S, _T, _PV = {}, {}, {}
            _EX, _PT, _YPC = {}, {}, {}
            _MK, _NM = {}, {}
            tc, vc, sc_ = t0, v0, s0
            for j in range(3):
                tc += 1; _S[j] = tc
            for i in range(NSLOT):
                if i + 3 < NI:
                    tc += 1; _S[i + 3] = tc
                if 0 <= i - 1 < NI:
                    tc += 1; _T[i - 1] = tc
                if 0 <= i - 2 < NI:
                    tc += 1; _PV[i - 2] = tc
            for j in range(2):
                sc_ += 1; _EX[j] = sc_
            for i in range(NSLOT):
                if i + 2 < NI:
                    sc_ += 1; _EX[i + 2] = sc_
                if 0 <= i - 1 < NI:
                    sc_ += 1; _PT[i - 1] = sc_
                if i >= 10 and (i - 10) % 8 == 0 and (i - 10) // 8 < 8:
                    sc_ += 1; _YPC[(i - 10) // 8] = sc_
            vc += 1; _MK[0] = vc
            for i in range(NSLOT):
                if i + 1 < NI:
                    vc += 1; _MK[i + 1] = vc
                if 0 <= i < NI:
                    vc += 1; _NM[i] = vc

            def emit_S(te, j):
                h, g = j // 4, j % 4
                par, hp = h % 2, h // 2
                if j >= 2:
                    sems.wait(te, SS, _EX[j - 2])
                lt = qT[64 * par:64 * par + 64, hp, g * 128:(g + 1) * 128]
                kv = kT[64 * par:64 * par + 64, hp, :]
                te.matmul(pss[j % 2][:, 0:512], lt, kv[:, g * 128:g * 128 + 512],
                          start=True, stop=True)
                instr = te.matmul(pss[j % 2][:, 512:640], lt,
                                  kv[:, g * 128 + 512:g * 128 + 640],
                                  start=True, stop=True)
                sems.inc(instr, ST)

            @blk.tensor
            def _(te):
                for j in range(3):
                    emit_S(te, j)
                for i in range(NSLOT):
                    if i + 3 < NI:
                        emit_S(te, i + 3)
                    if 0 <= i - 1 < NI:
                        j = i - 1
                        sems.wait(te, SV, _NM[j])
                        if j >= 2:
                            sems.wait(te, SS, _PT[j - 2])
                        for kc in range(5):
                            instr = te.transpose(pstr2[j % 2][:, kc * 128:(kc + 1) * 128],
                                                 Pn[:, j % 4, kc * 128:(kc + 1) * 128],
                                                 id_bf[:, :])
                        sems.inc(instr, ST)
                    if 0 <= i - 2 < NI:
                        j = i - 2
                        h, g = j // 4, j % 4
                        par, hp = h % 2, h // 2
                        sems.wait(te, SS, _PT[j])
                        if j % 8 == 0 and hp >= 2:
                            sems.wait(te, SS, _YPC[hp - 2])
                        for kc in range(5):
                            instr = te.matmul(
                                psyp[hp % 2][64 * par:64 * par + 64, g * 128:(g + 1) * 128],
                                vbv[:, g + kc, h * 64:(h + 1) * 64],
                                PTs[:, j % 2, kc * 128:(kc + 1) * 128],
                                start=(kc == 0), stop=(kc == 4))
                        sems.inc(instr, ST)

            @blk.vector
            def _(v):
                v = VP(v)

                def emit_stt(j):
                    g = j % 4
                    sems.wait(v, SS, _EX[j])
                    instr = v.scalar_tensor_tensor(
                        Pn[:, j % 4, :], Pn[:, j % 4, :], 1.0, mask_sb[:, g, :],
                        Op.mult, Op.mult,
                        accum_out=cols[:, 2 + (j % 2):3 + (j % 2)])
                    sems.inc(instr, SV)

                emit_stt(0)
                for i in range(NSLOT):
                    if i + 1 < NI:
                        emit_stt(i + 1)
                    if 0 <= i < NI:
                        v.reciprocal(cols[:, 4 + (i % 2):5 + (i % 2)],
                                     cols[:, 2 + (i % 2):3 + (i % 2)])
                        instr = v.tensor_scalar(Pn[:, i % 4, :], Pn[:, i % 4, :],
                                                cols[:, 4 + (i % 2):5 + (i % 2)],
                                                None, Op.mult)
                        sems.inc(instr, SV)

            @blk.scalar
            def _(sc):
                def emit_exp(j):
                    sems.wait(sc, ST, _S[j])
                    if j >= 4:
                        sems.wait(sc, ST, _T[j - 4])
                    instr = sc.activation(Pn[:, j % 4, :], pss[j % 2][:, 0:640],
                                          AF.Exp, bias=cols[:, 6:7])
                    sems.inc(instr, SS)

                emit_exp(0)
                emit_exp(1)
                for i in range(NSLOT):
                    if i + 2 < NI:
                        emit_exp(i + 2)
                    if 0 <= i - 1 < NI:
                        j = i - 1
                        sems.wait(sc, ST, _T[j])
                        if j >= 2:
                            sems.wait(sc, ST, _PV[j - 2])
                        instr = sc.copy(PTs[:, j % 2, :], pstr2[j % 2][:, 0:640])
                        sems.inc(instr, SS)
                    if i >= 10 and (i - 10) % 8 == 0 and (i - 10) // 8 < 8:
                        hp = (i - 10) // 8
                        sems.wait(sc, ST, _PV[8 * hp + 7])
                        instr = sc.copy(yfm[:, hp, :], psyp[hp % 2][:, :])
                        sems.inc(instr, SS)

        with ExitStack() as pst, nc.Block() as blk:
            pso = [pst.enter_context(nc.psum_tensor(f"ps_p1_o{i}_b{b}", [P, D], F32))
                   for i in range(2)]
            t0, w0 = sems.val(ST), sems.val("p1w")
            mm_vals = {g: t0 + g + 1 for g in range(4)}
            scrOv = bufA[:, 10:16, :].rearrange("p a b -> p (a b)").bitcast(
                F32).rearrange("p (a b) -> p a b", b=D)
            _EV = emit_postln_pipe(nc, blk, sems, "p1", mm_vals, pso,
                                   [4 * b + g for g in range(4)],
                                   xbuf, scrOv, bufA[:, 8, :], stats, epsc, 4)

            @blk.gpsimd
            def _(g):
                for half in range(2):
                    src = w["WprojT"][:, half * 512:(half + 1) * 512]
                    sems.inc(g.dma_start(wq2[half][:, :, :],
                                         src.rearrange("(c p) n -> p c n", p=P)),
                             "p1w", 16)

            @blk.tensor
            def _(te):
                sems.wait(te, "p1w", w0 + 32)
                for g in range(4):
                    if g >= 2:
                        sems.wait(te, SV, _EV[g - 2])
                    for half in range(2):
                        for hp in range(8):
                            te.matmul(pso[g % 2][:, half * 512:(half + 1) * 512],
                                      yfm[:, hp, g * 128:(g + 1) * 128],
                                      wq2[half][:, hp, :],
                                      start=(hp == 0), stop=False)
                        instr = te.matmul(pso[g % 2][:, half * 512:(half + 1) * 512],
                                          ones_row[:, :],
                                          bprojr[:, half * 512:(half + 1) * 512],
                                          start=False, stop=True)
                    sems.inc(instr, ST)

    return sems



def emit_phase2(nc, st, env):
    xbuf, bufA, bufB = env["xbuf"], env["bufA"], env["bufB"]
    ones_row, id_bf = env["ones_row"], env["id_bf"]
    epsc = env["epsc"][:, :]
    w = env["w2"]
    sems = Sems(nc, st, ["p2dma", "p2ds", "v_p2", "s_p2", "t_p2"])
    SV, SS, ST, SD = "v_p2", "s_p2", "t_p2", "p2dma"

    Win_bf = st.enter_context(nc.sbuf_tensor("sb_p2_Win", [P, 8, 64], BF16))
    binr = st.enter_context(nc.sbuf_tensor("sb_p2_binr", [1, 64], BF16))
    Wout_bf = st.enter_context(nc.sbuf_tensor("sb_p2_Wout", [64, D], BF16))
    boutr = st.enter_context(nc.sbuf_tensor("sb_p2_boutr", [1, D], BF16))
    W1b = st.enter_context(nc.sbuf_tensor("sb_p2_W1b", [P, D], F32))
    b1b = st.enter_context(nc.sbuf_tensor("sb_p2_b1b", [P, D], F32))
    W2b = st.enter_context(nc.sbuf_tensor("sb_p2_W2b", [P, D], F32))
    b2b = st.enter_context(nc.sbuf_tensor("sb_p2_b2b", [P, 64], F32))
    s_f32 = st.enter_context(nc.sbuf_tensor("sb_p2_s", [P, NT, 64], F32))
    gbuf = st.enter_context(nc.sbuf_tensor("sb_p2_g", [P, NT, 64], BF16))
    gfm = st.enter_context(nc.sbuf_tensor("sb_p2_gfm", [64, NT, P], BF16))
    scrE1 = st.enter_context(nc.sbuf_tensor("sb_p2_scr1", [P, 2, D], F32))
    scrE2 = st.enter_context(nc.sbuf_tensor("sb_p2_scr2", [P, 2, D], BF16))
    stats = st.enter_context(nc.sbuf_tensor("sb_p2_stats", [P, 5, NT], F32))
    stats2 = st.enter_context(nc.sbuf_tensor("sb_p2_stats2", [P, 5, NT], F32))


    with nc.Block() as blk:
        @blk.gpsimd
        def _(g):
            sems.inc(g.dma_start(Win_bf[:, :, :], w["Win"].rearrange("(c p) n -> p c n", p=P)), SD, 16)
            sems.inc(g.dma_start(binr[:, :], w["binr"]), SD, 16)
            sems.inc(g.dma_start(Wout_bf[:, :], w["Wout"]), SD, 16)
            sems.inc(g.dma_start(boutr[:, :], w["boutr"]), SD, 16)

        @blk.sync
        def _(s):
            sems.inc(s.dma_start(W1b[:, :], w["W1b"]), "p2ds", 16)
            sems.inc(s.dma_start(b1b[:, :], w["b1b"]), "p2ds", 16)
            sems.inc(s.dma_start(W2b[:, :], w["W2b"]), "p2ds", 16)
            sems.inc(s.dma_start(b2b[:, :], w["b2b"]), "p2ds", 16)
            s.wait_ge(sems.h[SD], sems.val(SD))
            s.wait_ge(sems.h["p2ds"], sems.val("p2ds"))

        @blk.gpsimd
        def _(g):
            g.wait_ge(sems.h[SD], sems.val(SD))
            g.wait_ge(sems.h["p2ds"], sems.val("p2ds"))

    with nc.Block() as blk:
        mean, rstd = emit_ln_stats(nc, blk, sems,
                                   [xbuf[:, t, :] for t in range(NT)], D,
                                   stats, bufA[:, 0, :], epsc, "p2")

    with nc.Block() as blk:
        @blk.vector
        def _(v):
            v = VP(v)
            for t in range(NT):
                instr = v.tensor_scalar(bufA[:, t, :], xbuf[:, t, :],
                                        mean[:, t:t + 1], rstd[:, t:t + 1],
                                        Op.subtract, Op.mult)

    with ExitStack() as pst, nc.Block() as blk:
        psT = [pst.enter_context(nc.psum_tensor(f"sb_p2_psT{i}", [P, D], BF16)) for i in range(2)]
        t0, v0 = sems.val(ST), sems.val(SV)

        @blk.tensor
        def _(te):
            for t in range(NT):
                if t >= 2:
                    sems.wait(te, SV, v0 + t - 1)
                for c in range(8):
                    instr = te.transpose(psT[t % 2][:, c * P:(c + 1) * P],
                                         bufA[:, t, c * P:(c + 1) * P], id_bf[:, :])
                sems.inc(instr, ST)

        @blk.vector
        def _(v):
            v = VP(v)
            for t in range(NT):
                sems.wait(v, ST, t0 + t + 1)
                instr = v.tensor_copy(
                    bufB[:, :, t * P:(t + 1) * P],
                    psT[t % 2][:, :].rearrange("p (c n) -> p c n", n=P))
                sems.inc(instr, SV)

    with ExitStack() as pst, nc.Block() as blk:
        psS = [pst.enter_context(nc.psum_tensor(f"sb_p2_psS{i}", [P, 512], F32)) for i in range(2)]
        t0, v0 = sems.val(ST), sems.val(SV)

        @blk.tensor
        def _(te):
            for t in range(NT):
                if t >= 2:
                    sems.wait(te, SV, v0 + t - 1)
                for c in range(8):
                    te.matmul(psS[t % 2][:, 0:64], bufB[:, c, t * P:(t + 1) * P],
                              Win_bf[:, c, :], start=(c == 0), stop=False)
                instr = te.matmul(psS[t % 2][:, 0:64], ones_row[:, :], binr[:, :],
                                  start=False, stop=True)
                sems.inc(instr, ST)

        @blk.vector
        def _(v):
            v = VP(v)
            for t in range(NT):
                sems.wait(v, ST, t0 + t + 1)
                instr = v.tensor_copy(s_f32[:, t, :], psS[t % 2][:, 0:64])
                sems.inc(instr, SV)

    with nc.Block() as blk:
        v0, s0 = sems.val(SV), sems.val(SS)
        W1v = W1b[:, :].rearrange("p (m i) -> p m i", i=16)
        _M1, _M2, _RL = {}, {}, {}
        vc, sc_ = v0, s0
        for i in range(NT + 2):
            if 0 <= i - 2 < NT:
                vc += 1; _M2[i - 2] = vc
            if 0 <= i < NT:
                vc += 1; _M1[i] = vc
        for i in range(NT + 2):
            if 0 <= i - 1 < NT:
                sc_ += 1; _RL[i - 1] = sc_

        @blk.vector
        def _(v):
            v = VP(v)
            for i in range(NT + 2):
                if 0 <= i - 2 < NT:
                    t = i - 2
                    pa = t % 2
                    sems.wait(v, SS, _RL[t])
                    v.tensor_tensor(scrE1[:, pa, :], scrE2[:, pa, :], W2b[:, :], Op.mult)
                    v.tensor_reduce(s_f32[:, t, :], scrE1[:, pa, :].rearrange(
                        "p (m i) -> p m i", i=16), AX.X, Op.add)
                    instr = v.tensor_tensor(gbuf[:, t, :], s_f32[:, t, :], b2b[:, :], Op.add)
                    sems.inc(instr, SV)
                if 0 <= i < NT:
                    t = i
                    pa = t % 2
                    sv = s_f32[:, t, :].rearrange("p (m o) -> p m o", o=1)
                    v.tensor_tensor(scrE1[:, pa, :].rearrange("p (m i) -> p m i", i=16),
                                    sv.broadcast_to([P, 64, 16]), W1v, Op.mult)
                    instr = v.tensor_tensor(scrE1[:, pa, :], scrE1[:, pa, :],
                                            b1b[:, :], Op.add)
                    sems.inc(instr, SV)

        @blk.scalar
        def _(sc):
            for i in range(NT + 2):
                if 0 <= i - 1 < NT:
                    t = i - 1
                    pa = t % 2
                    sems.wait(sc, SV, _M1[t])
                    instr = sc.activation(scrE2[:, pa, :], scrE1[:, pa, :], AF.Relu)
                    sems.inc(instr, SS)


    with ExitStack() as pst, nc.Block() as blk:
        psG = [pst.enter_context(nc.psum_tensor(f"sb_p2_psG{i}", [64, 1024], BF16)) for i in range(2)]
        t0, v0 = sems.val(ST), sems.val(SV)

        @blk.tensor
        def _(te):
            for bank in range(2):
                for slot in range(8):
                    t = bank * 8 + slot
                    instr = te.transpose(psG[bank][:, slot * P:(slot + 1) * P],
                                         gbuf[:, t, :], id_bf[:, :])
                sems.inc(instr, ST)

        @blk.vector
        def _(v):
            v = VP(v)
            for bank in range(2):
                sems.wait(v, ST, t0 + bank + 1)
                instr = v.tensor_copy(
                    gfm[:, bank * 8:(bank + 1) * 8, :],
                    psG[bank][:, :].rearrange("p (s n) -> p s n", n=P))
                sems.inc(instr, SV)

    with ExitStack() as pst, nc.Block() as blk:
        psO = [pst.enter_context(nc.psum_tensor(f"sb_p2_psO{i}", [P, D], F32)) for i in range(2)]
        scr3 = pst.enter_context(nc.sbuf_tensor("sb_p2_scr3", [P, 3, D], F32))
        t0 = sems.val(ST)
        mm_vals = {t: t0 + t + 1 for t in range(NT)}
        _EV = emit_postln_pipe(nc, blk, sems, "p2", mm_vals, psO, list(range(NT)),
                               xbuf, scr3, bufA[:, 0, :], stats2, epsc, NT)

        @blk.tensor
        def _(te):
            for t in range(NT):
                if t >= 2:
                    sems.wait(te, SV, _EV[t - 2])
                for half in range(2):
                    te.matmul(psO[t % 2][:, half * 512:(half + 1) * 512],
                              gfm[:, t, :], Wout_bf[:, half * 512:(half + 1) * 512],
                              start=True, stop=False)
                    instr = te.matmul(psO[t % 2][:, half * 512:(half + 1) * 512],
                                      ones_row[:, :], boutr[:, half * 512:(half + 1) * 512],
                                      start=False, stop=True)
                sems.inc(instr, ST)



def emit_ln_to_fm(nc, env, sems, stats, tag):
    xbuf, bufA, bufB = env["xbuf"], env["bufA"], env["bufB"]
    id_bf = env["id_bf"]
    epsc = env["epsc"][:, :]
    SV, SS, ST = f"v_{tag}", f"s_{tag}", f"t_{tag}"
    sums, sumsq = stats[:, 0, :], stats[:, 1, :]
    mean, var, rstd = stats[:, 2, :], stats[:, 3, :], stats[:, 4, :]

    with ExitStack() as pst, nc.Block() as blk:
        psT = [pst.enter_context(nc.psum_tensor(f"ps_{tag}_lnT{i}", [P, D], BF16))
               for i in range(2)]
        v0, s0, t0 = sems.val(SV), sems.val(SS), sems.val(ST)
        NS = NT + 7
        _RD, _VA, _APL, _CP, _SQ, _RS, _TR = {}, {}, {}, {}, {}, {}, {}
        vc, sc_, tc = v0, s0, t0
        for i in range(NS):
            if 0 <= i < NT:
                vc += 1; _RD[i] = vc
            if 0 <= i - 2 < NT:
                vc += 1; _VA[i - 2] = vc
            if 0 <= i - 4 < NT:
                vc += 1; _APL[i - 4] = vc
            if 0 <= i - 6 < NT:
                vc += 1; _CP[i - 6] = vc
        for i in range(NS):
            if 0 <= i - 1 < NT:
                sc_ += 1; _SQ[i - 1] = sc_
            if 0 <= i - 3 < NT:
                sc_ += 1; _RS[i - 3] = sc_
        for i in range(NS):
            if 0 <= i - 5 < NT:
                tc += 1; _TR[i - 5] = tc

        @blk.vector
        def _(v):
            v = VP(v)
            for i in range(NS):
                if 0 <= i < NT:
                    t = i
                    instr = v.tensor_reduce(sums[:, t:t + 1], xbuf[:, t, :], AX.X, Op.add)
                    sems.inc(instr, SV)
                if 0 <= i - 2 < NT:
                    t = i - 2
                    sems.wait(v, SS, _SQ[t])
                    v.tensor_scalar(mean[:, t:t + 1], sums[:, t:t + 1], 1.0 / D, None, Op.mult)
                    v.tensor_scalar(var[:, t:t + 1], sumsq[:, t:t + 1], 1.0 / D, None, Op.mult)
                    v.tensor_tensor(sums[:, t:t + 1], mean[:, t:t + 1], mean[:, t:t + 1], Op.mult)
                    instr = v.tensor_tensor(var[:, t:t + 1], var[:, t:t + 1],
                                            sums[:, t:t + 1], Op.subtract)
                    sems.inc(instr, SV)
                if 0 <= i - 4 < NT:
                    t = i - 4
                    sems.wait(v, SS, _RS[t])
                    instr = v.tensor_scalar(bufA[:, t, :], xbuf[:, t, :],
                                            mean[:, t:t + 1], rstd[:, t:t + 1],
                                            Op.subtract, Op.mult)
                    sems.inc(instr, SV)
                if 0 <= i - 6 < NT:
                    t = i - 6
                    sems.wait(v, ST, _TR[t])
                    instr = v.tensor_copy(
                        bufB[:, :, t * P:(t + 1) * P],
                        psT[t % 2][:, :].rearrange("p (c n) -> p c n", n=P))
                    sems.inc(instr, SV)

        @blk.scalar
        def _(sc):
            for i in range(NS):
                if 0 <= i - 1 < NT:
                    t = i - 1
                    instr = sc.activation(bufA[:, t, :], xbuf[:, t, :], AF.Square,
                                          accum_out=sumsq[:, t:t + 1])
                    sc.drain()
                    sems.inc(instr, SS)
                if 0 <= i - 3 < NT:
                    t = i - 3
                    sems.wait(sc, SV, _VA[t])
                    sc.activation(rstd[:, t:t + 1], var[:, t:t + 1], AF.Ln, bias=epsc)
                    instr = sc.activation(rstd[:, t:t + 1], rstd[:, t:t + 1],
                                          AF.Exp, scale=-0.5)
                    sems.inc(instr, SS)

        @blk.tensor
        def _(te):
            for i in range(NS):
                if 0 <= i - 5 < NT:
                    t = i - 5
                    sems.wait(te, SV, _APL[t])
                    if t >= 2:
                        sems.wait(te, SV, _CP[t - 2])
                    for c in range(8):
                        instr = te.transpose(psT[t % 2][:, c * P:(c + 1) * P],
                                             bufA[:, t, c * P:(c + 1) * P], id_bf[:, :])
                    sems.inc(instr, ST)
    return mean, rstd



_TMFM_N = [0]


def emit_tm_to_fm(nc, sems, tag, src, dst, id_bf):
    SV, ST = f"v_{tag}", f"t_{tag}"
    _TMFM_N[0] += 1
    with ExitStack() as pst, nc.Block() as blk:
        psT = [pst.enter_context(nc.psum_tensor(f"ps_{tag}_T{i}_{_TMFM_N[0]}",
                                                [P, D], BF16)) for i in range(2)]
        t0, v0 = sems.val(ST), sems.val(SV)

        @blk.tensor
        def _(te):
            for t in range(NT):
                if t >= 2:
                    sems.wait(te, SV, v0 + t - 1)
                for c in range(8):
                    instr = te.transpose(psT[t % 2][:, c * P:(c + 1) * P],
                                         src[:, t, c * P:(c + 1) * P], id_bf[:, :])
                sems.inc(instr, ST)

        @blk.vector
        def _(v):
            v = VP(v)
            for t in range(NT):
                sems.wait(v, ST, t0 + t + 1)
                instr = v.tensor_copy(
                    dst[:, :, t * P:(t + 1) * P],
                    psT[t % 2][:, :].rearrange("p (c n) -> p c n", n=P))
                sems.inc(instr, SV)



def emit_phase3(nc, st, env):
    xbuf, bufA, bufB = env["xbuf"], env["bufA"], env["bufB"]
    ones_row, id_bf = env["ones_row"], env["id_bf"]
    epsc = env["epsc"][:, :]
    w = env["w3"]
    sems = Sems(nc, st, ["p3dma", "p3ds", "v_p3", "s_p3", "t_p3"])
    SV, SS, ST, SD = "v_p3", "s_p3", "t_p3", "p3dma"

    b1r = st.enter_context(nc.sbuf_tensor("sb_p3_b1r", [1, D], BF16))
    bwr = st.enter_context(nc.sbuf_tensor("sb_p3_bwr", [1, 496], BF16))
    b2r = st.enter_context(nc.sbuf_tensor("sb_p3_b2r", [1, D], BF16))
    h1buf = st.enter_context(nc.sbuf_tensor("sb_p3_h1", [P, NT, D], BF16))
    wsel = st.enter_context(nc.sbuf_tensor("sb_p3_wsel", [P, NT, 64], F32))
    wlscr = st.enter_context(nc.sbuf_tensor("sb_p3_wl", [P, 496], F32))
    escr = st.enter_context(nc.sbuf_tensor("sb_p3_e", [P, 496], F32))
    hstat = st.enter_context(nc.sbuf_tensor("sb_p3_hs", [P, 3, 16], F32))
    scr1 = st.enter_context(nc.sbuf_tensor("sb_p3_s1", [P, D], F32))
    scr2 = st.enter_context(nc.sbuf_tensor("sb_p3_s2", [P, D], F32))
    gb = env["bufB"][:, 2, :].bitcast(F32)
    bb = env["bufB"][:, 4, :].bitcast(F32)
    sqd = env["bufB"][:, 0, 0:D]
    stats = st.enter_context(nc.sbuf_tensor("sb_p3_stats", [P, 5, NT], F32))
    statsR = st.enter_context(nc.sbuf_tensor("sb_p3_statsR", [P, 5, NT], F32))
    statsA = st.enter_context(nc.sbuf_tensor("sb_p3_statsA", [P, 5, NT], F32))
    wst = ExitStack()
    w1T = wst.enter_context(nc.sbuf_tensor("sb_p3_w1T", [P, 8, D], BF16))
    wwT = wst.enter_context(nc.sbuf_tensor("sb_p3_wwT", [P, 8, 496], BF16))

    with nc.Block() as blk:
        @blk.gpsimd
        def _(g):
            sems.inc(g.dma_start(w1T[:, :, :], w["w1T"].rearrange("(c p) n -> p c n", p=P)), SD, 16)
            sems.inc(g.dma_start(wwT[:, :, :], w["wwT"].rearrange("(c p) n -> p c n", p=P)), SD, 16)
            sems.inc(g.dma_start(b1r[:, :], w["b1r"]), SD, 16)
            sems.inc(g.dma_start(bwr[:, :], w["bwr"]), SD, 16)
            sems.inc(g.dma_start(b2r[:, :], w["b2r"]), SD, 16)
            g.wait_ge(sems.h[SD], sems.val(SD))

        @blk.sync
        def _(s):
            s.wait_ge(sems.h[SD], sems.val(SD))

    mean3, rstd3 = emit_ln_to_fm(nc, env, sems, stats, "p3")

    with ExitStack() as pst, nc.Block() as blk:
        psH = [pst.enter_context(nc.psum_tensor(f"ps_p3_h{i}", [P, D], F32))
               for i in range(2)]
        t0, v0 = sems.val(ST), sems.val(SV)

        @blk.tensor
        def _(te):
            for t in range(NT):
                if t >= 2:
                    sems.wait(te, SV, v0 + t - 1)
                for half in range(2):
                    for c in range(8):
                        te.matmul(psH[t % 2][:, half * 512:(half + 1) * 512],
                                  bufB[:, c, t * P:(t + 1) * P],
                                  w1T[:, c, half * 512:(half + 1) * 512],
                                  start=(c == 0), stop=False)
                    instr = te.matmul(psH[t % 2][:, half * 512:(half + 1) * 512],
                                      ones_row[:, :], b1r[:, half * 512:(half + 1) * 512],
                                      start=False, stop=True)
                sems.inc(instr, ST)

        @blk.vector
        def _(v):
            v = VP(v)
            for t in range(NT):
                sems.wait(v, ST, t0 + t + 1)
                instr = v.tensor_copy(h1buf[:, t, :], psH[t % 2][:, :])
                sems.inc(instr, SV)

    emit_tm_to_fm(nc, sems, "p3", h1buf, bufB, id_bf)

    with ExitStack() as pst, nc.Block() as blk:
        psW = [pst.enter_context(nc.psum_tensor(f"ps_p3_w{i}", [P, 512], F32))
               for i in range(2)]
        t0, v0, s0 = sems.val(ST), sems.val(SV), sems.val(SS)

        @blk.tensor
        def _(te):
            for t in range(NT):
                if t >= 2:
                    sems.wait(te, SV, v0 + 3 * (t - 2) + 1)
                for c in range(8):
                    te.matmul(psW[t % 2][:, 0:496], bufB[:, c, t * P:(t + 1) * P],
                              wwT[:, c, :], start=(c == 0), stop=False)
                instr = te.matmul(psW[t % 2][:, 0:496], ones_row[:, :], bwr[:, :],
                                  start=False, stop=True)
                sems.inc(instr, ST)

        @blk.vector
        def _(v):
            v = VP(v)
            for t in range(NT):
                sems.wait(v, ST, t0 + t + 1)
                v.tensor_copy(wlscr[:, :], psW[t % 2][:, :496])
                instr = v.tensor_reduce(
                    hstat[:, 0, :], wlscr[:, :].rearrange("p (h k) -> p h k", k=31),
                    AX.X, Op.max, negate=True)
                sems.inc(instr, SV)
                nm = hstat[:, 0, :].rearrange("p (h o) -> p h o", o=1)
                instr = v.tensor_tensor(
                    wlscr[:, :].rearrange("p (h k) -> p h k", k=31),
                    wlscr[:, :].rearrange("p (h k) -> p h k", k=31),
                    nm.broadcast_to([P, 16, 31]), Op.add)
                sems.inc(instr, SV)
                sems.wait(v, SS, s0 + t + 1)
                v.tensor_reduce(hstat[:, 1, :],
                                escr[:, :].rearrange("p (h k) -> p h k", k=31),
                                AX.X, Op.add)
                v.reciprocal(hstat[:, 2, :], hstat[:, 1, :])
                rs = hstat[:, 2, :].rearrange("p (h o) -> p h o", o=1)
                ev = escr[:, :].rearrange("p (h k) -> p h k", k=31)[:, :, 27:31]
                instr = v.tensor_tensor(
                    wsel[:, t, :].rearrange("p (h k) -> p h k", k=4),
                    ev, rs.broadcast_to([P, 16, 4]), Op.mult)
                sems.inc(instr, SV)
                tb, gph = t // 4, t % 4
                acc = scr1[:, :]
                for j in range(tb + 1):
                    wv = wsel[:, t, :].rearrange("p (h k) -> p h k", k=4)[
                        :, :, 3 - tb + j:4 - tb + j]
                    hv = h1buf[:, j * 4 + gph, :].rearrange("p (h r) -> p h r", r=64)
                    if j == 0:
                        v.tensor_tensor(acc.rearrange("p (h r) -> p h r", r=64),
                                        hv, wv.broadcast_to([P, 16, 64]), Op.mult)
                    else:
                        v.tensor_tensor(scr2[:, :].rearrange("p (h r) -> p h r", r=64),
                                        hv, wv.broadcast_to([P, 16, 64]), Op.mult)
                        v.tensor_tensor(acc, acc, scr2[:, :], Op.add)
                v.tensor_copy(bufA[:, t, :], acc)

        @blk.scalar
        def _(sc):
            for t in range(NT):
                sems.wait(sc, SV, v0 + 3 * t + 2)
                instr = sc.activation(escr[:, :], wlscr[:, :], AF.Exp)
                sems.inc(instr, SS)

    wst.close()
    w2T = st.enter_context(nc.sbuf_tensor("sb_p3_w2T", [P, 8, D], BF16))
    with nc.Block() as blk:
        @blk.gpsimd
        def _(g):
            sems.inc(g.dma_start(w2T[:, :, :], w["w2T"].rearrange("(c p) n -> p c n", p=P)), SD, 16)
            g.wait_ge(sems.h[SD], sems.val(SD))

    emit_tm_to_fm(nc, sems, "p3", bufA, bufB, id_bf)

    gb = h1buf[:, 0:2, :].rearrange("p a b -> p (a b)").bitcast(F32)
    bb = h1buf[:, 2:4, :].rearrange("p a b -> p (a b)").bitcast(F32)
    scrH = h1buf[:, 4:8, :].rearrange("p a b -> p (a b)").bitcast(F32).rearrange(
        "p (a b) -> p a b", b=D)
    scrR = h1buf[:, 8:12, :].rearrange("p a b -> p (a b)").bitcast(F32).rearrange(
        "p (a b) -> p a b", b=D)
    scrF = h1buf[:, 12:14, :].rearrange("p a b -> p (a b)").bitcast(F32)
    with nc.Block() as blk:
        @blk.sync
        def _(s):
            sems.inc(s.dma_start(gb, w["gb"]), "p3ds", 16)
            sems.inc(s.dma_start(bb, w["bb"]), "p3ds", 16)
            s.wait_ge(sems.h["p3ds"], sems.val("p3ds"))

        @blk.vector
        def _(v):
            v = VP(v)
            v.tensor_tensor(statsR[:, 0, :], mean3[:, 0:NT], rstd3[:, 0:NT], Op.mult)
            v.tensor_scalar(statsR[:, 0, :], statsR[:, 0, :], -1.0, None, Op.mult)

    with ExitStack() as pst, nc.Block() as blk:
        psH = [pst.enter_context(nc.psum_tensor(f"ps_p3_o{i}", [P, D], F32))
               for i in range(2)]
        t0, v0, s0 = sems.val(ST), sems.val(SV), sems.val(SS)
        negmr3 = statsR[:, 0, :]
        sumsR, sumsqR = statsR[:, 1, :], statsR[:, 2, :]
        varR, rstdR, meanR = statsR[:, 3, :], statsR[:, 4, :], stats[:, 0, :]
        sumsA, sumsqA = statsA[:, 0, :], statsA[:, 1, :]
        meanA, varA, rstdA = statsA[:, 2, :], statsA[:, 3, :], statsA[:, 4, :]
        NS = NT + 6
        _MM, _EVH, _RS_T, _VAR, _T12, _VAA, _APF = {}, {}, {}, {}, {}, {}, {}
        _SQR, _RSR, _SQA, _RSA = {}, {}, {}, {}
        tc, vc, sc_ = t0, v0, s0
        for i in range(NS):
            if 0 <= i < NT:
                tc += 1; _MM[i] = tc
        for i in range(NS):
            if 0 <= i < NT:
                vc += 1; _EVH[i] = vc
            if 0 <= i - 3 < NT:
                vc += 1; _T12[i - 3] = vc
            if 0 <= i - 1 < NT:
                vc += 1; _RS_T[i - 1] = vc
            if 0 <= i - 2 < NT:
                vc += 1; _VAR[i - 2] = vc
            if 0 <= i - 4 < NT:
                vc += 1; _VAA[i - 4] = vc
            if 0 <= i - 5 < NT:
                vc += 1; _APF[i - 5] = vc
        for i in range(NS):
            if 0 <= i - 2 < NT:
                sc_ += 1; _SQR[i - 2] = sc_
                sc_ += 1; _RSR[i - 2] = sc_
            if 0 <= i - 4 < NT:
                sc_ += 1; _SQA[i - 4] = sc_
                sc_ += 1; _RSA[i - 4] = sc_

        @blk.tensor
        def _(te):
            for t in range(NT):
                if t >= 2:
                    sems.wait(te, SV, _EVH[t - 2])
                for half in range(2):
                    for c in range(8):
                        te.matmul(psH[t % 2][:, half * 512:(half + 1) * 512],
                                  bufB[:, c, t * P:(t + 1) * P],
                                  w2T[:, c, half * 512:(half + 1) * 512],
                                  start=(c == 0), stop=False)
                    instr = te.matmul(psH[t % 2][:, half * 512:(half + 1) * 512],
                                      ones_row[:, :], b2r[:, half * 512:(half + 1) * 512],
                                      start=False, stop=True)
                sems.inc(instr, ST)

        @blk.vector
        def _(v):
            v = VP(v)
            for i in range(NS):
                if 0 <= i < NT:
                    t = i
                    sems.wait(v, ST, _MM[t])
                    instr = v.tensor_scalar(scrH[:, t % 2, :], psH[t % 2][:, :],
                                            negmr3[:, t:t + 1], None, Op.add)
                    sems.inc(instr, SV)
                if 0 <= i - 3 < NT:
                    t = i - 3
                    sems.wait(v, SS, _RSR[t])
                    v.tensor_scalar(scrR[:, t % 2, :], scrR[:, t % 2, :],
                                    meanR[:, t:t + 1], rstdR[:, t:t + 1],
                                    Op.subtract, Op.mult)
                    v.tensor_tensor(scrR[:, t % 2, :], scrR[:, t % 2, :], gb, Op.mult)
                    instr = v.scalar_tensor_tensor(
                        bufA[:, t, :], scrR[:, t % 2, :], 1.0, bb,
                        Op.mult, Op.add, accum_out=sumsA[:, t:t + 1])
                    sems.inc(instr, SV)
                if 0 <= i - 1 < NT:
                    t = i - 1
                    instr = v.scalar_tensor_tensor(
                        scrR[:, t % 2, :], xbuf[:, t, :], rstd3[:, t:t + 1],
                        scrH[:, t % 2, :], Op.mult, Op.add,
                        accum_out=sumsR[:, t:t + 1])
                    sems.inc(instr, SV)
                if 0 <= i - 2 < NT:
                    t = i - 2
                    sems.wait(v, SS, _SQR[t])
                    v.tensor_scalar(meanR[:, t:t + 1], sumsR[:, t:t + 1], 1.0 / D, None, Op.mult)
                    v.tensor_scalar(varR[:, t:t + 1], sumsqR[:, t:t + 1], 1.0 / D, None, Op.mult)
                    v.tensor_tensor(sumsR[:, t:t + 1], meanR[:, t:t + 1],
                                    meanR[:, t:t + 1], Op.mult)
                    instr = v.tensor_tensor(varR[:, t:t + 1], varR[:, t:t + 1],
                                            sumsR[:, t:t + 1], Op.subtract)
                    sems.inc(instr, SV)
                if 0 <= i - 4 < NT:
                    t = i - 4
                    sems.wait(v, SS, _SQA[t])
                    v.tensor_scalar(meanA[:, t:t + 1], sumsA[:, t:t + 1], 1.0 / D, None, Op.mult)
                    v.tensor_scalar(varA[:, t:t + 1], sumsqA[:, t:t + 1], 1.0 / D, None, Op.mult)
                    v.tensor_tensor(sumsA[:, t:t + 1], meanA[:, t:t + 1],
                                    meanA[:, t:t + 1], Op.mult)
                    instr = v.tensor_tensor(varA[:, t:t + 1], varA[:, t:t + 1],
                                            sumsA[:, t:t + 1], Op.subtract)
                    sems.inc(instr, SV)
                if 0 <= i - 5 < NT:
                    t = i - 5
                    sems.wait(v, SS, _RSA[t])
                    v.tensor_scalar(scrF, bufA[:, t, :], meanA[:, t:t + 1],
                                    rstdA[:, t:t + 1], Op.subtract, Op.mult)
                    instr = v.tensor_tensor(xbuf[:, t, :], xbuf[:, t, :], scrF, Op.add)
                    sems.inc(instr, SV)

        @blk.scalar
        def _(sc):
            for i in range(NS):
                if 0 <= i - 2 < NT:
                    t = i - 2
                    sems.wait(sc, SV, _RS_T[t])
                    instr = sc.activation(h1buf[:, 15, :], scrR[:, t % 2, :], AF.Square,
                                          accum_out=sumsqR[:, t:t + 1])
                    sc.drain()
                    sems.inc(instr, SS)
                    sems.wait(sc, SV, _VAR[t])
                    sc.activation(rstdR[:, t:t + 1], varR[:, t:t + 1], AF.Ln, bias=epsc)
                    instr = sc.activation(rstdR[:, t:t + 1], rstdR[:, t:t + 1],
                                          AF.Exp, scale=-0.5)
                    sems.inc(instr, SS)
                if 0 <= i - 4 < NT:
                    t = i - 4
                    sems.wait(sc, SV, _T12[t])
                    instr = sc.activation(h1buf[:, 15, :], bufA[:, t, :], AF.Square,
                                          accum_out=sumsqA[:, t:t + 1])
                    sc.drain()
                    sems.inc(instr, SS)
                    sems.wait(sc, SV, _VAA[t])
                    sc.activation(rstdA[:, t:t + 1], varA[:, t:t + 1], AF.Ln, bias=epsc)
                    instr = sc.activation(rstdA[:, t:t + 1], rstdA[:, t:t + 1],
                                          AF.Exp, scale=-0.5)
                    sems.inc(instr, SS)

    return sems



def emit_phase4(nc, st, env, x_reload_src, x_out_dst, spill_dst=None):
    xbuf, bufA, bufB = env["xbuf"], env["bufA"], env["bufB"]
    ones_row, id_bf = env["ones_row"], env["id_bf"]
    id_f32 = env["id_f32"]
    epsc = env["epsc"][:, :]
    w = env["w4"]
    sems = Sems(nc, st, ["p4w", "p4x", "p4y", "p4b", "v_p4", "s_p4", "t_p4"])
    SV, SS, ST = "v_p4", "s_p4", "t_p4"
    y_acc = xbuf

    W2h = st.enter_context(nc.sbuf_tensor("sb_p4_W2h", [P, 16, D], BF16))
    hbuf = st.enter_context(nc.sbuf_tensor("sb_p4_h", [P, 16, 512], BF16))
    Wg_bf = st.enter_context(nc.sbuf_tensor("sb_p4_Wg", [P, 8, 8], BF16))
    bgr = st.enter_context(nc.sbuf_tensor("sb_p4_bgr", [1, 8], BF16))
    b2s = st.enter_context(nc.sbuf_tensor("sb_p4_b2s", [8, D], BF16))
    Gt = st.enter_context(nc.sbuf_tensor("sb_p4_G", [P, NT, 8], F32))
    wgt = st.enter_context(nc.sbuf_tensor("sb_p4_wgt", [P, NT, 8], F32))
    wgtb = st.enter_context(nc.sbuf_tensor("sb_p4_wgtb", [P, NT, 8], BF16))
    wT = st.enter_context(nc.sbuf_tensor("sb_p4_wT", [8, NT, P], BF16))
    b1s = st.enter_context(nc.sbuf_tensor("sb_p4_b1s", [32, P], F32))
    b1T = st.enter_context(nc.sbuf_tensor("sb_p4_b1T", [P, 32], F32))
    stats = st.enter_context(nc.sbuf_tensor("sb_p4_stats", [P, 5, NT], F32))
    sm = st.enter_context(nc.sbuf_tensor("sb_p4_sm", [P, 2, 8], F32))
    sm2 = st.enter_context(nc.sbuf_tensor("sb_p4_sm2", [P, 2, 8], F32))
    Ebuf = st.enter_context(nc.sbuf_tensor("sb_p4_E", [P, 2, 8], F32))
    cols = st.enter_context(nc.sbuf_tensor("sb_p4_cols", [P, 8], F32))
    xstage = st.enter_context(nc.sbuf_tensor("sb_p4_xst", [P, 2, D], F32))
    ostage = st.enter_context(nc.sbuf_tensor("sb_p4_ost", [P, 2, D], F32))

    import os
    STAGES = int(os.environ.get("P4_STAGES", "9"))

    emit_ln_to_fm(nc, env, sems, stats, "p4")

    if spill_dst is not None:
        with nc.Block() as blk:
            @blk.sync
            def _(s):
                sems.inc(s.dma_start(spill_dst.rearrange("t p d -> p t d"),
                                     xbuf[:, :, :]), "p4x", 16)
                s.wait_ge(sems.h["p4x"], sems.val("p4x"))

    if STAGES < 2:
        return sems
    with ExitStack() as pst, nc.Block() as blk:
        psg = [pst.enter_context(nc.psum_tensor(f"ps_p4_g{i}", [P, 512], F32))
               for i in range(2)]
        pswT = [pst.enter_context(nc.psum_tensor(f"ps_p4_wT{i}", [8, 1024], BF16))
                for i in range(2)]
        st0, sv0, ss0, w0 = sems.val(ST), sems.val(SV), sems.val(SS), sems.val("p4w")

        @blk.gpsimd
        def _(g):
            sems.inc(g.dma_start(Wg_bf[:, :, :],
                                 w["Wg"].rearrange("(c p) e -> p c e", p=P)), "p4w", 16)
            sems.inc(g.dma_start(bgr[:, :], w["bgr"]), "p4w", 16)
            sems.inc(g.dma_start(b2s[:, :], w["b2s"]), "p4w", 16)
            sems.inc(g.dma_start(
                b1s[:, :], w["b1"][0, :].rearrange("(a p) -> a p", p=P)), "p4w", 16)
            for grp in range(4):
                srcw = w["W1"][0, :, grp * 512:(grp + 1) * 512]
                dst = bufA[:, (grp // 2)::2, (grp % 2) * 512:(grp % 2 + 1) * 512]
                sems.inc(g.dma_start(dst, srcw.rearrange("(c p) i -> p c i", p=P)),
                         "p4w", 16)
            for grp in range(4):
                srcw = w["W2"][0, grp * 512:(grp + 1) * 512, :]
                sems.inc(g.dma_start(W2h[:, 4 * grp:4 * grp + 4, :],
                                     srcw.rearrange("(c p) o -> p c o", p=P)),
                         "p4w", 16)

        @blk.tensor
        def _(te):
            sems.wait(te, "p4w", w0 + 48)
            for t in range(NT):
                if t >= 2:
                    sems.wait(te, SV, sv0 + 2 * (t - 2) + 1)
                for c in range(8):
                    te.matmul(psg[t % 2][:, 0:8], bufB[:, c, t * P:(t + 1) * P],
                              Wg_bf[:, c, :], start=(c == 0), stop=False)
                instr = te.matmul(psg[t % 2][:, 0:8], ones_row[:, :], bgr[:, :],
                                  start=False, stop=True)
                sems.inc(instr, ST)
            for t in range(NT):
                sems.wait(te, SV, sv0 + 2 * t + 2)
                instr = te.transpose(pswT[t // 8][:, (t % 8) * P:(t % 8 + 1) * P],
                                     wgtb[:, t, :], id_bf[:, :])
                sems.inc(instr, ST)

        @blk.vector
        def _(v):
            v = VP(v)
            negm1, negm2p = cols[:, 0:1], cols[:, 1:2]
            den, rden = cols[:, 2:3], cols[:, 3:4]
            for t in range(NT):
                G = Gt[:, t, :]
                sems.wait(v, ST, st0 + t + 1)
                v.tensor_copy(G, psg[t % 2][:, 0:8])
                v.tensor_reduce(negm1, G, AX.X, Op.max, negate=True)
                v.tensor_scalar(sm[:, 0, :], G, negm1, None, Op.add)
                v.tensor_scalar(sm2[:, 0, :], sm[:, 0, :], 0.0, None, Op.is_ge)
                v.scalar_tensor_tensor(sm2[:, 0, :], sm2[:, 0, :], -1e30,
                                       sm[:, 0, :], Op.mult, Op.add)
                instr = v.tensor_reduce(negm2p, sm2[:, 0, :], AX.X, Op.max, negate=True)
                sems.inc(instr, SV)
                sems.wait(v, SS, ss0 + t + 1)
                v.tensor_scalar(sm2[:, 0, :], sm[:, 0, :], negm2p, 0.0, Op.add, Op.is_ge)
                v.tensor_scalar(den, den, 1.0, None, Op.add)
                v.reciprocal(rden, den)
                v.tensor_tensor(sm[:, 0, :], Ebuf[:, 0, :], sm2[:, 0, :], Op.mult)
                v.tensor_scalar(wgt[:, t, :], sm[:, 0, :], rden, None, Op.mult)
                instr = v.tensor_copy(wgtb[:, t, :], wgt[:, t, :])
                sems.inc(instr, SV)
            for bank in range(2):
                sems.wait(v, ST, st0 + NT + bank * 8 + 8)
                v.tensor_copy(wT[:, bank * 8:(bank + 1) * 8, :],
                              pswT[bank][:, :].rearrange("p (s n) -> p s n", n=P))

        @blk.scalar
        def _(sc):
            negm1, negm2p, den = cols[:, 0:1], cols[:, 1:2], cols[:, 2:3]
            for t in range(NT):
                sems.wait(sc, SV, sv0 + 2 * t + 1)
                sc.activation(Ebuf[:, 0, :], Gt[:, t, :], AF.Exp, bias=negm1)
                instr = sc.activation(den, negm2p, AF.Exp, scale=-1.0)
                sems.inc(instr, SS)

    if STAGES < 3:
        return sems
    with ExitStack() as pst, nc.Block() as blk:
        psb = [pst.enter_context(nc.psum_tensor(f"ps_p4_binit{i}", [P, D], F32))
               for i in range(2)]
        st0, sv0 = sems.val(ST), sems.val(SV)
        spill_done = sems.val("p4x")

        @blk.tensor
        def _(te):
            for t in range(NT):
                if t >= 2:
                    sems.wait(te, SV, sv0 + t - 1)
                for half in range(2):
                    instr = te.matmul(psb[t % 2][:, half * 512:(half + 1) * 512],
                                      wT[:, t, :], b2s[:, half * 512:(half + 1) * 512],
                                      start=True, stop=True)
                sems.inc(instr, ST)

        @blk.vector
        def _(v):
            v = VP(v)
            if spill_done:
                sems.wait(v, "p4x", spill_done)
            for t in range(NT):
                sems.wait(v, ST, st0 + t + 1)
                instr = v.tensor_copy(y_acc[:, t, :], psb[t % 2][:, :])
                sems.inc(instr, SV)

    if STAGES < 4:
        return sems
    with ExitStack() as pst, nc.Block() as blk:
        psh = [pst.enter_context(nc.psum_tensor(f"ps_p4_h{i}", [P, 512], F32))
               for i in range(2)]
        psy = [pst.enter_context(nc.psum_tensor(f"ps_p4_y{i}", [P, D], F32))
               for i in range(2)]
        psb1 = pst.enter_context(nc.psum_tensor("ps_p4_b1", [P, 512], F32))
        st0, sv0, ss0, w0 = sems.val(ST), sems.val(SV), sems.val(SS), sems.val("p4w")
        NIDX = int(os.environ.get("P4_NIDX", "16"))
        DSUB = int(os.environ.get("P4_DSUB", "3"))
        NTAU = int(os.environ.get("P4_NTAU", "4"))

        TSTRIDE = (16 if DSUB >= 2 else 0) + (4 if DSUB >= 3 else 0)
        H64 = 16 * NTAU if DSUB >= 2 else 0
        Y16 = 4 * NTAU if DSUB >= 3 else 0

        def tbase(idx):
            return st0 + idx * (H64 + Y16)

        def sbase(idx):
            return ss0 + H64 * idx

        NDMA = 8 if DSUB >= 1 else 0

        def wvals(idx):
            base = w0 - 16 * (NDMA + 1) + 16 * sum(
                (NDMA + (1 if i % 2 == 0 else 0)) for i in range(idx))
            n = (NDMA + 1) if idx % 2 == 0 else max(NDMA, 1)
            return [base + 16 * (j + 1) for j in range(n)]

        @blk.gpsimd
        def _(g):
            for idx in range(NIDX):
                if idx == 0:
                    continue
                e, half = idx // 2, idx % 2
                if half == 0:
                    if e > 0:
                        sems.wait(g, "p4b", 2 * e - 1)
                    sems.inc(g.dma_start(
                        b1s[:, :], w["b1"][e, :].rearrange("(a p) -> a p", p=P)),
                        "p4w", 16)
                if DSUB < 1:
                    continue
                if idx >= 1:
                    if DSUB >= 2:
                        sems.wait(g, ST, tbase(idx - 1) + (NTAU - 1) * TSTRIDE + 16)
                    else:
                        sems.wait(g, ST, tbase(idx - 1))
                for grp in range(4):
                    src = w["W1"][e, :, half * 2048 + grp * 512:
                                  half * 2048 + (grp + 1) * 512]
                    dst = bufA[:, (grp // 2)::2, (grp % 2) * 512:(grp % 2 + 1) * 512]
                    sems.inc(g.dma_start(dst, src.rearrange("(c p) i -> p c i", p=P)),
                             "p4w", 16)
                if idx >= 1:
                    sems.wait(g, ST, tbase(idx))
                for grp in range(4):
                    src = w["W2"][e, half * 2048 + grp * 512:
                                  half * 2048 + (grp + 1) * 512, :]
                    sems.inc(g.dma_start(W2h[:, 4 * grp:4 * grp + 4, :],
                                         src.rearrange("(c p) o -> p c o", p=P)),
                             "p4w", 16)

        @blk.tensor
        def _(te):
            for idx in range(NIDX):
                e, half = idx // 2, idx % 2
                wv = wvals(idx)
                b1off = 1 if half == 0 else 0
                if half == 0:
                    sems.wait(te, "p4w", wv[0])
                    if e > 0:
                        sems.wait(te, "p4b", 2 * e)
                    instr = te.transpose(psb1[:, 0:32], b1s[:, :], id_f32[0:32, 0:32])
                    sems.inc(instr, "p4b", 1)
                for tau in range(NTAU):
                    if DSUB < 2:
                        break
                    if int(os.environ.get("P4_SER", "0")) and DSUB >= 3:
                        sems.wait(te, SV, sv0 + idx * 4 * NTAU + tau * 4)
                    for ic in range(16):
                        glob = tau * 16 + ic
                        sems.wait(te, "p4w", wv[b1off + ic // 4])
                        if sbase(idx) + glob - 1 >= 1:
                            sems.wait(te, SS, sbase(idx) + glob - 1)
                        for c in range(8):
                            instr = te.matmul(
                                psh[ic % 2][:, :],
                                bufA[:, 2 * c + ic // 8, (ic % 8) * P:(ic % 8 + 1) * P],
                                bufB[:, c, tau * 512:(tau + 1) * 512],
                                start=(c == 0), stop=(c == 7))
                        sems.inc(instr, ST)
                    if DSUB < 3:
                        continue
                    sems.wait(te, "p4w", wv[b1off + 7])
                    sems.wait(te, SS, sbase(idx) + tau * 16 + 16)
                    for tt in range(4):
                        r = idx * 16 + tau * 4 + tt
                        if r >= 2:
                            sems.wait(te, SV, sv0 + r - 1)
                        for half_o in range(2):
                            for ic in range(16):
                                instr = te.matmul(
                                    psy[r % 2][:, half_o * 512:(half_o + 1) * 512],
                                    hbuf[:, ic, tt * P:(tt + 1) * P],
                                    W2h[:, ic, half_o * 512:(half_o + 1) * 512],
                                    start=(ic == 0), stop=(ic == 15))
                        sems.inc(instr, ST)

        @blk.scalar
        def _(sc):
            for idx in range(NIDX):
                e, half = idx // 2, idx % 2
                b1off = 1 if half == 0 else 0
                for tau in range(NTAU):
                    if DSUB < 2:
                        break
                    for ic in range(16):
                        glob = tau * 16 + ic
                        sems.wait(sc, ST, tbase(idx) + tau * TSTRIDE + ic + 1)
                        if half == 0 and glob == 0:
                            sems.wait(sc, "p4b", 2 * e + 2)
                        instr = sc.activation(hbuf[:, ic, :], psh[ic % 2][:, :],
                                              AF.Gelu,
                                              bias=b1T[:, half * 16 + ic:half * 16 + ic + 1])
                        sems.inc(instr, SS)

        @blk.vector
        def _(v):
            v = VP(v)
            for idx in range(NIDX):
                e, half = idx // 2, idx % 2
                b1off = 1 if half == 0 else 0
                if half == 0:
                    sems.wait(v, "p4b", 2 * e + 1)
                    instr = v.tensor_copy(b1T[:, :], psb1[:, 0:32])
                    sems.inc(instr, "p4b", 1)
                for tau in range(NTAU):
                    if DSUB < 3:
                        break
                    for tt in range(4):
                        r = idx * 16 + tau * 4 + tt
                        t = tau * 4 + tt
                        sems.wait(v, ST, tbase(idx) + tau * TSTRIDE + 16 + tt + 1)
                        if int(os.environ.get("P4_STT", "1")):
                            instr = v.scalar_tensor_tensor(
                                y_acc[:, t, :], psy[r % 2][:, :], wgt[:, t, e:e + 1],
                                y_acc[:, t, :], Op.mult, Op.add)
                        else:
                            instr = v.tensor_copy(hbuf[:, 0, :].bitcast(F32)[:, 0:256], psy[r % 2][:, 0:256])
                        sems.inc(instr, SV)

    if STAGES < 5:
        return sems
    with nc.Block() as blk:
        v0, s0 = sems.val(SV), sems.val(SS)
        x0, y0 = sems.val("p4x"), sems.val("p4y")
        sums, sumsq = stats[:, 0, :], stats[:, 1, :]
        meanE, varE, rstdE = stats[:, 2, :], stats[:, 3, :], stats[:, 4, :]
        NS = NT + 6
        _RD, _VA, _AP2, _SQ, _RS2 = {}, {}, {}, {}, {}
        vc, sc_ = v0, s0
        for i in range(NS):
            if 0 <= i < NT:
                vc += 1; _RD[i] = vc
            if 0 <= i - 2 < NT:
                vc += 1; _VA[i - 2] = vc
            if 0 <= i - 4 < NT:
                vc += 1; _AP2[i - 4] = vc
        for i in range(NS):
            if 0 <= i - 1 < NT:
                sc_ += 1; _SQ[i - 1] = sc_
            if 0 <= i - 3 < NT:
                sc_ += 1; _RS2[i - 3] = sc_

        @blk.sync
        def _(s):
            for t in range(NT):
                if t >= 2:
                    sems.wait(s, SV, _AP2[t - 2])
                sems.inc(s.dma_start(xstage[:, t % 2, :],
                                     x_reload_src[t, :, :]), "p4x", 16)

        @blk.vector
        def _(v):
            v = VP(v)
            for i in range(NS):
                if 0 <= i < NT:
                    t = i
                    instr = v.tensor_reduce(sums[:, t:t + 1], y_acc[:, t, :], AX.X, Op.add)
                    sems.inc(instr, SV)
                if 0 <= i - 2 < NT:
                    t = i - 2
                    sems.wait(v, SS, _SQ[t])
                    v.tensor_scalar(meanE[:, t:t + 1], sums[:, t:t + 1], 1.0 / D, None, Op.mult)
                    v.tensor_scalar(varE[:, t:t + 1], sumsq[:, t:t + 1], 1.0 / D, None, Op.mult)
                    v.tensor_tensor(sums[:, t:t + 1], meanE[:, t:t + 1],
                                    meanE[:, t:t + 1], Op.mult)
                    instr = v.tensor_tensor(varE[:, t:t + 1], varE[:, t:t + 1],
                                            sums[:, t:t + 1], Op.subtract)
                    sems.inc(instr, SV)
                if 0 <= i - 4 < NT:
                    t = i - 4
                    sems.wait(v, SS, _RS2[t])
                    sems.wait(v, "p4x", x0 + 16 * (t + 1))
                    if t >= 2:
                        sems.wait(v, "p4y", y0 + 16 * (t - 1))
                    v.tensor_scalar(ostage[:, t % 2, :], y_acc[:, t, :],
                                    meanE[:, t:t + 1], rstdE[:, t:t + 1],
                                    Op.subtract, Op.mult)
                    instr = v.tensor_tensor(ostage[:, t % 2, :], ostage[:, t % 2, :],
                                            xstage[:, t % 2, :], Op.add)
                    sems.inc(instr, SV)

        @blk.scalar
        def _(sc):
            for i in range(NS):
                if 0 <= i - 1 < NT:
                    t = i - 1
                    instr = sc.activation(bufA[:, 0, :], y_acc[:, t, :], AF.Square,
                                          accum_out=sumsq[:, t:t + 1])
                    sc.drain()
                    sems.inc(instr, SS)
                if 0 <= i - 3 < NT:
                    t = i - 3
                    sems.wait(sc, SV, _VA[t])
                    sc.activation(rstdE[:, t:t + 1], varE[:, t:t + 1], AF.Ln, bias=epsc)
                    instr = sc.activation(rstdE[:, t:t + 1], rstdE[:, t:t + 1],
                                          AF.Exp, scale=-0.5)
                    sems.inc(instr, SS)

        @blk.gpsimd
        def _(g):
            for t in range(NT):
                sems.wait(g, SV, _AP2[t])
                sems.inc(g.dma_start(x_out_dst[t, :, :], ostage[:, t % 2, :]),
                         "p4y", 16)
            g.wait_ge(sems.h["p4y"], sems.val("p4y"))

    return sems

def emit_postln_pipe(nc, blk, sems, tag, mm_vals, psum, tile_ids, xbuf, scr3,
                     sqdump, stats, epsc, nt):
    SV, SS = f"v_{tag}", f"s_{tag}"
    v0, s0 = sems.val(SV), sems.val(SS)
    sums, sumsq = stats[:, 0, :], stats[:, 1, :]
    mean, var, rstd = stats[:, 2, :], stats[:, 3, :], stats[:, 4, :]
    _EV, _VA, _AP, _SQ, _RS = {}, {}, {}, {}, {}
    vc, sc_ = v0, s0
    for i in range(nt + 3):
        if 0 <= i - 1 < nt:
            vc += 1; _EV[i - 1] = vc
        if 0 <= i - 2 < nt:
            vc += 1; _VA[i - 2] = vc
        if 0 <= i - 3 < nt:
            vc += 1; _AP[i - 3] = vc
    for i in range(nt + 3):
        if 0 <= i - 2 < nt:
            sc_ += 1; _SQ[i - 2] = sc_
        if 0 <= i - 3 < nt:
            sc_ += 1; _RS[i - 3] = sc_

    @blk.vector
    def _(v):
        v = VP(v)
        for i in range(nt + 3):
            if 0 <= i - 1 < nt:
                t = i - 1
                sems.wait(v, f"t_{tag}", mm_vals[t])
                instr = v.tensor_scalar(scr3[:, t % 3, :], psum[t % 2][:, :], 1.0,
                                        None, Op.mult, Op.add,
                                        accum_out=sums[:, t:t + 1])
                sems.inc(instr, SV)
            if 0 <= i - 2 < nt:
                t = i - 2
                sems.wait(v, SS, _SQ[t])
                v.tensor_scalar(mean[:, t:t + 1], sums[:, t:t + 1], 1.0 / D, None, Op.mult)
                v.tensor_scalar(var[:, t:t + 1], sumsq[:, t:t + 1], 1.0 / D, None, Op.mult)
                v.tensor_tensor(sums[:, t:t + 1], mean[:, t:t + 1], mean[:, t:t + 1], Op.mult)
                instr = v.tensor_tensor(var[:, t:t + 1], var[:, t:t + 1],
                                        sums[:, t:t + 1], Op.subtract)
                sems.inc(instr, SV)
            if 0 <= i - 3 < nt:
                t = i - 3
                sems.wait(v, SS, _RS[t])
                v.tensor_scalar(scr3[:, t % 3, :], scr3[:, t % 3, :],
                                mean[:, t:t + 1], rstd[:, t:t + 1],
                                Op.subtract, Op.mult)
                instr = v.tensor_tensor(xbuf[:, tile_ids[t], :], xbuf[:, tile_ids[t], :],
                                        scr3[:, t % 3, :], Op.add)
                sems.inc(instr, SV)

    @blk.scalar
    def _(sc):
        for i in range(nt + 3):
            if 0 <= i - 2 < nt:
                t = i - 2
                sems.wait(sc, SV, _EV[t])
                instr = sc.activation(sqdump, scr3[:, t % 3, :], AF.Square,
                                      accum_out=sumsq[:, t:t + 1])
                sc.drain()
                sems.inc(instr, SS)
            if 0 <= i - 3 < nt:
                t = i - 3
                sems.wait(sc, SV, _VA[t])
                sc.activation(rstd[:, t:t + 1], var[:, t:t + 1], AF.Ln, bias=epsc)
                instr = sc.activation(rstd[:, t:t + 1], rstd[:, t:t + 1], AF.Exp, scale=-0.5)
                sems.inc(instr, SS)
    return _EV


def _prep_shared(inputs):
    t = {}
    t["p2_Win"] = np.ascontiguousarray(inputs["L0_ffn_Win"])
    t["p2_binr"] = np.ascontiguousarray(inputs["L0_ffn_bin_"][None, :])
    t["p2_W1b"] = np.ascontiguousarray(
        np.tile(inputs["L0_ffn_W1"].reshape(1, D), (P, 1)))
    t["p2_b1b"] = np.ascontiguousarray(
        np.tile(inputs["L0_ffn_b1"].reshape(1, D), (P, 1)))
    t["p2_W2b"] = np.ascontiguousarray(
        np.tile(inputs["L0_ffn_W2"].reshape(1, D), (P, 1)))
    t["p2_b2b"] = np.ascontiguousarray(
        np.tile(inputs["L0_ffn_b2"].reshape(1, 64), (P, 1)))
    t["p2_Wout"] = np.ascontiguousarray(inputs["L0_ffn_Wout"])
    t["p2_boutr"] = np.ascontiguousarray(inputs["L0_ffn_bout"][None, :])
    t["p1_WqkvT"] = np.ascontiguousarray(inputs["L0_attn_Wqkv"].T)
    t["p1_WprojT"] = np.ascontiguousarray(inputs["L0_attn_Wproj"].T)
    t["p1_bprojr"] = np.ascontiguousarray(inputs["L0_attn_bproj"][None, :])
    t["p3_w1T"] = np.ascontiguousarray(inputs["L1_attn_w1"].T)
    t["p3_wwT"] = np.ascontiguousarray(inputs["L1_attn_ww"].T)
    t["p3_w2T"] = np.ascontiguousarray(inputs["L1_attn_w2"].T)
    t["p3_b1r"] = np.ascontiguousarray(inputs["L1_attn_b1"][None, :])
    t["p3_bwr"] = np.ascontiguousarray(inputs["L1_attn_bw"][None, :])
    t["p3_b2r"] = np.ascontiguousarray(inputs["L1_attn_b2"][None, :])
    t["p3_gb"] = np.ascontiguousarray(np.tile(inputs["L1_attn_gamma"][None, :], (P, 1)))
    t["p3_bb"] = np.ascontiguousarray(np.tile(inputs["L1_attn_beta"][None, :], (P, 1)))
    t["p4_Wg"] = np.ascontiguousarray(inputs["L1_ffn_Wg"])
    t["p4_bgr"] = np.ascontiguousarray(inputs["L1_ffn_bg"][None, :])
    t["p4_b2s"] = np.ascontiguousarray(inputs["L1_ffn_b2"])
    t["p4_W1"] = np.ascontiguousarray(inputs["L1_ffn_W1"])
    t["p4_W2"] = np.ascontiguousarray(inputs["L1_ffn_W2"])
    t["p4_b1"] = np.ascontiguousarray(inputs["L1_ffn_b1"])
    t["ones_row"] = np.ones((1, P), dtype=BF)
    t["id_bf"] = np.eye(P, dtype=BF)
    t["id_f32"] = np.eye(P, dtype=np.float32)
    return t


PHASE_KEYS = {
    1: ["p1_WqkvT", "p1_WprojT", "p1_bprojr"],
    2: ["p2_Win", "p2_binr", "p2_W1b", "p2_b1b", "p2_W2b", "p2_b2b",
        "p2_Wout", "p2_boutr"],
    3: ["p3_w1T", "p3_wwT", "p3_w2T", "p3_b1r", "p3_bwr", "p3_b2r",
        "p3_gb", "p3_bb"],
    4: ["p4_Wg", "p4_bgr", "p4_b2s", "p4_W1", "p4_W2", "p4_b1"],
}
COMMON_KEYS = ["ones_row", "id_bf", "id_f32"]


def _phase_weight_aps(nc, t):
    aps = {}
    for k, v in t.items():
        dt = BF16 if v.dtype == BF else F32
        aps[k] = nc.dram_tensor(k, list(v.shape), dt, kind="ExternalInput").ap()
    return aps


def _common_env(nc, st, aps):
    env = {}
    env["xbuf"] = st.enter_context(nc.sbuf_tensor("xbuf", [P, NT, D], F32))
    env["bufA"] = st.enter_context(nc.sbuf_tensor("bufA", [P, NT, D], BF16))
    env["bufB"] = st.enter_context(nc.sbuf_tensor("bufB", [P, 8, 2048], BF16))
    env["ones_row"] = st.enter_context(nc.sbuf_tensor("c_ones", [1, P], BF16))
    env["id_bf"] = st.enter_context(nc.sbuf_tensor("c_idbf", [P, P], BF16))
    env["id_f32"] = st.enter_context(nc.sbuf_tensor("c_idf32", [P, P], F32))
    env["epsc"] = st.enter_context(nc.sbuf_tensor("c_eps", [P, 1], F32))
    return env


def _wire_weights(aps, phase):
    if phase == 1:
        return {"WqkvT": aps["p1_WqkvT"], "WprojT": aps["p1_WprojT"],
                "bprojr": aps["p1_bprojr"], "mask1": aps["mask1"],
                "ropeq": aps["ropeq"], "ropek": aps["ropek"]}
    if phase == 2:
        return {"Win": aps["p2_Win"], "binr": aps["p2_binr"],
                "W1b": aps["p2_W1b"], "b1b": aps["p2_b1b"],
                "W2b": aps["p2_W2b"], "b2b": aps["p2_b2b"],
                "Wout": aps["p2_Wout"], "boutr": aps["p2_boutr"]}
    if phase == 3:
        return {"w1T": aps["p3_w1T"], "wwT": aps["p3_wwT"], "w2T": aps["p3_w2T"],
                "b1r": aps["p3_b1r"], "bwr": aps["p3_bwr"], "b2r": aps["p3_b2r"],
                "gb": aps["p3_gb"], "bb": aps["p3_bb"]}
    if phase == 4:
        return {"Wg": aps["p4_Wg"], "bgr": aps["p4_bgr"], "b2s": aps["p4_b2s"],
                "W1": aps["p4_W1"], "W2": aps["p4_W2"], "b1": aps["p4_b1"]}
    raise NotImplementedError(phase)


def build_phase_program(phase, shared):
    nc = bass.Bass("TRN2", target_bir_lowering=False)
    xin = nc.dram_tensor("xin", [NT, P, D], F32, kind="ExternalInput").ap()
    xout = nc.dram_tensor("xout", [NT, P, D], F32, kind="ExternalOutput").ap()
    keys = COMMON_KEYS + PHASE_KEYS[phase]
    aps = _phase_weight_aps(nc, {k: shared[k] for k in keys})
    if phase == 1:
        aps["mask1"] = nc.dram_tensor("mask1", [4, P, 640], F32, kind="ExternalInput").ap()
        aps["ropeq"] = nc.dram_tensor("ropeq", [2, 4, P, 64], F32, kind="ExternalInput").ap()
        aps["ropek"] = nc.dram_tensor("ropek", [2, 8, P, 64], F32, kind="ExternalInput").ap()
        xh = nc.dram_tensor("xh", [NT, P, D], F32, kind="ExternalInput").ap()

    st = ExitStack()
    with st:
        env = _common_env(nc, st, aps)
        env[f"w{phase}"] = _wire_weights(aps, phase)
        gsems = Sems(nc, st, ["io"])
        with nc.Block() as blk:
            @blk.sync
            def _(s):
                gsems.inc(s.dma_start(env["xbuf"][:, :, :], xin.rearrange("t p d -> p t d")), "io", 16)
                gsems.inc(s.dma_start(env["ones_row"][:, :], aps["ones_row"]), "io", 16)
                gsems.inc(s.dma_start(env["id_bf"][:, :], aps["id_bf"]), "io", 16)
                gsems.inc(s.dma_start(env["id_f32"][:, :], aps["id_f32"]), "io", 16)
                s.wait_ge(gsems.h["io"], gsems.val("io"))

            @blk.vector
            def _(v):
                v = VP(v)
                v.memset(env["epsc"][:, :], EPS)

        if phase == 1:
            emit_phase1(nc, st, env, xh)
        elif phase == 2:
            emit_phase2(nc, st, env)
        elif phase == 3:
            emit_phase3(nc, st, env)
        elif phase == 4:
            emit_phase4(nc, st, env, xin, xout, spill_dst=None)
        else:
            raise NotImplementedError(phase)

        if phase != 4:
            with nc.Block() as blk:
                @blk.sync
                def _(s):
                    instr = s.dma_start(xout.rearrange("t p d -> p t d"),
                                        env["xbuf"][:, :, :])
                    gsems.inc(instr, "io", 16)
                    s.wait_ge(gsems.h["io"], gsems.val("io"))
    return nc


def _shard_x(x_full):
    shards = []
    for k in range(NCORES):
        xc = x_full[:, k * SLOC:(k + 1) * SLOC, :]
        shards.append(np.ascontiguousarray(
            xc.reshape(B, SLOC // P, P, D).reshape(NT, P, D)))
    return shards


def _unshard_x(shards):
    out = np.empty((B, SEQ, D), dtype=np.float32)
    for k, sh in enumerate(shards):
        out[:, k * SLOC:(k + 1) * SLOC, :] = sh.reshape(B, SLOC // P, P, D).reshape(
            B, SLOC, D)
    return out


def _percore_p1(k, x_full):
    out = {}
    if k == 0:
        xh = np.zeros((B, SLOC, D), dtype=np.float32)
    else:
        xh = np.ascontiguousarray(x_full[:, (k - 1) * SLOC:k * SLOC, :])
    out["xh"] = np.ascontiguousarray(
        xh.reshape(B, SLOC // P, P, D).reshape(NT, P, D))
    ql = np.arange(P)[:, None]
    cc = np.arange(640)[None, :]
    mask = np.empty((4, P, 640), dtype=np.float32)
    for g in range(4):
        allow = (cc >= ql) & (cc <= ql + 512)
        jglob = 512 * (k - 1) + 128 * g + cc
        allow = allow & (jglob >= 0)
        mask[g] = np.where(allow, 1.0, 0.0).astype(np.float32)
    out["mask1"] = mask
    half = HD // 2
    inv_freq = (1.0 / 10000.0 ** (np.arange(half, dtype=np.float32) / half)).astype(np.float32)
    sgn = np.empty(HD, dtype=np.float32)
    sgn[0::2], sgn[1::2] = -1.0, 1.0

    def tables(pos):
        ang = pos[:, :, None].astype(np.float32) * inv_freq[None, None, :]
        cosr = np.repeat(np.cos(ang), 2, axis=-1).astype(np.float32)
        sinr = np.repeat(np.sin(ang), 2, axis=-1).astype(np.float32) * sgn
        return cosr, sinr

    posq = 512 * k + 128 * np.arange(4)[:, None] + np.arange(P)[None, :]
    cosr, sinr = tables(posq)
    out["ropeq"] = np.ascontiguousarray(
        np.stack([cosr, sinr]) * np.float32(0.125))
    posk = 512 * (k - 1) + 128 * np.arange(8)[:, None] + np.arange(P)[None, :]
    cosr, sinr = tables(posk)
    out["ropek"] = np.ascontiguousarray(np.stack([cosr, sinr]))
    return out


def run_phase(phase, x_full, inputs):
    shared = _prep_shared(inputs)
    nc = build_phase_program(phase, shared)
    shards = _shard_x(x_full)
    keys = COMMON_KEYS + PHASE_KEYS[phase]
    in_maps = []
    for k in range(NCORES):
        m = {kk: shared[kk] for kk in keys}
        m["xin"] = shards[k]
        if phase == 1:
            m.update(_percore_p1(k, x_full))
        in_maps.append(m)
    res = run_bass_kernel_spmd(nc, in_maps, core_ids=list(range(NCORES)))
    return _unshard_x([r["xout"] for r in res.results])


def build_fused_program(shared):
    nc = bass.Bass("TRN2", target_bir_lowering=False)
    xin = nc.dram_tensor("xin", [NT, P, D], F32, kind="ExternalInput").ap()
    xout = nc.dram_tensor("xout", [NT, P, D], F32, kind="ExternalOutput").ap()
    spill = nc.dram_tensor("spill", [NT, P, D], F32, kind="Internal").ap()
    keys = COMMON_KEYS + sum((PHASE_KEYS[p] for p in (1, 2, 3, 4)), [])
    aps = _phase_weight_aps(nc, {k: shared[k] for k in keys})
    aps["mask1"] = nc.dram_tensor("mask1", [4, P, 640], F32, kind="ExternalInput").ap()
    aps["ropeq"] = nc.dram_tensor("ropeq", [2, 4, P, 64], F32, kind="ExternalInput").ap()
    aps["ropek"] = nc.dram_tensor("ropek", [2, 8, P, 64], F32, kind="ExternalInput").ap()
    xh = nc.dram_tensor("xh", [NT, P, D], F32, kind="ExternalInput").ap()

    st = ExitStack()
    with st:
        env = _common_env(nc, st, aps)
        for p in (1, 2, 3, 4):
            env[f"w{p}"] = _wire_weights(aps, p)
        gsems = Sems(nc, st, ["io"])
        with nc.Block() as blk:
            @blk.sync
            def _(s):
                gsems.inc(s.dma_start(env["xbuf"][:, :, :], xin.rearrange("t p d -> p t d")), "io", 16)
                gsems.inc(s.dma_start(env["ones_row"][:, :], aps["ones_row"]), "io", 16)
                gsems.inc(s.dma_start(env["id_bf"][:, :], aps["id_bf"]), "io", 16)
                gsems.inc(s.dma_start(env["id_f32"][:, :], aps["id_f32"]), "io", 16)
                s.wait_ge(gsems.h["io"], gsems.val("io"))

            @blk.vector
            def _(v):
                v = VP(v)
                v.memset(env["epsc"][:, :], EPS)

        with ExitStack() as pstk:
            emit_phase1(nc, pstk, env, xh)
        with ExitStack() as pstk:
            emit_phase2(nc, pstk, env)
        with ExitStack() as pstk:
            emit_phase3(nc, pstk, env)
        with ExitStack() as pstk:
            emit_phase4(nc, pstk, env, spill, xout, spill_dst=spill)
    return nc


def run_fused(x_full, inputs):
    shared = _prep_shared(inputs)
    nc = build_fused_program(shared)
    shards = _shard_x(x_full)
    keys = COMMON_KEYS + sum((PHASE_KEYS[p] for p in (1, 2, 3, 4)), [])
    in_maps = []
    for k in range(NCORES):
        m = {kk: shared[kk] for kk in keys}
        m["xin"] = shards[k]
        m.update(_percore_p1(k, x_full))
        in_maps.append(m)
    res = run_bass_kernel_spmd(nc, in_maps, core_ids=list(range(NCORES)))
    return _unshard_x([r["xout"] for r in res.results])


def kernel(**inputs):
    x = np.asarray(inputs["L0_attn_x"], dtype=np.float32)
    return run_fused(x, inputs)


# version 16
# speedup vs baseline: 1.0128x; 1.0128x over previous
import numpy as np
import ml_dtypes
from contextlib import ExitStack

import concourse.bass as bass
import concourse.mybir as mybir
from concourse.bass_utils import run_bass_kernel_spmd
from concourse.alu_op_type import AluOpType as Op

F32 = mybir.dt.float32
BF16 = mybir.dt.bfloat16
AF = mybir.ActivationFunctionType
AX = mybir.AxisListType
BF = ml_dtypes.bfloat16

P = 128
D = 1024
B = 4
SEQ = 4096
NCORES = 8
SLOC = SEQ // NCORES
NT = B * (SLOC // P)
NH = 16
HD = 64
WINLEN = 512
KTOK = SLOC + WINLEN
EPS = 1e-5



class Sems:

    def __init__(self, nc, st, names):
        self.h = {}
        self.n = {}
        for nm in names:
            self.h[nm] = st.enter_context(nc.semaphore(nm))
            self.n[nm] = 0

    def inc(self, instr, nm, k=1):
        instr.then_inc(self.h[nm], k)
        self.n[nm] += k
        return self.n[nm]

    def dma(self, instr, nm="dma"):
        return self.inc(instr, nm, 16)

    def val(self, nm):
        return self.n[nm]

    def wait(self, eng, nm, v):
        eng.wait_ge(self.h[nm], v)


class VP:

    _PASS = {"wait_ge", "drain", "engine_nop"}

    def __init__(self, v):
        self._v = v

    def __getattr__(self, name):
        fn = getattr(self._v, name)
        if name in VP._PASS or not callable(fn):
            return fn

        def wrapped(*a, **k):
            r = fn(*a, **k)
            self._v.drain()
            return r
        return wrapped


def emit_ln_stats(nc, blk, sems, src_tiles, nfeat, stats, sqdump, epsc, tag):
    n = len(src_tiles)
    sums, sumsq = stats[:, 0, :n], stats[:, 1, :n]
    mean, var, rstd = stats[:, 2, :n], stats[:, 3, :n], stats[:, 4, :n]
    sv, ss = f"v_{tag}", f"s_{tag}"
    v0, s0 = sems.val(sv), sems.val(ss)

    @blk.vector
    def _(v):
        v = VP(v)
        for i, srct in enumerate(src_tiles):
            instr = v.tensor_reduce(sums[:, i:i + 1], srct, AX.X, Op.add)
        sems.inc(instr, sv)

    @blk.scalar
    def _(sc):
        for i, srct in enumerate(src_tiles):
            instr = sc.activation(sqdump, srct, AF.Square,
                                  accum_out=sumsq[:, i:i + 1])
            sc.drain()
        sems.inc(instr, ss)

    @blk.vector
    def _(v):
        v = VP(v)
        sems.wait(v, ss, s0 + 1)
        v.tensor_scalar(mean, sums, 1.0 / nfeat, None, Op.mult)
        v.tensor_scalar(var, sumsq, 1.0 / nfeat, None, Op.mult)
        v.tensor_tensor(sums, mean, mean, Op.mult)
        instr = v.tensor_tensor(var, var, sums, Op.subtract)
        sems.inc(instr, sv)

    @blk.scalar
    def _(sc):
        sems.wait(sc, sv, v0 + 2)
        sc.activation(rstd, var, AF.Ln, bias=epsc)
        instr = sc.activation(rstd, rstd, AF.Exp, scale=-0.5)
        sems.inc(instr, ss)

    return mean, rstd



def emit_phase1(nc, st, env, xh_ap):
    xbuf, bufA, bufB = env["xbuf"], env["bufA"], env["bufB"]
    ones_row, id_bf = env["ones_row"], env["id_bf"]
    epsc = env["epsc"][:, :]
    w = env["w1"]
    sems = Sems(nc, st, ["p1dma", "p1ds", "p1w", "v_p1", "s_p1", "t_p1"])
    SV, SS, ST, SD = "v_p1", "s_p1", "t_p1", "p1dma"

    mask_sb = st.enter_context(nc.sbuf_tensor("sb_p1_mask", [P, 4, 640], BF16))
    cosq = st.enter_context(nc.sbuf_tensor("sb_p1_cq", [P, 4, 64], F32))
    sinq = st.enter_context(nc.sbuf_tensor("sb_p1_sq", [P, 4, 64], F32))
    cosk = st.enter_context(nc.sbuf_tensor("sb_p1_ck", [P, 8, 64], F32))
    sink = st.enter_context(nc.sbuf_tensor("sb_p1_sk", [P, 8, 64], F32))
    bprojr = st.enter_context(nc.sbuf_tensor("sb_p1_bpr", [1, D], BF16))
    wq2 = [st.enter_context(nc.sbuf_tensor(f"sb_p1_wq{i}", [P, 8, 512], BF16))
           for i in range(2)]
    qT = st.enter_context(nc.sbuf_tensor("sb_p1_qT", [P, 8, 512], BF16))
    yfm = st.enter_context(nc.sbuf_tensor("sb_p1_yfm", [P, 8, 512], BF16))
    hstage = st.enter_context(nc.sbuf_tensor("sb_p1_hst", [P, 4, D], F32))
    qrope = st.enter_context(nc.sbuf_tensor("sb_p1_qr", [P, 2, 512], BF16))
    scrA = st.enter_context(nc.sbuf_tensor("sb_p1_sA", [P, 512], F32))
    scrB = st.enter_context(nc.sbuf_tensor("sb_p1_sB", [P, 512], F32))
    Pn = st.enter_context(nc.sbuf_tensor("sb_p1_Pn", [P, 4, 640], BF16))
    PTs = st.enter_context(nc.sbuf_tensor("sb_p1_PT", [P, 2, 640], BF16))
    stats = st.enter_context(nc.sbuf_tensor("sb_p1_stats", [P, 5, 8], F32))
    cols = st.enter_context(nc.sbuf_tensor("sb_p1_cols", [P, 8], F32))
    fmb = bufB[:, :, 0:1024]
    kT = bufB[:, :, 1024:2048]
    vbv = bufA[:, 8:16, :]

    with nc.Block() as blk:
        @blk.gpsimd
        def _(g):
            sems.inc(g.dma_start(mask_sb[:, :, :],
                                 w["mask1"].rearrange("g p n -> p g n")), SD, 16)
            sems.inc(g.dma_start(bprojr[:, :], w["bprojr"]), SD, 16)
            g.wait_ge(sems.h[SD], sems.val(SD))

        @blk.vector
        def _(v):
            v.memset(cols[:, 6:7], -16.0)

        @blk.sync
        def _(s):
            sems.inc(s.dma_start(cosq[:, :, :], w["ropeq"][0].rearrange("g p n -> p g n")), "p1ds", 16)
            sems.inc(s.dma_start(sinq[:, :, :], w["ropeq"][1].rearrange("g p n -> p g n")), "p1ds", 16)
            sems.inc(s.dma_start(cosk[:, :, :], w["ropek"][0].rearrange("g p n -> p g n")), "p1ds", 16)
            sems.inc(s.dma_start(sink[:, :, :], w["ropek"][1].rearrange("g p n -> p g n")), "p1ds", 16)
            s.wait_ge(sems.h["p1ds"], sems.val("p1ds"))

    for b in range(B):
        with nc.Block() as blk:
            x0 = sems.val("p1ds")

            @blk.sync
            def _(s):
                sems.inc(s.dma_start(
                    hstage[:, :, :],
                    xh_ap.rearrange("t p d -> p t d")[:, 4 * b:4 * b + 4, :]), "p1ds", 16)

            @blk.vector
            def _(v):
                v.wait_ge(sems.h["p1ds"], x0 + 16)

            @blk.scalar
            def _(sc):
                sc.wait_ge(sems.h["p1ds"], x0 + 16)

            tiles8 = [xbuf[:, 4 * b + i, :] for i in range(4)] + \
                     [hstage[:, j, :] for j in range(4)]
            meanl, rstdl = emit_ln_stats(nc, blk, sems, tiles8, D,
                                         stats, bufA[:, 8, :], epsc, "p1")

        with nc.Block() as blk:
            @blk.vector
            def _(v):
                v = VP(v)
                for i in range(8):
                    srct = (xbuf[:, 4 * b + i, :] if i < 4 else hstage[:, i - 4, :])
                    v.tensor_scalar(bufA[:, i, :], srct, meanl[:, i:i + 1],
                                    rstdl[:, i:i + 1], Op.subtract, Op.mult)

        with ExitStack() as pst, nc.Block() as blk:
            psT = [pst.enter_context(nc.psum_tensor(f"ps_p1_T{i}_b{b}", [P, D], BF16))
                   for i in range(2)]
            t0, v0 = sems.val(ST), sems.val(SV)

            @blk.tensor
            def _(te):
                for i in range(8):
                    if i >= 2:
                        sems.wait(te, SV, v0 + i - 1)
                    for c in range(8):
                        instr = te.transpose(psT[i % 2][:, c * P:(c + 1) * P],
                                             bufA[:, i, c * P:(c + 1) * P], id_bf[:, :])
                    sems.inc(instr, ST)

            @blk.vector
            def _(v):
                v = VP(v)
                for i in range(8):
                    col = 512 + 128 * i if i < 4 else 128 * (i - 4)
                    sems.wait(v, ST, t0 + i + 1)
                    instr = v.tensor_copy(
                        fmb[:, :, col:col + 128],
                        psT[i % 2][:, :].rearrange("p (c n) -> p c n", n=P))
                    sems.inc(instr, SV)

        with ExitStack() as pst, nc.Block() as blk:
            psq = [pst.enter_context(nc.psum_tensor(f"ps_p1_q{i}_b{b}", [P, 512], F32))
                   for i in range(2)]
            pstr = [pst.enter_context(nc.psum_tensor(f"ps_p1_r{i}_b{b}", [P, 1024], BF16))
                    for i in range(2)]
            t0, v0, w0 = sems.val(ST), sems.val(SV), sems.val("p1w")
            sched = []
            for ns in range(6):
                for ti in range(4 if ns < 2 else 8):
                    sched.append((ns, ti))
            NJ = len(sched)
            tmm, ttr = {}, {}
            tc = t0 + 1
            tmm[0] = tc
            for j in range(NJ):
                if j + 1 < NJ:
                    tc += 1
                    tmm[j + 1] = tc
                if sched[j][0] < 4:
                    tc += 1
                    ttr[j] = tc
            vps, vrp, vtc = {}, {}, {}
            vc = v0
            for j in range(NJ):
                if sched[j][0] < 4:
                    vc += 1; vps[j] = vc
                    vc += 1; vrp[j] = vc
                else:
                    vc += 1; vps[j] = vc
                if j >= 1 and sched[j - 1][0] < 4:
                    vc += 1; vtc[j - 1] = vc
            if sched[NJ - 1][0] < 4:
                vc += 1; vtc[NJ - 1] = vc

            @blk.gpsimd
            def _(g):
                for ns in range(6):
                    if ns >= 2:
                        last = max(j for j, (n2, _) in enumerate(sched) if n2 == ns - 2)
                        sems.wait(g, ST, tmm[last])
                    src = w["WqkvT"][:, ns * 512:(ns + 1) * 512]
                    sems.inc(g.dma_start(wq2[ns % 2][:, :, :],
                                         src.rearrange("(c p) n -> p c n", p=P)),
                             "p1w", 16)

            def emit_mm(te, j):
                ns, ti = sched[j]
                sems.wait(te, "p1w", w0 + 16 * (ns + 1))
                if j >= 2:
                    sems.wait(te, SV, vps[j - 2])
                col = 512 + 128 * ti if ns < 2 else 128 * ti
                for c in range(8):
                    instr = te.matmul(psq[j % 2][:, :], fmb[:, c, col:col + 128],
                                      wq2[ns % 2][:, c, :],
                                      start=(c == 0), stop=(c == 7))
                sems.inc(instr, ST)

            @blk.tensor
            def _(te):
                emit_mm(te, 0)
                for j, (ns, ti) in enumerate(sched):
                    if j + 1 < NJ:
                        emit_mm(te, j + 1)
                    if ns < 4:
                        sems.wait(te, SV, vrp[j])
                        if j >= 2 and sched[j - 2][0] < 4:
                            sems.wait(te, SV, vtc[j - 2])
                        for i in range(8):
                            par, slot = i % 2, i // 2
                            instr = te.transpose(
                                pstr[j % 2][64 * par:64 * par + 64, slot * 128:(slot + 1) * 128],
                                qrope[:, j % 2, i * 64:(i + 1) * 64], id_bf[:, :])
                        sems.inc(instr, ST)

            def emit_copy(v, j):
                ns, ti = sched[j]
                sems.wait(v, ST, ttr[j])
                if ns < 2:
                    dst = qT[:, 4 * ns:4 * ns + 4, ti * 128:(ti + 1) * 128]
                else:
                    dst = kT[:, 4 * (ns - 2):4 * (ns - 2) + 4, ti * 128:(ti + 1) * 128]
                instr = v.tensor_copy(
                    dst, pstr[j % 2][:, 0:512].rearrange("p (s n) -> p s n", n=P))
                sems.inc(instr, SV)

            @blk.vector
            def _(v):
                v = VP(v)
                for j, (ns, ti) in enumerate(sched):
                    sems.wait(v, ST, tmm[j])
                    if ns < 4:
                        if ns < 2:
                            cosv, sinv, gidx = cosq, sinq, ti
                        else:
                            cosv, sinv, gidx = cosk, sink, ti
                        pv = psq[j % 2][:, :].rearrange("p (h d) -> p h d", d=64)
                        pvs = psq[j % 2][:, :].rearrange("p (h a c) -> p h a c", a=32, c=2)[
                            :, :, :, ::-1]
                        cb = cosv[:, gidx, :].unsqueeze(1).broadcast_to([P, 8, 64])
                        sb = sinv[:, gidx, :].rearrange("p (a c) -> p a c", c=2).unsqueeze(
                            1).broadcast_to([P, 8, 32, 2])
                        v.tensor_tensor(scrA[:, :].rearrange("p (h d) -> p h d", d=64),
                                        pv, cb, Op.mult)
                        instr = v.tensor_tensor(
                            scrB[:, :].rearrange("p (h a c) -> p h a c", a=32, c=2),
                            pvs, sb, Op.mult)
                        sems.inc(instr, SV)
                        if j >= 2 and sched[j - 2][0] < 4:
                            sems.wait(v, ST, ttr[j - 2])
                        instr = v.tensor_tensor(qrope[:, j % 2, :], scrA[:, :],
                                                scrB[:, :], Op.add)
                        sems.inc(instr, SV)
                    else:
                        instr = v.tensor_copy(vbv[:, ti, (ns - 4) * 512:(ns - 3) * 512],
                                              psq[j % 2][:, :])
                        sems.inc(instr, SV)
                    if j >= 1 and sched[j - 1][0] < 4:
                        emit_copy(v, j - 1)
                if sched[NJ - 1][0] < 4:
                    emit_copy(v, NJ - 1)

        with ExitStack() as pst, nc.Block() as blk:
            pss = [pst.enter_context(nc.psum_tensor(f"ps_p1_s{i}_b{b}", [P, D], F32))
                   for i in range(2)]
            pstr2 = [pst.enter_context(nc.psum_tensor(f"ps_p1_t{i}_b{b}", [P, 1024], BF16))
                     for i in range(2)]
            psyp = [pst.enter_context(nc.psum_tensor(f"ps_p1_y{i}_b{b}", [P, 512], F32))
                    for i in range(2)]
            t0, v0, s0 = sems.val(ST), sems.val(SV), sems.val(SS)
            NI = 64
            NSLOT = NI + 3
            _S, _T, _PV = {}, {}, {}
            _EX, _PT, _YPC = {}, {}, {}
            _MK, _NM = {}, {}
            tc, vc, sc_ = t0, v0, s0
            for j in range(3):
                tc += 1; _S[j] = tc
            for i in range(NSLOT):
                if i + 3 < NI:
                    tc += 1; _S[i + 3] = tc
                if 0 <= i - 1 < NI:
                    tc += 1; _T[i - 1] = tc
                if 0 <= i - 2 < NI:
                    tc += 1; _PV[i - 2] = tc
            for j in range(2):
                sc_ += 1; _EX[j] = sc_
            for i in range(NSLOT):
                if i + 2 < NI:
                    sc_ += 1; _EX[i + 2] = sc_
                if 0 <= i - 1 < NI:
                    sc_ += 1; _PT[i - 1] = sc_
                if i >= 10 and (i - 10) % 8 == 0 and (i - 10) // 8 < 8:
                    sc_ += 1; _YPC[(i - 10) // 8] = sc_
            vc += 1; _MK[0] = vc
            for i in range(NSLOT):
                if i + 1 < NI:
                    vc += 1; _MK[i + 1] = vc
                if 0 <= i < NI:
                    vc += 1; _NM[i] = vc

            def emit_S(te, j):
                h, g = j // 4, j % 4
                par, hp = h % 2, h // 2
                if j >= 2:
                    sems.wait(te, SS, _EX[j - 2])
                lt = qT[64 * par:64 * par + 64, hp, g * 128:(g + 1) * 128]
                kv = kT[64 * par:64 * par + 64, hp, :]
                te.matmul(pss[j % 2][:, 0:512], lt, kv[:, g * 128:g * 128 + 512],
                          start=True, stop=True)
                instr = te.matmul(pss[j % 2][:, 512:640], lt,
                                  kv[:, g * 128 + 512:g * 128 + 640],
                                  start=True, stop=True)
                sems.inc(instr, ST)

            @blk.tensor
            def _(te):
                for j in range(3):
                    emit_S(te, j)
                for i in range(NSLOT):
                    if i + 3 < NI:
                        emit_S(te, i + 3)
                    if 0 <= i - 1 < NI:
                        j = i - 1
                        sems.wait(te, SV, _NM[j])
                        if j >= 2:
                            sems.wait(te, SS, _PT[j - 2])
                        for kc in range(5):
                            instr = te.transpose(pstr2[j % 2][:, kc * 128:(kc + 1) * 128],
                                                 Pn[:, j % 4, kc * 128:(kc + 1) * 128],
                                                 id_bf[:, :])
                        sems.inc(instr, ST)
                    if 0 <= i - 2 < NI:
                        j = i - 2
                        h, g = j // 4, j % 4
                        par, hp = h % 2, h // 2
                        sems.wait(te, SS, _PT[j])
                        if j % 8 == 0 and hp >= 2:
                            sems.wait(te, SS, _YPC[hp - 2])
                        for kc in range(5):
                            instr = te.matmul(
                                psyp[hp % 2][64 * par:64 * par + 64, g * 128:(g + 1) * 128],
                                vbv[:, g + kc, h * 64:(h + 1) * 64],
                                PTs[:, j % 2, kc * 128:(kc + 1) * 128],
                                start=(kc == 0), stop=(kc == 4))
                        sems.inc(instr, ST)

            @blk.vector
            def _(v):
                v = VP(v)

                def emit_stt(j):
                    g = j % 4
                    sems.wait(v, SS, _EX[j])
                    instr = v.scalar_tensor_tensor(
                        Pn[:, j % 4, :], Pn[:, j % 4, :], 1.0, mask_sb[:, g, :],
                        Op.mult, Op.mult,
                        accum_out=cols[:, 2 + (j % 2):3 + (j % 2)])
                    sems.inc(instr, SV)

                emit_stt(0)
                for i in range(NSLOT):
                    if i + 1 < NI:
                        emit_stt(i + 1)
                    if 0 <= i < NI:
                        v.reciprocal(cols[:, 4 + (i % 2):5 + (i % 2)],
                                     cols[:, 2 + (i % 2):3 + (i % 2)])
                        instr = v.tensor_scalar(Pn[:, i % 4, :], Pn[:, i % 4, :],
                                                cols[:, 4 + (i % 2):5 + (i % 2)],
                                                None, Op.mult)
                        sems.inc(instr, SV)

            @blk.scalar
            def _(sc):
                def emit_exp(j):
                    sems.wait(sc, ST, _S[j])
                    if j >= 4:
                        sems.wait(sc, ST, _T[j - 4])
                    instr = sc.activation(Pn[:, j % 4, :], pss[j % 2][:, 0:640],
                                          AF.Exp, bias=cols[:, 6:7])
                    sems.inc(instr, SS)

                emit_exp(0)
                emit_exp(1)
                for i in range(NSLOT):
                    if i + 2 < NI:
                        emit_exp(i + 2)
                    if 0 <= i - 1 < NI:
                        j = i - 1
                        sems.wait(sc, ST, _T[j])
                        if j >= 2:
                            sems.wait(sc, ST, _PV[j - 2])
                        instr = sc.copy(PTs[:, j % 2, :], pstr2[j % 2][:, 0:640])
                        sems.inc(instr, SS)
                    if i >= 10 and (i - 10) % 8 == 0 and (i - 10) // 8 < 8:
                        hp = (i - 10) // 8
                        sems.wait(sc, ST, _PV[8 * hp + 7])
                        instr = sc.copy(yfm[:, hp, :], psyp[hp % 2][:, :])
                        sems.inc(instr, SS)

        with ExitStack() as pst, nc.Block() as blk:
            pso = [pst.enter_context(nc.psum_tensor(f"ps_p1_o{i}_b{b}", [P, D], F32))
                   for i in range(2)]
            t0, w0 = sems.val(ST), sems.val("p1w")
            mm_vals = {g: t0 + g + 1 for g in range(4)}
            scrOv = bufA[:, 10:16, :].rearrange("p a b -> p (a b)").bitcast(
                F32).rearrange("p (a b) -> p a b", b=D)
            _EV = emit_postln_pipe(nc, blk, sems, "p1", mm_vals, pso,
                                   [4 * b + g for g in range(4)],
                                   xbuf, scrOv, bufA[:, 8, :], stats, epsc, 4)

            @blk.gpsimd
            def _(g):
                for half in range(2):
                    src = w["WprojT"][:, half * 512:(half + 1) * 512]
                    sems.inc(g.dma_start(wq2[half][:, :, :],
                                         src.rearrange("(c p) n -> p c n", p=P)),
                             "p1w", 16)

            @blk.tensor
            def _(te):
                sems.wait(te, "p1w", w0 + 32)
                for g in range(4):
                    if g >= 2:
                        sems.wait(te, SV, _EV[g - 2])
                    for half in range(2):
                        for hp in range(8):
                            te.matmul(pso[g % 2][:, half * 512:(half + 1) * 512],
                                      yfm[:, hp, g * 128:(g + 1) * 128],
                                      wq2[half][:, hp, :],
                                      start=(hp == 0), stop=False)
                        instr = te.matmul(pso[g % 2][:, half * 512:(half + 1) * 512],
                                          ones_row[:, :],
                                          bprojr[:, half * 512:(half + 1) * 512],
                                          start=False, stop=True)
                    sems.inc(instr, ST)

    return sems



def emit_phase2(nc, st, env):
    xbuf, bufA, bufB = env["xbuf"], env["bufA"], env["bufB"]
    ones_row, id_bf = env["ones_row"], env["id_bf"]
    epsc = env["epsc"][:, :]
    w = env["w2"]
    sems = Sems(nc, st, ["p2dma", "p2ds", "v_p2", "s_p2", "t_p2"])
    SV, SS, ST, SD = "v_p2", "s_p2", "t_p2", "p2dma"

    Win_bf = st.enter_context(nc.sbuf_tensor("sb_p2_Win", [P, 8, 64], BF16))
    binr = st.enter_context(nc.sbuf_tensor("sb_p2_binr", [1, 64], BF16))
    Wout_bf = st.enter_context(nc.sbuf_tensor("sb_p2_Wout", [64, D], BF16))
    boutr = st.enter_context(nc.sbuf_tensor("sb_p2_boutr", [1, D], BF16))
    W1b = st.enter_context(nc.sbuf_tensor("sb_p2_W1b", [P, D], F32))
    b1b = st.enter_context(nc.sbuf_tensor("sb_p2_b1b", [P, D], F32))
    W2b = st.enter_context(nc.sbuf_tensor("sb_p2_W2b", [P, D], F32))
    b2b = st.enter_context(nc.sbuf_tensor("sb_p2_b2b", [P, 64], F32))
    s_f32 = st.enter_context(nc.sbuf_tensor("sb_p2_s", [P, NT, 64], F32))
    gbuf = st.enter_context(nc.sbuf_tensor("sb_p2_g", [P, NT, 64], BF16))
    gfm = st.enter_context(nc.sbuf_tensor("sb_p2_gfm", [64, NT, P], BF16))
    scrE1 = st.enter_context(nc.sbuf_tensor("sb_p2_scr1", [P, 2, D], F32))
    scrE2 = st.enter_context(nc.sbuf_tensor("sb_p2_scr2", [P, 2, D], BF16))
    stats = st.enter_context(nc.sbuf_tensor("sb_p2_stats", [P, 5, NT], F32))
    stats2 = st.enter_context(nc.sbuf_tensor("sb_p2_stats2", [P, 5, NT], F32))


    with nc.Block() as blk:
        @blk.gpsimd
        def _(g):
            sems.inc(g.dma_start(Win_bf[:, :, :], w["Win"].rearrange("(c p) n -> p c n", p=P)), SD, 16)
            sems.inc(g.dma_start(binr[:, :], w["binr"]), SD, 16)
            sems.inc(g.dma_start(Wout_bf[:, :], w["Wout"]), SD, 16)
            sems.inc(g.dma_start(boutr[:, :], w["boutr"]), SD, 16)

        @blk.sync
        def _(s):
            sems.inc(s.dma_start(W1b[:, :], w["W1b"]), "p2ds", 16)
            sems.inc(s.dma_start(b1b[:, :], w["b1b"]), "p2ds", 16)
            sems.inc(s.dma_start(W2b[:, :], w["W2b"]), "p2ds", 16)
            sems.inc(s.dma_start(b2b[:, :], w["b2b"]), "p2ds", 16)
            s.wait_ge(sems.h[SD], sems.val(SD))
            s.wait_ge(sems.h["p2ds"], sems.val("p2ds"))

        @blk.gpsimd
        def _(g):
            g.wait_ge(sems.h[SD], sems.val(SD))
            g.wait_ge(sems.h["p2ds"], sems.val("p2ds"))

    with nc.Block() as blk:
        mean, rstd = emit_ln_stats(nc, blk, sems,
                                   [xbuf[:, t, :] for t in range(NT)], D,
                                   stats, bufA[:, 0, :], epsc, "p2")

    with nc.Block() as blk:
        @blk.vector
        def _(v):
            v = VP(v)
            for t in range(NT):
                instr = v.tensor_scalar(bufA[:, t, :], xbuf[:, t, :],
                                        mean[:, t:t + 1], rstd[:, t:t + 1],
                                        Op.subtract, Op.mult)

    with ExitStack() as pst, nc.Block() as blk:
        psT = [pst.enter_context(nc.psum_tensor(f"sb_p2_psT{i}", [P, D], BF16)) for i in range(2)]
        t0, v0 = sems.val(ST), sems.val(SV)

        @blk.tensor
        def _(te):
            for t in range(NT):
                if t >= 2:
                    sems.wait(te, SV, v0 + t - 1)
                for c in range(8):
                    instr = te.transpose(psT[t % 2][:, c * P:(c + 1) * P],
                                         bufA[:, t, c * P:(c + 1) * P], id_bf[:, :])
                sems.inc(instr, ST)

        @blk.vector
        def _(v):
            v = VP(v)
            for t in range(NT):
                sems.wait(v, ST, t0 + t + 1)
                instr = v.tensor_copy(
                    bufB[:, :, t * P:(t + 1) * P],
                    psT[t % 2][:, :].rearrange("p (c n) -> p c n", n=P))
                sems.inc(instr, SV)

    with ExitStack() as pst, nc.Block() as blk:
        psS = [pst.enter_context(nc.psum_tensor(f"sb_p2_psS{i}", [P, 512], F32)) for i in range(2)]
        t0, v0 = sems.val(ST), sems.val(SV)

        @blk.tensor
        def _(te):
            for t in range(NT):
                if t >= 2:
                    sems.wait(te, SV, v0 + t - 1)
                for c in range(8):
                    te.matmul(psS[t % 2][:, 0:64], bufB[:, c, t * P:(t + 1) * P],
                              Win_bf[:, c, :], start=(c == 0), stop=False)
                instr = te.matmul(psS[t % 2][:, 0:64], ones_row[:, :], binr[:, :],
                                  start=False, stop=True)
                sems.inc(instr, ST)

        @blk.vector
        def _(v):
            v = VP(v)
            for t in range(NT):
                sems.wait(v, ST, t0 + t + 1)
                instr = v.tensor_copy(s_f32[:, t, :], psS[t % 2][:, 0:64])
                sems.inc(instr, SV)

    with nc.Block() as blk:
        v0, s0 = sems.val(SV), sems.val(SS)
        W1v = W1b[:, :].rearrange("p (m i) -> p m i", i=16)
        _M1, _M2, _RL = {}, {}, {}
        vc, sc_ = v0, s0
        for i in range(NT + 2):
            if 0 <= i - 2 < NT:
                vc += 1; _M2[i - 2] = vc
            if 0 <= i < NT:
                vc += 1; _M1[i] = vc
        for i in range(NT + 2):
            if 0 <= i - 1 < NT:
                sc_ += 1; _RL[i - 1] = sc_

        @blk.vector
        def _(v):
            v = VP(v)
            for i in range(NT + 2):
                if 0 <= i - 2 < NT:
                    t = i - 2
                    pa = t % 2
                    sems.wait(v, SS, _RL[t])
                    v.tensor_tensor(scrE1[:, pa, :], scrE2[:, pa, :], W2b[:, :], Op.mult)
                    v.tensor_reduce(s_f32[:, t, :], scrE1[:, pa, :].rearrange(
                        "p (m i) -> p m i", i=16), AX.X, Op.add)
                    instr = v.tensor_tensor(gbuf[:, t, :], s_f32[:, t, :], b2b[:, :], Op.add)
                    sems.inc(instr, SV)
                if 0 <= i < NT:
                    t = i
                    pa = t % 2
                    sv = s_f32[:, t, :].rearrange("p (m o) -> p m o", o=1)
                    v.tensor_tensor(scrE1[:, pa, :].rearrange("p (m i) -> p m i", i=16),
                                    sv.broadcast_to([P, 64, 16]), W1v, Op.mult)
                    instr = v.tensor_tensor(scrE1[:, pa, :], scrE1[:, pa, :],
                                            b1b[:, :], Op.add)
                    sems.inc(instr, SV)

        @blk.scalar
        def _(sc):
            for i in range(NT + 2):
                if 0 <= i - 1 < NT:
                    t = i - 1
                    pa = t % 2
                    sems.wait(sc, SV, _M1[t])
                    instr = sc.activation(scrE2[:, pa, :], scrE1[:, pa, :], AF.Relu)
                    sems.inc(instr, SS)


    with ExitStack() as pst, nc.Block() as blk:
        psG = [pst.enter_context(nc.psum_tensor(f"sb_p2_psG{i}", [64, 1024], BF16)) for i in range(2)]
        t0, v0 = sems.val(ST), sems.val(SV)

        @blk.tensor
        def _(te):
            for bank in range(2):
                for slot in range(8):
                    t = bank * 8 + slot
                    instr = te.transpose(psG[bank][:, slot * P:(slot + 1) * P],
                                         gbuf[:, t, :], id_bf[:, :])
                sems.inc(instr, ST)

        @blk.vector
        def _(v):
            v = VP(v)
            for bank in range(2):
                sems.wait(v, ST, t0 + bank + 1)
                instr = v.tensor_copy(
                    gfm[:, bank * 8:(bank + 1) * 8, :],
                    psG[bank][:, :].rearrange("p (s n) -> p s n", n=P))
                sems.inc(instr, SV)

    with ExitStack() as pst, nc.Block() as blk:
        psO = [pst.enter_context(nc.psum_tensor(f"sb_p2_psO{i}", [P, D], F32)) for i in range(2)]
        scr3 = pst.enter_context(nc.sbuf_tensor("sb_p2_scr3", [P, 3, D], F32))
        t0 = sems.val(ST)
        mm_vals = {t: t0 + t + 1 for t in range(NT)}
        _EV = emit_postln_pipe(nc, blk, sems, "p2", mm_vals, psO, list(range(NT)),
                               xbuf, scr3, bufA[:, 0, :], stats2, epsc, NT)

        @blk.tensor
        def _(te):
            for t in range(NT):
                if t >= 2:
                    sems.wait(te, SV, _EV[t - 2])
                for half in range(2):
                    te.matmul(psO[t % 2][:, half * 512:(half + 1) * 512],
                              gfm[:, t, :], Wout_bf[:, half * 512:(half + 1) * 512],
                              start=True, stop=False)
                    instr = te.matmul(psO[t % 2][:, half * 512:(half + 1) * 512],
                                      ones_row[:, :], boutr[:, half * 512:(half + 1) * 512],
                                      start=False, stop=True)
                sems.inc(instr, ST)



def emit_ln_to_fm(nc, env, sems, stats, tag):
    xbuf, bufA, bufB = env["xbuf"], env["bufA"], env["bufB"]
    id_bf = env["id_bf"]
    epsc = env["epsc"][:, :]
    SV, SS, ST = f"v_{tag}", f"s_{tag}", f"t_{tag}"
    sums, sumsq = stats[:, 0, :], stats[:, 1, :]
    mean, var, rstd = stats[:, 2, :], stats[:, 3, :], stats[:, 4, :]

    with ExitStack() as pst, nc.Block() as blk:
        psT = [pst.enter_context(nc.psum_tensor(f"ps_{tag}_lnT{i}", [P, D], BF16))
               for i in range(2)]
        v0, s0, t0 = sems.val(SV), sems.val(SS), sems.val(ST)
        NS = NT + 7
        _RD, _VA, _APL, _CP, _SQ, _RS, _TR = {}, {}, {}, {}, {}, {}, {}
        vc, sc_, tc = v0, s0, t0
        for i in range(NS):
            if 0 <= i < NT:
                vc += 1; _RD[i] = vc
            if 0 <= i - 2 < NT:
                vc += 1; _VA[i - 2] = vc
            if 0 <= i - 4 < NT:
                vc += 1; _APL[i - 4] = vc
            if 0 <= i - 6 < NT:
                vc += 1; _CP[i - 6] = vc
        for i in range(NS):
            if 0 <= i - 1 < NT:
                sc_ += 1; _SQ[i - 1] = sc_
            if 0 <= i - 3 < NT:
                sc_ += 1; _RS[i - 3] = sc_
        for i in range(NS):
            if 0 <= i - 5 < NT:
                tc += 1; _TR[i - 5] = tc

        @blk.vector
        def _(v):
            v = VP(v)
            for i in range(NS):
                if 0 <= i < NT:
                    t = i
                    instr = v.tensor_reduce(sums[:, t:t + 1], xbuf[:, t, :], AX.X, Op.add)
                    sems.inc(instr, SV)
                if 0 <= i - 2 < NT:
                    t = i - 2
                    sems.wait(v, SS, _SQ[t])
                    v.tensor_scalar(mean[:, t:t + 1], sums[:, t:t + 1], 1.0 / D, None, Op.mult)
                    v.tensor_scalar(var[:, t:t + 1], sumsq[:, t:t + 1], 1.0 / D, None, Op.mult)
                    v.tensor_tensor(sums[:, t:t + 1], mean[:, t:t + 1], mean[:, t:t + 1], Op.mult)
                    instr = v.tensor_tensor(var[:, t:t + 1], var[:, t:t + 1],
                                            sums[:, t:t + 1], Op.subtract)
                    sems.inc(instr, SV)
                if 0 <= i - 4 < NT:
                    t = i - 4
                    sems.wait(v, SS, _RS[t])
                    instr = v.tensor_scalar(bufA[:, t, :], xbuf[:, t, :],
                                            mean[:, t:t + 1], rstd[:, t:t + 1],
                                            Op.subtract, Op.mult)
                    sems.inc(instr, SV)
                if 0 <= i - 6 < NT:
                    t = i - 6
                    sems.wait(v, ST, _TR[t])
                    instr = v.tensor_copy(
                        bufB[:, :, t * P:(t + 1) * P],
                        psT[t % 2][:, :].rearrange("p (c n) -> p c n", n=P))
                    sems.inc(instr, SV)

        @blk.scalar
        def _(sc):
            for i in range(NS):
                if 0 <= i - 1 < NT:
                    t = i - 1
                    instr = sc.activation(bufA[:, t, :], xbuf[:, t, :], AF.Square,
                                          accum_out=sumsq[:, t:t + 1])
                    sc.drain()
                    sems.inc(instr, SS)
                if 0 <= i - 3 < NT:
                    t = i - 3
                    sems.wait(sc, SV, _VA[t])
                    sc.activation(rstd[:, t:t + 1], var[:, t:t + 1], AF.Ln, bias=epsc)
                    instr = sc.activation(rstd[:, t:t + 1], rstd[:, t:t + 1],
                                          AF.Exp, scale=-0.5)
                    sems.inc(instr, SS)

        @blk.tensor
        def _(te):
            for i in range(NS):
                if 0 <= i - 5 < NT:
                    t = i - 5
                    sems.wait(te, SV, _APL[t])
                    if t >= 2:
                        sems.wait(te, SV, _CP[t - 2])
                    for c in range(8):
                        instr = te.transpose(psT[t % 2][:, c * P:(c + 1) * P],
                                             bufA[:, t, c * P:(c + 1) * P], id_bf[:, :])
                    sems.inc(instr, ST)
    return mean, rstd



_TMFM_N = [0]


def emit_tm_to_fm(nc, sems, tag, src, dst, id_bf):
    SV, ST = f"v_{tag}", f"t_{tag}"
    _TMFM_N[0] += 1
    with ExitStack() as pst, nc.Block() as blk:
        psT = [pst.enter_context(nc.psum_tensor(f"ps_{tag}_T{i}_{_TMFM_N[0]}",
                                                [P, D], BF16)) for i in range(2)]
        t0, v0 = sems.val(ST), sems.val(SV)

        @blk.tensor
        def _(te):
            for t in range(NT):
                if t >= 2:
                    sems.wait(te, SV, v0 + t - 1)
                for c in range(8):
                    instr = te.transpose(psT[t % 2][:, c * P:(c + 1) * P],
                                         src[:, t, c * P:(c + 1) * P], id_bf[:, :])
                sems.inc(instr, ST)

        @blk.vector
        def _(v):
            v = VP(v)
            for t in range(NT):
                sems.wait(v, ST, t0 + t + 1)
                instr = v.tensor_copy(
                    dst[:, :, t * P:(t + 1) * P],
                    psT[t % 2][:, :].rearrange("p (c n) -> p c n", n=P))
                sems.inc(instr, SV)



def emit_phase3(nc, st, env):
    xbuf, bufA, bufB = env["xbuf"], env["bufA"], env["bufB"]
    ones_row, id_bf = env["ones_row"], env["id_bf"]
    epsc = env["epsc"][:, :]
    w = env["w3"]
    sems = Sems(nc, st, ["p3dma", "p3ds", "v_p3", "s_p3", "t_p3"])
    SV, SS, ST, SD = "v_p3", "s_p3", "t_p3", "p3dma"

    b1r = st.enter_context(nc.sbuf_tensor("sb_p3_b1r", [1, D], BF16))
    bwr = st.enter_context(nc.sbuf_tensor("sb_p3_bwr", [1, 496], BF16))
    b2r = st.enter_context(nc.sbuf_tensor("sb_p3_b2r", [1, D], BF16))
    h1buf = st.enter_context(nc.sbuf_tensor("sb_p3_h1", [P, NT, D], BF16))
    wsel = st.enter_context(nc.sbuf_tensor("sb_p3_wsel", [P, NT, 64], F32))
    wlscr = st.enter_context(nc.sbuf_tensor("sb_p3_wl", [P, 496], F32))
    escr = st.enter_context(nc.sbuf_tensor("sb_p3_e", [P, 496], F32))
    hstat = st.enter_context(nc.sbuf_tensor("sb_p3_hs", [P, 3, 16], F32))
    scr1 = st.enter_context(nc.sbuf_tensor("sb_p3_s1", [P, D], F32))
    scr2 = st.enter_context(nc.sbuf_tensor("sb_p3_s2", [P, D], F32))
    gb = env["bufB"][:, 2, :].bitcast(F32)
    bb = env["bufB"][:, 4, :].bitcast(F32)
    sqd = env["bufB"][:, 0, 0:D]
    stats = st.enter_context(nc.sbuf_tensor("sb_p3_stats", [P, 5, NT], F32))
    statsR = st.enter_context(nc.sbuf_tensor("sb_p3_statsR", [P, 5, NT], F32))
    statsA = st.enter_context(nc.sbuf_tensor("sb_p3_statsA", [P, 5, NT], F32))
    wst = ExitStack()
    w1T = wst.enter_context(nc.sbuf_tensor("sb_p3_w1T", [P, 8, D], BF16))
    wwT = wst.enter_context(nc.sbuf_tensor("sb_p3_wwT", [P, 8, 496], BF16))

    with nc.Block() as blk:
        @blk.gpsimd
        def _(g):
            sems.inc(g.dma_start(w1T[:, :, :], w["w1T"].rearrange("(c p) n -> p c n", p=P)), SD, 16)
            sems.inc(g.dma_start(wwT[:, :, :], w["wwT"].rearrange("(c p) n -> p c n", p=P)), SD, 16)
            sems.inc(g.dma_start(b1r[:, :], w["b1r"]), SD, 16)
            sems.inc(g.dma_start(bwr[:, :], w["bwr"]), SD, 16)
            sems.inc(g.dma_start(b2r[:, :], w["b2r"]), SD, 16)
            g.wait_ge(sems.h[SD], sems.val(SD))

        @blk.sync
        def _(s):
            s.wait_ge(sems.h[SD], sems.val(SD))

    mean3, rstd3 = emit_ln_to_fm(nc, env, sems, stats, "p3")

    with ExitStack() as pst, nc.Block() as blk:
        psH = [pst.enter_context(nc.psum_tensor(f"ps_p3_h{i}", [P, D], F32))
               for i in range(2)]
        t0, v0 = sems.val(ST), sems.val(SV)

        @blk.tensor
        def _(te):
            for t in range(NT):
                if t >= 2:
                    sems.wait(te, SV, v0 + t - 1)
                for half in range(2):
                    for c in range(8):
                        te.matmul(psH[t % 2][:, half * 512:(half + 1) * 512],
                                  bufB[:, c, t * P:(t + 1) * P],
                                  w1T[:, c, half * 512:(half + 1) * 512],
                                  start=(c == 0), stop=False)
                    instr = te.matmul(psH[t % 2][:, half * 512:(half + 1) * 512],
                                      ones_row[:, :], b1r[:, half * 512:(half + 1) * 512],
                                      start=False, stop=True)
                sems.inc(instr, ST)

        @blk.vector
        def _(v):
            v = VP(v)
            for t in range(NT):
                sems.wait(v, ST, t0 + t + 1)
                instr = v.tensor_copy(h1buf[:, t, :], psH[t % 2][:, :])
                sems.inc(instr, SV)

    emit_tm_to_fm(nc, sems, "p3", h1buf, bufB, id_bf)

    with ExitStack() as pst, nc.Block() as blk:
        psW = [pst.enter_context(nc.psum_tensor(f"ps_p3_w{i}", [P, 512], F32))
               for i in range(2)]
        t0, v0, s0 = sems.val(ST), sems.val(SV), sems.val(SS)

        @blk.tensor
        def _(te):
            for t in range(NT):
                if t >= 2:
                    sems.wait(te, SV, v0 + 3 * (t - 2) + 1)
                for c in range(8):
                    te.matmul(psW[t % 2][:, 0:496], bufB[:, c, t * P:(t + 1) * P],
                              wwT[:, c, :], start=(c == 0), stop=False)
                instr = te.matmul(psW[t % 2][:, 0:496], ones_row[:, :], bwr[:, :],
                                  start=False, stop=True)
                sems.inc(instr, ST)

        @blk.vector
        def _(v):
            v = VP(v)
            for t in range(NT):
                sems.wait(v, ST, t0 + t + 1)
                v.tensor_copy(wlscr[:, :], psW[t % 2][:, :496])
                instr = v.tensor_reduce(
                    hstat[:, 0, :], wlscr[:, :].rearrange("p (h k) -> p h k", k=31),
                    AX.X, Op.max, negate=True)
                sems.inc(instr, SV)
                nm = hstat[:, 0, :].rearrange("p (h o) -> p h o", o=1)
                instr = v.tensor_tensor(
                    wlscr[:, :].rearrange("p (h k) -> p h k", k=31),
                    wlscr[:, :].rearrange("p (h k) -> p h k", k=31),
                    nm.broadcast_to([P, 16, 31]), Op.add)
                sems.inc(instr, SV)
                sems.wait(v, SS, s0 + t + 1)
                v.tensor_reduce(hstat[:, 1, :],
                                escr[:, :].rearrange("p (h k) -> p h k", k=31),
                                AX.X, Op.add)
                v.reciprocal(hstat[:, 2, :], hstat[:, 1, :])
                rs = hstat[:, 2, :].rearrange("p (h o) -> p h o", o=1)
                ev = escr[:, :].rearrange("p (h k) -> p h k", k=31)[:, :, 27:31]
                instr = v.tensor_tensor(
                    wsel[:, t, :].rearrange("p (h k) -> p h k", k=4),
                    ev, rs.broadcast_to([P, 16, 4]), Op.mult)
                sems.inc(instr, SV)
                tb, gph = t // 4, t % 4
                acc = scr1[:, :]
                for j in range(tb + 1):
                    wv = wsel[:, t, :].rearrange("p (h k) -> p h k", k=4)[
                        :, :, 3 - tb + j:4 - tb + j]
                    hv = h1buf[:, j * 4 + gph, :].rearrange("p (h r) -> p h r", r=64)
                    if j == 0:
                        v.tensor_tensor(acc.rearrange("p (h r) -> p h r", r=64),
                                        hv, wv.broadcast_to([P, 16, 64]), Op.mult)
                    else:
                        v.tensor_tensor(scr2[:, :].rearrange("p (h r) -> p h r", r=64),
                                        hv, wv.broadcast_to([P, 16, 64]), Op.mult)
                        v.tensor_tensor(acc, acc, scr2[:, :], Op.add)
                v.tensor_copy(bufA[:, t, :], acc)

        @blk.scalar
        def _(sc):
            for t in range(NT):
                sems.wait(sc, SV, v0 + 3 * t + 2)
                instr = sc.activation(escr[:, :], wlscr[:, :], AF.Exp)
                sems.inc(instr, SS)

    wst.close()
    w2T = st.enter_context(nc.sbuf_tensor("sb_p3_w2T", [P, 8, D], BF16))
    with nc.Block() as blk:
        @blk.gpsimd
        def _(g):
            sems.inc(g.dma_start(w2T[:, :, :], w["w2T"].rearrange("(c p) n -> p c n", p=P)), SD, 16)
            g.wait_ge(sems.h[SD], sems.val(SD))

    emit_tm_to_fm(nc, sems, "p3", bufA, bufB, id_bf)

    gb = h1buf[:, 0:2, :].rearrange("p a b -> p (a b)").bitcast(F32)
    bb = h1buf[:, 2:4, :].rearrange("p a b -> p (a b)").bitcast(F32)
    scrH = h1buf[:, 4:8, :].rearrange("p a b -> p (a b)").bitcast(F32).rearrange(
        "p (a b) -> p a b", b=D)
    scrR = h1buf[:, 8:12, :].rearrange("p a b -> p (a b)").bitcast(F32).rearrange(
        "p (a b) -> p a b", b=D)
    scrF = h1buf[:, 12:14, :].rearrange("p a b -> p (a b)").bitcast(F32)
    with nc.Block() as blk:
        @blk.sync
        def _(s):
            sems.inc(s.dma_start(gb, w["gb"]), "p3ds", 16)
            sems.inc(s.dma_start(bb, w["bb"]), "p3ds", 16)
            s.wait_ge(sems.h["p3ds"], sems.val("p3ds"))

        @blk.vector
        def _(v):
            v = VP(v)
            v.tensor_tensor(statsR[:, 0, :], mean3[:, 0:NT], rstd3[:, 0:NT], Op.mult)
            v.tensor_scalar(statsR[:, 0, :], statsR[:, 0, :], -1.0, None, Op.mult)

    with ExitStack() as pst, nc.Block() as blk:
        psH = [pst.enter_context(nc.psum_tensor(f"ps_p3_o{i}", [P, D], F32))
               for i in range(2)]
        t0, v0, s0 = sems.val(ST), sems.val(SV), sems.val(SS)
        negmr3 = statsR[:, 0, :]
        sumsR, sumsqR = statsR[:, 1, :], statsR[:, 2, :]
        varR, rstdR, meanR = statsR[:, 3, :], statsR[:, 4, :], stats[:, 0, :]
        sumsA, sumsqA = statsA[:, 0, :], statsA[:, 1, :]
        meanA, varA, rstdA = statsA[:, 2, :], statsA[:, 3, :], statsA[:, 4, :]
        NS = NT + 6
        _MM, _EVH, _RS_T, _VAR, _T12, _VAA, _APF = {}, {}, {}, {}, {}, {}, {}
        _SQR, _RSR, _SQA, _RSA = {}, {}, {}, {}
        tc, vc, sc_ = t0, v0, s0
        for i in range(NS):
            if 0 <= i < NT:
                tc += 1; _MM[i] = tc
        for i in range(NS):
            if 0 <= i < NT:
                vc += 1; _EVH[i] = vc
            if 0 <= i - 3 < NT:
                vc += 1; _T12[i - 3] = vc
            if 0 <= i - 1 < NT:
                vc += 1; _RS_T[i - 1] = vc
            if 0 <= i - 2 < NT:
                vc += 1; _VAR[i - 2] = vc
            if 0 <= i - 4 < NT:
                vc += 1; _VAA[i - 4] = vc
            if 0 <= i - 5 < NT:
                vc += 1; _APF[i - 5] = vc
        for i in range(NS):
            if 0 <= i - 2 < NT:
                sc_ += 1; _SQR[i - 2] = sc_
                sc_ += 1; _RSR[i - 2] = sc_
            if 0 <= i - 4 < NT:
                sc_ += 1; _SQA[i - 4] = sc_
                sc_ += 1; _RSA[i - 4] = sc_

        @blk.tensor
        def _(te):
            for t in range(NT):
                if t >= 2:
                    sems.wait(te, SV, _EVH[t - 2])
                for half in range(2):
                    for c in range(8):
                        te.matmul(psH[t % 2][:, half * 512:(half + 1) * 512],
                                  bufB[:, c, t * P:(t + 1) * P],
                                  w2T[:, c, half * 512:(half + 1) * 512],
                                  start=(c == 0), stop=False)
                    instr = te.matmul(psH[t % 2][:, half * 512:(half + 1) * 512],
                                      ones_row[:, :], b2r[:, half * 512:(half + 1) * 512],
                                      start=False, stop=True)
                sems.inc(instr, ST)

        @blk.vector
        def _(v):
            v = VP(v)
            for i in range(NS):
                if 0 <= i < NT:
                    t = i
                    sems.wait(v, ST, _MM[t])
                    instr = v.tensor_scalar(scrH[:, t % 2, :], psH[t % 2][:, :],
                                            negmr3[:, t:t + 1], None, Op.add)
                    sems.inc(instr, SV)
                if 0 <= i - 3 < NT:
                    t = i - 3
                    sems.wait(v, SS, _RSR[t])
                    v.tensor_scalar(scrR[:, t % 2, :], scrR[:, t % 2, :],
                                    meanR[:, t:t + 1], rstdR[:, t:t + 1],
                                    Op.subtract, Op.mult)
                    v.tensor_tensor(scrR[:, t % 2, :], scrR[:, t % 2, :], gb, Op.mult)
                    instr = v.scalar_tensor_tensor(
                        bufA[:, t, :], scrR[:, t % 2, :], 1.0, bb,
                        Op.mult, Op.add, accum_out=sumsA[:, t:t + 1])
                    sems.inc(instr, SV)
                if 0 <= i - 1 < NT:
                    t = i - 1
                    instr = v.scalar_tensor_tensor(
                        scrR[:, t % 2, :], xbuf[:, t, :], rstd3[:, t:t + 1],
                        scrH[:, t % 2, :], Op.mult, Op.add,
                        accum_out=sumsR[:, t:t + 1])
                    sems.inc(instr, SV)
                if 0 <= i - 2 < NT:
                    t = i - 2
                    sems.wait(v, SS, _SQR[t])
                    v.tensor_scalar(meanR[:, t:t + 1], sumsR[:, t:t + 1], 1.0 / D, None, Op.mult)
                    v.tensor_scalar(varR[:, t:t + 1], sumsqR[:, t:t + 1], 1.0 / D, None, Op.mult)
                    v.tensor_tensor(sumsR[:, t:t + 1], meanR[:, t:t + 1],
                                    meanR[:, t:t + 1], Op.mult)
                    instr = v.tensor_tensor(varR[:, t:t + 1], varR[:, t:t + 1],
                                            sumsR[:, t:t + 1], Op.subtract)
                    sems.inc(instr, SV)
                if 0 <= i - 4 < NT:
                    t = i - 4
                    sems.wait(v, SS, _SQA[t])
                    v.tensor_scalar(meanA[:, t:t + 1], sumsA[:, t:t + 1], 1.0 / D, None, Op.mult)
                    v.tensor_scalar(varA[:, t:t + 1], sumsqA[:, t:t + 1], 1.0 / D, None, Op.mult)
                    v.tensor_tensor(sumsA[:, t:t + 1], meanA[:, t:t + 1],
                                    meanA[:, t:t + 1], Op.mult)
                    instr = v.tensor_tensor(varA[:, t:t + 1], varA[:, t:t + 1],
                                            sumsA[:, t:t + 1], Op.subtract)
                    sems.inc(instr, SV)
                if 0 <= i - 5 < NT:
                    t = i - 5
                    sems.wait(v, SS, _RSA[t])
                    v.tensor_scalar(scrF, bufA[:, t, :], meanA[:, t:t + 1],
                                    rstdA[:, t:t + 1], Op.subtract, Op.mult)
                    instr = v.tensor_tensor(xbuf[:, t, :], xbuf[:, t, :], scrF, Op.add)
                    sems.inc(instr, SV)

        @blk.scalar
        def _(sc):
            for i in range(NS):
                if 0 <= i - 2 < NT:
                    t = i - 2
                    sems.wait(sc, SV, _RS_T[t])
                    instr = sc.activation(h1buf[:, 15, :], scrR[:, t % 2, :], AF.Square,
                                          accum_out=sumsqR[:, t:t + 1])
                    sc.drain()
                    sems.inc(instr, SS)
                    sems.wait(sc, SV, _VAR[t])
                    sc.activation(rstdR[:, t:t + 1], varR[:, t:t + 1], AF.Ln, bias=epsc)
                    instr = sc.activation(rstdR[:, t:t + 1], rstdR[:, t:t + 1],
                                          AF.Exp, scale=-0.5)
                    sems.inc(instr, SS)
                if 0 <= i - 4 < NT:
                    t = i - 4
                    sems.wait(sc, SV, _T12[t])
                    instr = sc.activation(h1buf[:, 15, :], bufA[:, t, :], AF.Square,
                                          accum_out=sumsqA[:, t:t + 1])
                    sc.drain()
                    sems.inc(instr, SS)
                    sems.wait(sc, SV, _VAA[t])
                    sc.activation(rstdA[:, t:t + 1], varA[:, t:t + 1], AF.Ln, bias=epsc)
                    instr = sc.activation(rstdA[:, t:t + 1], rstdA[:, t:t + 1],
                                          AF.Exp, scale=-0.5)
                    sems.inc(instr, SS)

    return sems



def emit_phase4(nc, st, env, x_reload_src, x_out_dst, spill_dst=None):
    xbuf, bufA, bufB = env["xbuf"], env["bufA"], env["bufB"]
    ones_row, id_bf = env["ones_row"], env["id_bf"]
    id_f32 = env["id_f32"]
    epsc = env["epsc"][:, :]
    w = env["w4"]
    sems = Sems(nc, st, ["p4w", "p4x", "p4y", "p4b", "v_p4", "s_p4", "t_p4"])
    SV, SS, ST = "v_p4", "s_p4", "t_p4"
    y_acc = xbuf

    W2h = st.enter_context(nc.sbuf_tensor("sb_p4_W2h", [P, 16, D], BF16))
    hbuf = st.enter_context(nc.sbuf_tensor("sb_p4_h", [P, 16, 512], BF16))
    Wg_bf = st.enter_context(nc.sbuf_tensor("sb_p4_Wg", [P, 8, 8], BF16))
    bgr = st.enter_context(nc.sbuf_tensor("sb_p4_bgr", [1, 8], BF16))
    b2s = st.enter_context(nc.sbuf_tensor("sb_p4_b2s", [8, D], BF16))
    Gt = st.enter_context(nc.sbuf_tensor("sb_p4_G", [P, NT, 8], F32))
    wgt = st.enter_context(nc.sbuf_tensor("sb_p4_wgt", [P, NT, 8], F32))
    wgtb = st.enter_context(nc.sbuf_tensor("sb_p4_wgtb", [P, NT, 8], BF16))
    wT = st.enter_context(nc.sbuf_tensor("sb_p4_wT", [8, NT, P], BF16))
    b1s = st.enter_context(nc.sbuf_tensor("sb_p4_b1s", [32, P], F32))
    b1T = st.enter_context(nc.sbuf_tensor("sb_p4_b1T", [P, 32], F32))
    stats = st.enter_context(nc.sbuf_tensor("sb_p4_stats", [P, 5, NT], F32))
    sm = st.enter_context(nc.sbuf_tensor("sb_p4_sm", [P, 2, 8], F32))
    sm2 = st.enter_context(nc.sbuf_tensor("sb_p4_sm2", [P, 2, 8], F32))
    Ebuf = st.enter_context(nc.sbuf_tensor("sb_p4_E", [P, 2, 8], F32))
    cols = st.enter_context(nc.sbuf_tensor("sb_p4_cols", [P, 8], F32))
    xstage = st.enter_context(nc.sbuf_tensor("sb_p4_xst", [P, 2, D], F32))
    ostage = st.enter_context(nc.sbuf_tensor("sb_p4_ost", [P, 2, D], F32))

    import os
    STAGES = int(os.environ.get("P4_STAGES", "9"))

    emit_ln_to_fm(nc, env, sems, stats, "p4")

    if spill_dst is not None:
        with nc.Block() as blk:
            @blk.sync
            def _(s):
                sems.inc(s.dma_start(spill_dst.rearrange("t p d -> p t d"),
                                     xbuf[:, :, :]), "p4x", 16)
                s.wait_ge(sems.h["p4x"], sems.val("p4x"))

    if STAGES < 2:
        return sems
    with ExitStack() as pst, nc.Block() as blk:
        psg = [pst.enter_context(nc.psum_tensor(f"ps_p4_g{i}", [P, 512], F32))
               for i in range(2)]
        pswT = [pst.enter_context(nc.psum_tensor(f"ps_p4_wT{i}", [8, 1024], BF16))
                for i in range(2)]
        st0, sv0, ss0, w0 = sems.val(ST), sems.val(SV), sems.val(SS), sems.val("p4w")

        @blk.gpsimd
        def _(g):
            sems.inc(g.dma_start(Wg_bf[:, :, :],
                                 w["Wg"].rearrange("(c p) e -> p c e", p=P)), "p4w", 16)
            sems.inc(g.dma_start(bgr[:, :], w["bgr"]), "p4w", 16)
            sems.inc(g.dma_start(b2s[:, :], w["b2s"]), "p4w", 16)
            sems.inc(g.dma_start(
                b1s[:, :], w["b1"][0, :].rearrange("(a p) -> a p", p=P)), "p4w", 16)
            for grp in range(4):
                srcw = w["W1"][0, :, grp * 512:(grp + 1) * 512]
                dst = bufA[:, (grp // 2)::2, (grp % 2) * 512:(grp % 2 + 1) * 512]
                sems.inc(g.dma_start(dst, srcw.rearrange("(c p) i -> p c i", p=P)),
                         "p4w", 16)
            for grp in range(4):
                srcw = w["W2"][0, grp * 512:(grp + 1) * 512, :]
                sems.inc(g.dma_start(W2h[:, 4 * grp:4 * grp + 4, :],
                                     srcw.rearrange("(c p) o -> p c o", p=P)),
                         "p4w", 16)

        @blk.tensor
        def _(te):
            sems.wait(te, "p4w", w0 + 48)
            for t in range(NT):
                if t >= 2:
                    sems.wait(te, SV, sv0 + 2 * (t - 2) + 1)
                for c in range(8):
                    te.matmul(psg[t % 2][:, 0:8], bufB[:, c, t * P:(t + 1) * P],
                              Wg_bf[:, c, :], start=(c == 0), stop=False)
                instr = te.matmul(psg[t % 2][:, 0:8], ones_row[:, :], bgr[:, :],
                                  start=False, stop=True)
                sems.inc(instr, ST)
            for t in range(NT):
                sems.wait(te, SV, sv0 + 2 * t + 2)
                instr = te.transpose(pswT[t // 8][:, (t % 8) * P:(t % 8 + 1) * P],
                                     wgtb[:, t, :], id_bf[:, :])
                sems.inc(instr, ST)

        @blk.vector
        def _(v):
            v = VP(v)
            negm1, negm2p = cols[:, 0:1], cols[:, 1:2]
            den, rden = cols[:, 2:3], cols[:, 3:4]
            for t in range(NT):
                G = Gt[:, t, :]
                sems.wait(v, ST, st0 + t + 1)
                v.tensor_copy(G, psg[t % 2][:, 0:8])
                v.tensor_reduce(negm1, G, AX.X, Op.max, negate=True)
                v.tensor_scalar(sm[:, 0, :], G, negm1, None, Op.add)
                v.tensor_scalar(sm2[:, 0, :], sm[:, 0, :], 0.0, None, Op.is_ge)
                v.scalar_tensor_tensor(sm2[:, 0, :], sm2[:, 0, :], -1e30,
                                       sm[:, 0, :], Op.mult, Op.add)
                instr = v.tensor_reduce(negm2p, sm2[:, 0, :], AX.X, Op.max, negate=True)
                sems.inc(instr, SV)
                sems.wait(v, SS, ss0 + t + 1)
                v.tensor_scalar(sm2[:, 0, :], sm[:, 0, :], negm2p, 0.0, Op.add, Op.is_ge)
                v.tensor_scalar(den, den, 1.0, None, Op.add)
                v.reciprocal(rden, den)
                v.tensor_tensor(sm[:, 0, :], Ebuf[:, 0, :], sm2[:, 0, :], Op.mult)
                v.tensor_scalar(wgt[:, t, :], sm[:, 0, :], rden, None, Op.mult)
                instr = v.tensor_copy(wgtb[:, t, :], wgt[:, t, :])
                sems.inc(instr, SV)
            for bank in range(2):
                sems.wait(v, ST, st0 + NT + bank * 8 + 8)
                v.tensor_copy(wT[:, bank * 8:(bank + 1) * 8, :],
                              pswT[bank][:, :].rearrange("p (s n) -> p s n", n=P))

        @blk.scalar
        def _(sc):
            negm1, negm2p, den = cols[:, 0:1], cols[:, 1:2], cols[:, 2:3]
            for t in range(NT):
                sems.wait(sc, SV, sv0 + 2 * t + 1)
                sc.activation(Ebuf[:, 0, :], Gt[:, t, :], AF.Exp, bias=negm1)
                instr = sc.activation(den, negm2p, AF.Exp, scale=-1.0)
                sems.inc(instr, SS)

    if STAGES < 3:
        return sems
    with ExitStack() as pst, nc.Block() as blk:
        psb = [pst.enter_context(nc.psum_tensor(f"ps_p4_binit{i}", [P, D], F32))
               for i in range(2)]
        st0, sv0 = sems.val(ST), sems.val(SV)
        spill_done = sems.val("p4x")

        @blk.tensor
        def _(te):
            for t in range(NT):
                if t >= 2:
                    sems.wait(te, SV, sv0 + t - 1)
                for half in range(2):
                    instr = te.matmul(psb[t % 2][:, half * 512:(half + 1) * 512],
                                      wT[:, t, :], b2s[:, half * 512:(half + 1) * 512],
                                      start=True, stop=True)
                sems.inc(instr, ST)

        @blk.vector
        def _(v):
            v = VP(v)
            if spill_done:
                sems.wait(v, "p4x", spill_done)
            for t in range(NT):
                sems.wait(v, ST, st0 + t + 1)
                instr = v.tensor_copy(y_acc[:, t, :], psb[t % 2][:, :])
                sems.inc(instr, SV)

    if STAGES < 4:
        return sems
    with ExitStack() as pst, nc.Block() as blk:
        psh = [pst.enter_context(nc.psum_tensor(f"ps_p4_h{i}", [P, 512], F32))
               for i in range(2)]
        psy = [pst.enter_context(nc.psum_tensor(f"ps_p4_y{i}", [P, D], F32))
               for i in range(2)]
        psb1 = pst.enter_context(nc.psum_tensor("ps_p4_b1", [P, 512], F32))
        st0, sv0, ss0, w0 = sems.val(ST), sems.val(SV), sems.val(SS), sems.val("p4w")
        NIDX = int(os.environ.get("P4_NIDX", "16"))
        DSUB = int(os.environ.get("P4_DSUB", "3"))
        NTAU = int(os.environ.get("P4_NTAU", "4"))

        TSTRIDE = (16 if DSUB >= 2 else 0) + (4 if DSUB >= 3 else 0)
        H64 = 16 * NTAU if DSUB >= 2 else 0
        Y16 = 4 * NTAU if DSUB >= 3 else 0

        def tbase(idx):
            return st0 + idx * (H64 + Y16)

        def sbase(idx):
            return ss0 + H64 * idx

        NDMA = 8 if DSUB >= 1 else 0

        def wvals(idx):
            base = w0 - 16 * (NDMA + 1) + 16 * sum(
                (NDMA + (1 if i % 2 == 0 else 0)) for i in range(idx))
            n = (NDMA + 1) if idx % 2 == 0 else max(NDMA, 1)
            return [base + 16 * (j + 1) for j in range(n)]

        @blk.gpsimd
        def _(g):
            for idx in range(NIDX):
                if idx == 0:
                    continue
                e, half = idx // 2, idx % 2
                if half == 0:
                    if e > 0:
                        sems.wait(g, "p4b", 2 * e - 1)
                    sems.inc(g.dma_start(
                        b1s[:, :], w["b1"][e, :].rearrange("(a p) -> a p", p=P)),
                        "p4w", 16)
                if DSUB < 1:
                    continue
                if idx >= 1:
                    if DSUB >= 2:
                        sems.wait(g, ST, tbase(idx - 1) + (NTAU - 1) * TSTRIDE + 16)
                    else:
                        sems.wait(g, ST, tbase(idx - 1))
                for grp in range(4):
                    src = w["W1"][e, :, half * 2048 + grp * 512:
                                  half * 2048 + (grp + 1) * 512]
                    dst = bufA[:, (grp // 2)::2, (grp % 2) * 512:(grp % 2 + 1) * 512]
                    sems.inc(g.dma_start(dst, src.rearrange("(c p) i -> p c i", p=P)),
                             "p4w", 16)
                if idx >= 1:
                    sems.wait(g, ST, tbase(idx))
                for grp in range(4):
                    src = w["W2"][e, half * 2048 + grp * 512:
                                  half * 2048 + (grp + 1) * 512, :]
                    sems.inc(g.dma_start(W2h[:, 4 * grp:4 * grp + 4, :],
                                         src.rearrange("(c p) o -> p c o", p=P)),
                             "p4w", 16)

        @blk.tensor
        def _(te):
            for idx in range(NIDX):
                e, half = idx // 2, idx % 2
                wv = wvals(idx)
                b1off = 1 if half == 0 else 0
                if half == 0:
                    sems.wait(te, "p4w", wv[0])
                    if e > 0:
                        sems.wait(te, "p4b", 2 * e)
                    instr = te.transpose(psb1[:, 0:32], b1s[:, :], id_f32[0:32, 0:32])
                    sems.inc(instr, "p4b", 1)
                for tau in range(NTAU):
                    if DSUB < 2:
                        break
                    if int(os.environ.get("P4_SER", "0")) and DSUB >= 3:
                        sems.wait(te, SV, sv0 + idx * 4 * NTAU + tau * 4)
                    for ic in range(16):
                        glob = tau * 16 + ic
                        sems.wait(te, "p4w", wv[b1off + ic // 4])
                        if sbase(idx) + glob - 1 >= 1:
                            sems.wait(te, SS, sbase(idx) + glob - 1)
                        for c in range(8):
                            instr = te.matmul(
                                psh[ic % 2][:, :],
                                bufA[:, 2 * c + ic // 8, (ic % 8) * P:(ic % 8 + 1) * P],
                                bufB[:, c, tau * 512:(tau + 1) * 512],
                                start=(c == 0), stop=(c == 7))
                        sems.inc(instr, ST)
                    if DSUB < 3:
                        continue
                    for tt in range(4):
                        r = idx * 16 + tau * 4 + tt
                        if r >= 2:
                            sems.wait(te, SV, sv0 + r - 1)
                        for half_o in range(2):
                            for ic in range(16):
                                if tt == 0 and half_o == 0:
                                    sems.wait(te, SS, sbase(idx) + tau * 16 + ic + 1)
                                    sems.wait(te, "p4w", wv[b1off + 4 + ic // 4])
                                instr = te.matmul(
                                    psy[r % 2][:, half_o * 512:(half_o + 1) * 512],
                                    hbuf[:, ic, tt * P:(tt + 1) * P],
                                    W2h[:, ic, half_o * 512:(half_o + 1) * 512],
                                    start=(ic == 0), stop=(ic == 15))
                        sems.inc(instr, ST)

        @blk.scalar
        def _(sc):
            for idx in range(NIDX):
                e, half = idx // 2, idx % 2
                b1off = 1 if half == 0 else 0
                for tau in range(NTAU):
                    if DSUB < 2:
                        break
                    for ic in range(16):
                        glob = tau * 16 + ic
                        sems.wait(sc, ST, tbase(idx) + tau * TSTRIDE + ic + 1)
                        if half == 0 and glob == 0:
                            sems.wait(sc, "p4b", 2 * e + 2)
                        instr = sc.activation(hbuf[:, ic, :], psh[ic % 2][:, :],
                                              AF.Gelu,
                                              bias=b1T[:, half * 16 + ic:half * 16 + ic + 1])
                        sems.inc(instr, SS)

        @blk.vector
        def _(v):
            v = VP(v)
            for idx in range(NIDX):
                e, half = idx // 2, idx % 2
                b1off = 1 if half == 0 else 0
                if half == 0:
                    sems.wait(v, "p4b", 2 * e + 1)
                    instr = v.tensor_copy(b1T[:, :], psb1[:, 0:32])
                    sems.inc(instr, "p4b", 1)
                for tau in range(NTAU):
                    if DSUB < 3:
                        break
                    for tt in range(4):
                        r = idx * 16 + tau * 4 + tt
                        t = tau * 4 + tt
                        sems.wait(v, ST, tbase(idx) + tau * TSTRIDE + 16 + tt + 1)
                        if int(os.environ.get("P4_STT", "1")):
                            instr = v.scalar_tensor_tensor(
                                y_acc[:, t, :], psy[r % 2][:, :], wgt[:, t, e:e + 1],
                                y_acc[:, t, :], Op.mult, Op.add)
                        else:
                            instr = v.tensor_copy(hbuf[:, 0, :].bitcast(F32)[:, 0:256], psy[r % 2][:, 0:256])
                        sems.inc(instr, SV)

    if STAGES < 5:
        return sems
    with nc.Block() as blk:
        v0, s0 = sems.val(SV), sems.val(SS)
        x0, y0 = sems.val("p4x"), sems.val("p4y")
        sums, sumsq = stats[:, 0, :], stats[:, 1, :]
        meanE, varE, rstdE = stats[:, 2, :], stats[:, 3, :], stats[:, 4, :]
        NS = NT + 6
        _RD, _VA, _AP2, _SQ, _RS2 = {}, {}, {}, {}, {}
        vc, sc_ = v0, s0
        for i in range(NS):
            if 0 <= i < NT:
                vc += 1; _RD[i] = vc
            if 0 <= i - 2 < NT:
                vc += 1; _VA[i - 2] = vc
            if 0 <= i - 4 < NT:
                vc += 1; _AP2[i - 4] = vc
        for i in range(NS):
            if 0 <= i - 1 < NT:
                sc_ += 1; _SQ[i - 1] = sc_
            if 0 <= i - 3 < NT:
                sc_ += 1; _RS2[i - 3] = sc_

        @blk.sync
        def _(s):
            for t in range(NT):
                if t >= 2:
                    sems.wait(s, SV, _AP2[t - 2])
                sems.inc(s.dma_start(xstage[:, t % 2, :],
                                     x_reload_src[t, :, :]), "p4x", 16)

        @blk.vector
        def _(v):
            v = VP(v)
            for i in range(NS):
                if 0 <= i < NT:
                    t = i
                    instr = v.tensor_reduce(sums[:, t:t + 1], y_acc[:, t, :], AX.X, Op.add)
                    sems.inc(instr, SV)
                if 0 <= i - 2 < NT:
                    t = i - 2
                    sems.wait(v, SS, _SQ[t])
                    v.tensor_scalar(meanE[:, t:t + 1], sums[:, t:t + 1], 1.0 / D, None, Op.mult)
                    v.tensor_scalar(varE[:, t:t + 1], sumsq[:, t:t + 1], 1.0 / D, None, Op.mult)
                    v.tensor_tensor(sums[:, t:t + 1], meanE[:, t:t + 1],
                                    meanE[:, t:t + 1], Op.mult)
                    instr = v.tensor_tensor(varE[:, t:t + 1], varE[:, t:t + 1],
                                            sums[:, t:t + 1], Op.subtract)
                    sems.inc(instr, SV)
                if 0 <= i - 4 < NT:
                    t = i - 4
                    sems.wait(v, SS, _RS2[t])
                    sems.wait(v, "p4x", x0 + 16 * (t + 1))
                    if t >= 2:
                        sems.wait(v, "p4y", y0 + 16 * (t - 1))
                    v.tensor_scalar(ostage[:, t % 2, :], y_acc[:, t, :],
                                    meanE[:, t:t + 1], rstdE[:, t:t + 1],
                                    Op.subtract, Op.mult)
                    instr = v.tensor_tensor(ostage[:, t % 2, :], ostage[:, t % 2, :],
                                            xstage[:, t % 2, :], Op.add)
                    sems.inc(instr, SV)

        @blk.scalar
        def _(sc):
            for i in range(NS):
                if 0 <= i - 1 < NT:
                    t = i - 1
                    instr = sc.activation(bufA[:, 0, :], y_acc[:, t, :], AF.Square,
                                          accum_out=sumsq[:, t:t + 1])
                    sc.drain()
                    sems.inc(instr, SS)
                if 0 <= i - 3 < NT:
                    t = i - 3
                    sems.wait(sc, SV, _VA[t])
                    sc.activation(rstdE[:, t:t + 1], varE[:, t:t + 1], AF.Ln, bias=epsc)
                    instr = sc.activation(rstdE[:, t:t + 1], rstdE[:, t:t + 1],
                                          AF.Exp, scale=-0.5)
                    sems.inc(instr, SS)

        @blk.gpsimd
        def _(g):
            for t in range(NT):
                sems.wait(g, SV, _AP2[t])
                sems.inc(g.dma_start(x_out_dst[t, :, :], ostage[:, t % 2, :]),
                         "p4y", 16)
            g.wait_ge(sems.h["p4y"], sems.val("p4y"))

    return sems

def emit_postln_pipe(nc, blk, sems, tag, mm_vals, psum, tile_ids, xbuf, scr3,
                     sqdump, stats, epsc, nt):
    SV, SS = f"v_{tag}", f"s_{tag}"
    v0, s0 = sems.val(SV), sems.val(SS)
    sums, sumsq = stats[:, 0, :], stats[:, 1, :]
    mean, var, rstd = stats[:, 2, :], stats[:, 3, :], stats[:, 4, :]
    _EV, _VA, _AP, _SQ, _RS = {}, {}, {}, {}, {}
    vc, sc_ = v0, s0
    for i in range(nt + 3):
        if 0 <= i - 1 < nt:
            vc += 1; _EV[i - 1] = vc
        if 0 <= i - 2 < nt:
            vc += 1; _VA[i - 2] = vc
        if 0 <= i - 3 < nt:
            vc += 1; _AP[i - 3] = vc
    for i in range(nt + 3):
        if 0 <= i - 2 < nt:
            sc_ += 1; _SQ[i - 2] = sc_
        if 0 <= i - 3 < nt:
            sc_ += 1; _RS[i - 3] = sc_

    @blk.vector
    def _(v):
        v = VP(v)
        for i in range(nt + 3):
            if 0 <= i - 1 < nt:
                t = i - 1
                sems.wait(v, f"t_{tag}", mm_vals[t])
                instr = v.tensor_scalar(scr3[:, t % 3, :], psum[t % 2][:, :], 1.0,
                                        None, Op.mult, Op.add,
                                        accum_out=sums[:, t:t + 1])
                sems.inc(instr, SV)
            if 0 <= i - 2 < nt:
                t = i - 2
                sems.wait(v, SS, _SQ[t])
                v.tensor_scalar(mean[:, t:t + 1], sums[:, t:t + 1], 1.0 / D, None, Op.mult)
                v.tensor_scalar(var[:, t:t + 1], sumsq[:, t:t + 1], 1.0 / D, None, Op.mult)
                v.tensor_tensor(sums[:, t:t + 1], mean[:, t:t + 1], mean[:, t:t + 1], Op.mult)
                instr = v.tensor_tensor(var[:, t:t + 1], var[:, t:t + 1],
                                        sums[:, t:t + 1], Op.subtract)
                sems.inc(instr, SV)
            if 0 <= i - 3 < nt:
                t = i - 3
                sems.wait(v, SS, _RS[t])
                v.tensor_scalar(scr3[:, t % 3, :], scr3[:, t % 3, :],
                                mean[:, t:t + 1], rstd[:, t:t + 1],
                                Op.subtract, Op.mult)
                instr = v.tensor_tensor(xbuf[:, tile_ids[t], :], xbuf[:, tile_ids[t], :],
                                        scr3[:, t % 3, :], Op.add)
                sems.inc(instr, SV)

    @blk.scalar
    def _(sc):
        for i in range(nt + 3):
            if 0 <= i - 2 < nt:
                t = i - 2
                sems.wait(sc, SV, _EV[t])
                instr = sc.activation(sqdump, scr3[:, t % 3, :], AF.Square,
                                      accum_out=sumsq[:, t:t + 1])
                sc.drain()
                sems.inc(instr, SS)
            if 0 <= i - 3 < nt:
                t = i - 3
                sems.wait(sc, SV, _VA[t])
                sc.activation(rstd[:, t:t + 1], var[:, t:t + 1], AF.Ln, bias=epsc)
                instr = sc.activation(rstd[:, t:t + 1], rstd[:, t:t + 1], AF.Exp, scale=-0.5)
                sems.inc(instr, SS)
    return _EV


def _prep_shared(inputs):
    t = {}
    t["p2_Win"] = np.ascontiguousarray(inputs["L0_ffn_Win"])
    t["p2_binr"] = np.ascontiguousarray(inputs["L0_ffn_bin_"][None, :])
    t["p2_W1b"] = np.ascontiguousarray(
        np.tile(inputs["L0_ffn_W1"].reshape(1, D), (P, 1)))
    t["p2_b1b"] = np.ascontiguousarray(
        np.tile(inputs["L0_ffn_b1"].reshape(1, D), (P, 1)))
    t["p2_W2b"] = np.ascontiguousarray(
        np.tile(inputs["L0_ffn_W2"].reshape(1, D), (P, 1)))
    t["p2_b2b"] = np.ascontiguousarray(
        np.tile(inputs["L0_ffn_b2"].reshape(1, 64), (P, 1)))
    t["p2_Wout"] = np.ascontiguousarray(inputs["L0_ffn_Wout"])
    t["p2_boutr"] = np.ascontiguousarray(inputs["L0_ffn_bout"][None, :])
    t["p1_WqkvT"] = np.ascontiguousarray(inputs["L0_attn_Wqkv"].T)
    t["p1_WprojT"] = np.ascontiguousarray(inputs["L0_attn_Wproj"].T)
    t["p1_bprojr"] = np.ascontiguousarray(inputs["L0_attn_bproj"][None, :])
    t["p3_w1T"] = np.ascontiguousarray(inputs["L1_attn_w1"].T)
    t["p3_wwT"] = np.ascontiguousarray(inputs["L1_attn_ww"].T)
    t["p3_w2T"] = np.ascontiguousarray(inputs["L1_attn_w2"].T)
    t["p3_b1r"] = np.ascontiguousarray(inputs["L1_attn_b1"][None, :])
    t["p3_bwr"] = np.ascontiguousarray(inputs["L1_attn_bw"][None, :])
    t["p3_b2r"] = np.ascontiguousarray(inputs["L1_attn_b2"][None, :])
    t["p3_gb"] = np.ascontiguousarray(np.tile(inputs["L1_attn_gamma"][None, :], (P, 1)))
    t["p3_bb"] = np.ascontiguousarray(np.tile(inputs["L1_attn_beta"][None, :], (P, 1)))
    t["p4_Wg"] = np.ascontiguousarray(inputs["L1_ffn_Wg"])
    t["p4_bgr"] = np.ascontiguousarray(inputs["L1_ffn_bg"][None, :])
    t["p4_b2s"] = np.ascontiguousarray(inputs["L1_ffn_b2"])
    t["p4_W1"] = np.ascontiguousarray(inputs["L1_ffn_W1"])
    t["p4_W2"] = np.ascontiguousarray(inputs["L1_ffn_W2"])
    t["p4_b1"] = np.ascontiguousarray(inputs["L1_ffn_b1"])
    t["ones_row"] = np.ones((1, P), dtype=BF)
    t["id_bf"] = np.eye(P, dtype=BF)
    t["id_f32"] = np.eye(P, dtype=np.float32)
    return t


PHASE_KEYS = {
    1: ["p1_WqkvT", "p1_WprojT", "p1_bprojr"],
    2: ["p2_Win", "p2_binr", "p2_W1b", "p2_b1b", "p2_W2b", "p2_b2b",
        "p2_Wout", "p2_boutr"],
    3: ["p3_w1T", "p3_wwT", "p3_w2T", "p3_b1r", "p3_bwr", "p3_b2r",
        "p3_gb", "p3_bb"],
    4: ["p4_Wg", "p4_bgr", "p4_b2s", "p4_W1", "p4_W2", "p4_b1"],
}
COMMON_KEYS = ["ones_row", "id_bf", "id_f32"]


def _phase_weight_aps(nc, t):
    aps = {}
    for k, v in t.items():
        dt = BF16 if v.dtype == BF else F32
        aps[k] = nc.dram_tensor(k, list(v.shape), dt, kind="ExternalInput").ap()
    return aps


def _common_env(nc, st, aps):
    env = {}
    env["xbuf"] = st.enter_context(nc.sbuf_tensor("xbuf", [P, NT, D], F32))
    env["bufA"] = st.enter_context(nc.sbuf_tensor("bufA", [P, NT, D], BF16))
    env["bufB"] = st.enter_context(nc.sbuf_tensor("bufB", [P, 8, 2048], BF16))
    env["ones_row"] = st.enter_context(nc.sbuf_tensor("c_ones", [1, P], BF16))
    env["id_bf"] = st.enter_context(nc.sbuf_tensor("c_idbf", [P, P], BF16))
    env["id_f32"] = st.enter_context(nc.sbuf_tensor("c_idf32", [P, P], F32))
    env["epsc"] = st.enter_context(nc.sbuf_tensor("c_eps", [P, 1], F32))
    return env


def _wire_weights(aps, phase):
    if phase == 1:
        return {"WqkvT": aps["p1_WqkvT"], "WprojT": aps["p1_WprojT"],
                "bprojr": aps["p1_bprojr"], "mask1": aps["mask1"],
                "ropeq": aps["ropeq"], "ropek": aps["ropek"]}
    if phase == 2:
        return {"Win": aps["p2_Win"], "binr": aps["p2_binr"],
                "W1b": aps["p2_W1b"], "b1b": aps["p2_b1b"],
                "W2b": aps["p2_W2b"], "b2b": aps["p2_b2b"],
                "Wout": aps["p2_Wout"], "boutr": aps["p2_boutr"]}
    if phase == 3:
        return {"w1T": aps["p3_w1T"], "wwT": aps["p3_wwT"], "w2T": aps["p3_w2T"],
                "b1r": aps["p3_b1r"], "bwr": aps["p3_bwr"], "b2r": aps["p3_b2r"],
                "gb": aps["p3_gb"], "bb": aps["p3_bb"]}
    if phase == 4:
        return {"Wg": aps["p4_Wg"], "bgr": aps["p4_bgr"], "b2s": aps["p4_b2s"],
                "W1": aps["p4_W1"], "W2": aps["p4_W2"], "b1": aps["p4_b1"]}
    raise NotImplementedError(phase)


def build_phase_program(phase, shared):
    nc = bass.Bass("TRN2", target_bir_lowering=False)
    xin = nc.dram_tensor("xin", [NT, P, D], F32, kind="ExternalInput").ap()
    xout = nc.dram_tensor("xout", [NT, P, D], F32, kind="ExternalOutput").ap()
    keys = COMMON_KEYS + PHASE_KEYS[phase]
    aps = _phase_weight_aps(nc, {k: shared[k] for k in keys})
    if phase == 1:
        aps["mask1"] = nc.dram_tensor("mask1", [4, P, 640], F32, kind="ExternalInput").ap()
        aps["ropeq"] = nc.dram_tensor("ropeq", [2, 4, P, 64], F32, kind="ExternalInput").ap()
        aps["ropek"] = nc.dram_tensor("ropek", [2, 8, P, 64], F32, kind="ExternalInput").ap()
        xh = nc.dram_tensor("xh", [NT, P, D], F32, kind="ExternalInput").ap()

    st = ExitStack()
    with st:
        env = _common_env(nc, st, aps)
        env[f"w{phase}"] = _wire_weights(aps, phase)
        gsems = Sems(nc, st, ["io"])
        with nc.Block() as blk:
            @blk.sync
            def _(s):
                gsems.inc(s.dma_start(env["xbuf"][:, :, :], xin.rearrange("t p d -> p t d")), "io", 16)
                gsems.inc(s.dma_start(env["ones_row"][:, :], aps["ones_row"]), "io", 16)
                gsems.inc(s.dma_start(env["id_bf"][:, :], aps["id_bf"]), "io", 16)
                gsems.inc(s.dma_start(env["id_f32"][:, :], aps["id_f32"]), "io", 16)
                s.wait_ge(gsems.h["io"], gsems.val("io"))

            @blk.vector
            def _(v):
                v = VP(v)
                v.memset(env["epsc"][:, :], EPS)

        if phase == 1:
            emit_phase1(nc, st, env, xh)
        elif phase == 2:
            emit_phase2(nc, st, env)
        elif phase == 3:
            emit_phase3(nc, st, env)
        elif phase == 4:
            emit_phase4(nc, st, env, xin, xout, spill_dst=None)
        else:
            raise NotImplementedError(phase)

        if phase != 4:
            with nc.Block() as blk:
                @blk.sync
                def _(s):
                    instr = s.dma_start(xout.rearrange("t p d -> p t d"),
                                        env["xbuf"][:, :, :])
                    gsems.inc(instr, "io", 16)
                    s.wait_ge(gsems.h["io"], gsems.val("io"))
    return nc


def _shard_x(x_full):
    shards = []
    for k in range(NCORES):
        xc = x_full[:, k * SLOC:(k + 1) * SLOC, :]
        shards.append(np.ascontiguousarray(
            xc.reshape(B, SLOC // P, P, D).reshape(NT, P, D)))
    return shards


def _unshard_x(shards):
    out = np.empty((B, SEQ, D), dtype=np.float32)
    for k, sh in enumerate(shards):
        out[:, k * SLOC:(k + 1) * SLOC, :] = sh.reshape(B, SLOC // P, P, D).reshape(
            B, SLOC, D)
    return out


def _percore_p1(k, x_full):
    out = {}
    if k == 0:
        xh = np.zeros((B, SLOC, D), dtype=np.float32)
    else:
        xh = np.ascontiguousarray(x_full[:, (k - 1) * SLOC:k * SLOC, :])
    out["xh"] = np.ascontiguousarray(
        xh.reshape(B, SLOC // P, P, D).reshape(NT, P, D))
    ql = np.arange(P)[:, None]
    cc = np.arange(640)[None, :]
    mask = np.empty((4, P, 640), dtype=np.float32)
    for g in range(4):
        allow = (cc >= ql) & (cc <= ql + 512)
        jglob = 512 * (k - 1) + 128 * g + cc
        allow = allow & (jglob >= 0)
        mask[g] = np.where(allow, 1.0, 0.0).astype(np.float32)
    out["mask1"] = mask
    half = HD // 2
    inv_freq = (1.0 / 10000.0 ** (np.arange(half, dtype=np.float32) / half)).astype(np.float32)
    sgn = np.empty(HD, dtype=np.float32)
    sgn[0::2], sgn[1::2] = -1.0, 1.0

    def tables(pos):
        ang = pos[:, :, None].astype(np.float32) * inv_freq[None, None, :]
        cosr = np.repeat(np.cos(ang), 2, axis=-1).astype(np.float32)
        sinr = np.repeat(np.sin(ang), 2, axis=-1).astype(np.float32) * sgn
        return cosr, sinr

    posq = 512 * k + 128 * np.arange(4)[:, None] + np.arange(P)[None, :]
    cosr, sinr = tables(posq)
    out["ropeq"] = np.ascontiguousarray(
        np.stack([cosr, sinr]) * np.float32(0.125))
    posk = 512 * (k - 1) + 128 * np.arange(8)[:, None] + np.arange(P)[None, :]
    cosr, sinr = tables(posk)
    out["ropek"] = np.ascontiguousarray(np.stack([cosr, sinr]))
    return out


def run_phase(phase, x_full, inputs):
    shared = _prep_shared(inputs)
    nc = build_phase_program(phase, shared)
    shards = _shard_x(x_full)
    keys = COMMON_KEYS + PHASE_KEYS[phase]
    in_maps = []
    for k in range(NCORES):
        m = {kk: shared[kk] for kk in keys}
        m["xin"] = shards[k]
        if phase == 1:
            m.update(_percore_p1(k, x_full))
        in_maps.append(m)
    res = run_bass_kernel_spmd(nc, in_maps, core_ids=list(range(NCORES)))
    return _unshard_x([r["xout"] for r in res.results])


def build_fused_program(shared):
    nc = bass.Bass("TRN2", target_bir_lowering=False)
    xin = nc.dram_tensor("xin", [NT, P, D], F32, kind="ExternalInput").ap()
    xout = nc.dram_tensor("xout", [NT, P, D], F32, kind="ExternalOutput").ap()
    spill = nc.dram_tensor("spill", [NT, P, D], F32, kind="Internal").ap()
    keys = COMMON_KEYS + sum((PHASE_KEYS[p] for p in (1, 2, 3, 4)), [])
    aps = _phase_weight_aps(nc, {k: shared[k] for k in keys})
    aps["mask1"] = nc.dram_tensor("mask1", [4, P, 640], F32, kind="ExternalInput").ap()
    aps["ropeq"] = nc.dram_tensor("ropeq", [2, 4, P, 64], F32, kind="ExternalInput").ap()
    aps["ropek"] = nc.dram_tensor("ropek", [2, 8, P, 64], F32, kind="ExternalInput").ap()
    xh = nc.dram_tensor("xh", [NT, P, D], F32, kind="ExternalInput").ap()

    st = ExitStack()
    with st:
        env = _common_env(nc, st, aps)
        for p in (1, 2, 3, 4):
            env[f"w{p}"] = _wire_weights(aps, p)
        gsems = Sems(nc, st, ["io"])
        with nc.Block() as blk:
            @blk.sync
            def _(s):
                gsems.inc(s.dma_start(env["xbuf"][:, :, :], xin.rearrange("t p d -> p t d")), "io", 16)
                gsems.inc(s.dma_start(env["ones_row"][:, :], aps["ones_row"]), "io", 16)
                gsems.inc(s.dma_start(env["id_bf"][:, :], aps["id_bf"]), "io", 16)
                gsems.inc(s.dma_start(env["id_f32"][:, :], aps["id_f32"]), "io", 16)
                s.wait_ge(gsems.h["io"], gsems.val("io"))

            @blk.vector
            def _(v):
                v = VP(v)
                v.memset(env["epsc"][:, :], EPS)

        with ExitStack() as pstk:
            emit_phase1(nc, pstk, env, xh)
        with ExitStack() as pstk:
            emit_phase2(nc, pstk, env)
        with ExitStack() as pstk:
            emit_phase3(nc, pstk, env)
        with ExitStack() as pstk:
            emit_phase4(nc, pstk, env, spill, xout, spill_dst=spill)
    return nc


def run_fused(x_full, inputs):
    shared = _prep_shared(inputs)
    nc = build_fused_program(shared)
    shards = _shard_x(x_full)
    keys = COMMON_KEYS + sum((PHASE_KEYS[p] for p in (1, 2, 3, 4)), [])
    in_maps = []
    for k in range(NCORES):
        m = {kk: shared[kk] for kk in keys}
        m["xin"] = shards[k]
        m.update(_percore_p1(k, x_full))
        in_maps.append(m)
    res = run_bass_kernel_spmd(nc, in_maps, core_ids=list(range(NCORES)))
    return _unshard_x([r["xout"] for r in res.results])


def kernel(**inputs):
    x = np.asarray(inputs["L0_attn_x"], dtype=np.float32)
    return run_fused(x, inputs)


# version 17
# speedup vs baseline: 1.0165x; 1.0037x over previous
import numpy as np
import ml_dtypes
from contextlib import ExitStack

import concourse.bass as bass
import concourse.mybir as mybir
from concourse.bass_utils import run_bass_kernel_spmd
from concourse.alu_op_type import AluOpType as Op

F32 = mybir.dt.float32
BF16 = mybir.dt.bfloat16
AF = mybir.ActivationFunctionType
AX = mybir.AxisListType
BF = ml_dtypes.bfloat16

P = 128
D = 1024
B = 4
SEQ = 4096
NCORES = 8
SLOC = SEQ // NCORES
NT = B * (SLOC // P)
NH = 16
HD = 64
WINLEN = 512
KTOK = SLOC + WINLEN
EPS = 1e-5



class Sems:

    def __init__(self, nc, st, names):
        self.h = {}
        self.n = {}
        for nm in names:
            self.h[nm] = st.enter_context(nc.semaphore(nm))
            self.n[nm] = 0

    def inc(self, instr, nm, k=1):
        instr.then_inc(self.h[nm], k)
        self.n[nm] += k
        return self.n[nm]

    def dma(self, instr, nm="dma"):
        return self.inc(instr, nm, 16)

    def val(self, nm):
        return self.n[nm]

    def wait(self, eng, nm, v):
        eng.wait_ge(self.h[nm], v)


class VP:

    _PASS = {"wait_ge", "drain", "engine_nop"}

    def __init__(self, v):
        self._v = v

    def __getattr__(self, name):
        fn = getattr(self._v, name)
        if name in VP._PASS or not callable(fn):
            return fn

        def wrapped(*a, **k):
            r = fn(*a, **k)
            self._v.drain()
            return r
        return wrapped


def emit_ln_stats(nc, blk, sems, src_tiles, nfeat, stats, sqdump, epsc, tag):
    n = len(src_tiles)
    sums, sumsq = stats[:, 0, :n], stats[:, 1, :n]
    mean, var, rstd = stats[:, 2, :n], stats[:, 3, :n], stats[:, 4, :n]
    sv, ss = f"v_{tag}", f"s_{tag}"
    v0, s0 = sems.val(sv), sems.val(ss)

    @blk.vector
    def _(v):
        v = VP(v)
        for i, srct in enumerate(src_tiles):
            instr = v.tensor_reduce(sums[:, i:i + 1], srct, AX.X, Op.add)
        sems.inc(instr, sv)

    @blk.scalar
    def _(sc):
        for i, srct in enumerate(src_tiles):
            instr = sc.activation(sqdump, srct, AF.Square,
                                  accum_out=sumsq[:, i:i + 1])
            sc.drain()
        sems.inc(instr, ss)

    @blk.vector
    def _(v):
        v = VP(v)
        sems.wait(v, ss, s0 + 1)
        v.tensor_scalar(mean, sums, 1.0 / nfeat, None, Op.mult)
        v.tensor_scalar(var, sumsq, 1.0 / nfeat, None, Op.mult)
        v.tensor_tensor(sums, mean, mean, Op.mult)
        instr = v.tensor_tensor(var, var, sums, Op.subtract)
        sems.inc(instr, sv)

    @blk.scalar
    def _(sc):
        sems.wait(sc, sv, v0 + 2)
        sc.activation(rstd, var, AF.Ln, bias=epsc)
        instr = sc.activation(rstd, rstd, AF.Exp, scale=-0.5)
        sems.inc(instr, ss)

    return mean, rstd



def emit_phase1(nc, st, env, xh_ap):
    xbuf, bufA, bufB = env["xbuf"], env["bufA"], env["bufB"]
    ones_row, id_bf = env["ones_row"], env["id_bf"]
    epsc = env["epsc"][:, :]
    w = env["w1"]
    sems = Sems(nc, st, ["p1dma", "p1ds", "p1w", "v_p1", "s_p1", "t_p1"])
    SV, SS, ST, SD = "v_p1", "s_p1", "t_p1", "p1dma"

    mask_sb = st.enter_context(nc.sbuf_tensor("sb_p1_mask", [P, 4, 640], BF16))
    cosq = st.enter_context(nc.sbuf_tensor("sb_p1_cq", [P, 4, 64], F32))
    sinq = st.enter_context(nc.sbuf_tensor("sb_p1_sq", [P, 4, 64], F32))
    cosk = st.enter_context(nc.sbuf_tensor("sb_p1_ck", [P, 8, 64], F32))
    sink = st.enter_context(nc.sbuf_tensor("sb_p1_sk", [P, 8, 64], F32))
    bprojr = st.enter_context(nc.sbuf_tensor("sb_p1_bpr", [1, D], BF16))
    wq2 = [st.enter_context(nc.sbuf_tensor(f"sb_p1_wq{i}", [P, 8, 512], BF16))
           for i in range(2)]
    qT = st.enter_context(nc.sbuf_tensor("sb_p1_qT", [P, 8, 512], BF16))
    yfm = st.enter_context(nc.sbuf_tensor("sb_p1_yfm", [P, 8, 512], BF16))
    hstage = st.enter_context(nc.sbuf_tensor("sb_p1_hst", [P, 4, D], F32))
    qrope = st.enter_context(nc.sbuf_tensor("sb_p1_qr", [P, 2, 512], BF16))
    scrA = st.enter_context(nc.sbuf_tensor("sb_p1_sA", [P, 512], F32))
    scrB = st.enter_context(nc.sbuf_tensor("sb_p1_sB", [P, 512], F32))
    Pn = st.enter_context(nc.sbuf_tensor("sb_p1_Pn", [P, 4, 640], BF16))
    PTs = st.enter_context(nc.sbuf_tensor("sb_p1_PT", [P, 2, 640], BF16))
    stats = st.enter_context(nc.sbuf_tensor("sb_p1_stats", [P, 5, 8], F32))
    cols = st.enter_context(nc.sbuf_tensor("sb_p1_cols", [P, 8], F32))
    fmb = bufB[:, :, 0:1024]
    kT = bufB[:, :, 1024:2048]
    vbv = bufA[:, 8:16, :]

    with nc.Block() as blk:
        @blk.gpsimd
        def _(g):
            sems.inc(g.dma_start(mask_sb[:, :, :],
                                 w["mask1"].rearrange("g p n -> p g n")), SD, 16)
            sems.inc(g.dma_start(bprojr[:, :], w["bprojr"]), SD, 16)
            g.wait_ge(sems.h[SD], sems.val(SD))

        @blk.vector
        def _(v):
            v.memset(cols[:, 6:7], -16.0)

        @blk.sync
        def _(s):
            sems.inc(s.dma_start(cosq[:, :, :], w["ropeq"][0].rearrange("g p n -> p g n")), "p1ds", 16)
            sems.inc(s.dma_start(sinq[:, :, :], w["ropeq"][1].rearrange("g p n -> p g n")), "p1ds", 16)
            sems.inc(s.dma_start(cosk[:, :, :], w["ropek"][0].rearrange("g p n -> p g n")), "p1ds", 16)
            sems.inc(s.dma_start(sink[:, :, :], w["ropek"][1].rearrange("g p n -> p g n")), "p1ds", 16)
            s.wait_ge(sems.h["p1ds"], sems.val("p1ds"))

    for b in range(B):
        with ExitStack() as pst, nc.Block() as blk:
            psT = [pst.enter_context(nc.psum_tensor(f"ps_p1_lnT{i}_b{b}", [P, D], BF16))
                   for i in range(2)]
            x0 = sems.val("p1ds")
            v0, s0, t0 = sems.val(SV), sems.val(SS), sems.val(ST)
            sums, sumsq = stats[:, 0, :], stats[:, 1, :]
            meanl, varl, rstdl = stats[:, 2, :], stats[:, 3, :], stats[:, 4, :]
            NB = 8
            NS = NB + 7
            _RD, _VA, _APL, _CP, _SQ, _RS, _TR = {}, {}, {}, {}, {}, {}, {}
            vc, sc_, tc = v0, s0, t0
            for i in range(NS):
                if 0 <= i < NB:
                    vc += 1; _RD[i] = vc
                if 0 <= i - 2 < NB:
                    vc += 1; _VA[i - 2] = vc
                if 0 <= i - 4 < NB:
                    vc += 1; _APL[i - 4] = vc
                if 0 <= i - 6 < NB:
                    vc += 1; _CP[i - 6] = vc
            for i in range(NS):
                if 0 <= i - 1 < NB:
                    sc_ += 1; _SQ[i - 1] = sc_
                if 0 <= i - 3 < NB:
                    sc_ += 1; _RS[i - 3] = sc_
            for i in range(NS):
                if 0 <= i - 5 < NB:
                    tc += 1; _TR[i - 5] = tc

            def srcof(t):
                return xbuf[:, 4 * b + t, :] if t < 4 else hstage[:, t - 4, :]

            @blk.sync
            def _(s):
                sems.inc(s.dma_start(
                    hstage[:, :, :],
                    xh_ap.rearrange("t p d -> p t d")[:, 4 * b:4 * b + 4, :]), "p1ds", 16)

            @blk.vector
            def _(v):
                v = VP(v)
                for i in range(NS):
                    if 0 <= i < NB:
                        t = i
                        if t == 4:
                            v.wait_ge(sems.h["p1ds"], x0 + 16)
                        instr = v.tensor_reduce(sums[:, t:t + 1], srcof(t), AX.X, Op.add)
                        sems.inc(instr, SV)
                    if 0 <= i - 2 < NB:
                        t = i - 2
                        sems.wait(v, SS, _SQ[t])
                        v.tensor_scalar(meanl[:, t:t + 1], sums[:, t:t + 1], 1.0 / D, None, Op.mult)
                        v.tensor_scalar(varl[:, t:t + 1], sumsq[:, t:t + 1], 1.0 / D, None, Op.mult)
                        v.tensor_tensor(sums[:, t:t + 1], meanl[:, t:t + 1],
                                        meanl[:, t:t + 1], Op.mult)
                        instr = v.tensor_tensor(varl[:, t:t + 1], varl[:, t:t + 1],
                                                sums[:, t:t + 1], Op.subtract)
                        sems.inc(instr, SV)
                    if 0 <= i - 4 < NB:
                        t = i - 4
                        sems.wait(v, SS, _RS[t])
                        instr = v.tensor_scalar(bufA[:, t, :], srcof(t),
                                                meanl[:, t:t + 1], rstdl[:, t:t + 1],
                                                Op.subtract, Op.mult)
                        sems.inc(instr, SV)
                    if 0 <= i - 6 < NB:
                        t = i - 6
                        col = 512 + 128 * t if t < 4 else 128 * (t - 4)
                        sems.wait(v, ST, _TR[t])
                        instr = v.tensor_copy(
                            fmb[:, :, col:col + 128],
                            psT[t % 2][:, :].rearrange("p (c n) -> p c n", n=P))
                        sems.inc(instr, SV)

            @blk.scalar
            def _(sc):
                for i in range(NS):
                    if 0 <= i - 1 < NB:
                        t = i - 1
                        if t == 4:
                            sc.wait_ge(sems.h["p1ds"], x0 + 16)
                        instr = sc.activation(bufA[:, t, :], srcof(t), AF.Square,
                                              accum_out=sumsq[:, t:t + 1])
                        sc.drain()
                        sems.inc(instr, SS)
                    if 0 <= i - 3 < NB:
                        t = i - 3
                        sems.wait(sc, SV, _VA[t])
                        sc.activation(rstdl[:, t:t + 1], varl[:, t:t + 1], AF.Ln, bias=epsc)
                        instr = sc.activation(rstdl[:, t:t + 1], rstdl[:, t:t + 1],
                                              AF.Exp, scale=-0.5)
                        sems.inc(instr, SS)

            @blk.tensor
            def _(te):
                for i in range(NS):
                    if 0 <= i - 5 < NB:
                        t = i - 5
                        sems.wait(te, SV, _APL[t])
                        if t >= 2:
                            sems.wait(te, SV, _CP[t - 2])
                        for c in range(8):
                            instr = te.transpose(psT[t % 2][:, c * P:(c + 1) * P],
                                                 bufA[:, t, c * P:(c + 1) * P], id_bf[:, :])
                        sems.inc(instr, ST)


        with ExitStack() as pst, nc.Block() as blk:
            psq = [pst.enter_context(nc.psum_tensor(f"ps_p1_q{i}_b{b}", [P, 512], F32))
                   for i in range(2)]
            pstr = [pst.enter_context(nc.psum_tensor(f"ps_p1_r{i}_b{b}", [P, 1024], BF16))
                    for i in range(2)]
            t0, v0, w0 = sems.val(ST), sems.val(SV), sems.val("p1w")
            sched = []
            for ns in range(6):
                for ti in range(4 if ns < 2 else 8):
                    sched.append((ns, ti))
            NJ = len(sched)
            tmm, ttr = {}, {}
            tc = t0 + 1
            tmm[0] = tc
            for j in range(NJ):
                if j + 1 < NJ:
                    tc += 1
                    tmm[j + 1] = tc
                if sched[j][0] < 4:
                    tc += 1
                    ttr[j] = tc
            vps, vrp, vtc = {}, {}, {}
            vc = v0
            for j in range(NJ):
                if sched[j][0] < 4:
                    vc += 1; vps[j] = vc
                    vc += 1; vrp[j] = vc
                else:
                    vc += 1; vps[j] = vc
                if j >= 1 and sched[j - 1][0] < 4:
                    vc += 1; vtc[j - 1] = vc
            if sched[NJ - 1][0] < 4:
                vc += 1; vtc[NJ - 1] = vc

            @blk.gpsimd
            def _(g):
                for ns in range(6):
                    if ns >= 2:
                        last = max(j for j, (n2, _) in enumerate(sched) if n2 == ns - 2)
                        sems.wait(g, ST, tmm[last])
                    src = w["WqkvT"][:, ns * 512:(ns + 1) * 512]
                    sems.inc(g.dma_start(wq2[ns % 2][:, :, :],
                                         src.rearrange("(c p) n -> p c n", p=P)),
                             "p1w", 16)

            def emit_mm(te, j):
                ns, ti = sched[j]
                sems.wait(te, "p1w", w0 + 16 * (ns + 1))
                if j >= 2:
                    sems.wait(te, SV, vps[j - 2])
                col = 512 + 128 * ti if ns < 2 else 128 * ti
                for c in range(8):
                    instr = te.matmul(psq[j % 2][:, :], fmb[:, c, col:col + 128],
                                      wq2[ns % 2][:, c, :],
                                      start=(c == 0), stop=(c == 7))
                sems.inc(instr, ST)

            @blk.tensor
            def _(te):
                emit_mm(te, 0)
                for j, (ns, ti) in enumerate(sched):
                    if j + 1 < NJ:
                        emit_mm(te, j + 1)
                    if ns < 4:
                        sems.wait(te, SV, vrp[j])
                        if j >= 2 and sched[j - 2][0] < 4:
                            sems.wait(te, SV, vtc[j - 2])
                        for i in range(8):
                            par, slot = i % 2, i // 2
                            instr = te.transpose(
                                pstr[j % 2][64 * par:64 * par + 64, slot * 128:(slot + 1) * 128],
                                qrope[:, j % 2, i * 64:(i + 1) * 64], id_bf[:, :])
                        sems.inc(instr, ST)

            def emit_copy(v, j):
                ns, ti = sched[j]
                sems.wait(v, ST, ttr[j])
                if ns < 2:
                    dst = qT[:, 4 * ns:4 * ns + 4, ti * 128:(ti + 1) * 128]
                else:
                    dst = kT[:, 4 * (ns - 2):4 * (ns - 2) + 4, ti * 128:(ti + 1) * 128]
                instr = v.tensor_copy(
                    dst, pstr[j % 2][:, 0:512].rearrange("p (s n) -> p s n", n=P))
                sems.inc(instr, SV)

            @blk.vector
            def _(v):
                v = VP(v)
                for j, (ns, ti) in enumerate(sched):
                    sems.wait(v, ST, tmm[j])
                    if ns < 4:
                        if ns < 2:
                            cosv, sinv, gidx = cosq, sinq, ti
                        else:
                            cosv, sinv, gidx = cosk, sink, ti
                        pv = psq[j % 2][:, :].rearrange("p (h d) -> p h d", d=64)
                        pvs = psq[j % 2][:, :].rearrange("p (h a c) -> p h a c", a=32, c=2)[
                            :, :, :, ::-1]
                        cb = cosv[:, gidx, :].unsqueeze(1).broadcast_to([P, 8, 64])
                        sb = sinv[:, gidx, :].rearrange("p (a c) -> p a c", c=2).unsqueeze(
                            1).broadcast_to([P, 8, 32, 2])
                        v.tensor_tensor(scrA[:, :].rearrange("p (h d) -> p h d", d=64),
                                        pv, cb, Op.mult)
                        instr = v.tensor_tensor(
                            scrB[:, :].rearrange("p (h a c) -> p h a c", a=32, c=2),
                            pvs, sb, Op.mult)
                        sems.inc(instr, SV)
                        if j >= 2 and sched[j - 2][0] < 4:
                            sems.wait(v, ST, ttr[j - 2])
                        instr = v.tensor_tensor(qrope[:, j % 2, :], scrA[:, :],
                                                scrB[:, :], Op.add)
                        sems.inc(instr, SV)
                    else:
                        instr = v.tensor_copy(vbv[:, ti, (ns - 4) * 512:(ns - 3) * 512],
                                              psq[j % 2][:, :])
                        sems.inc(instr, SV)
                    if j >= 1 and sched[j - 1][0] < 4:
                        emit_copy(v, j - 1)
                if sched[NJ - 1][0] < 4:
                    emit_copy(v, NJ - 1)

        with ExitStack() as pst, nc.Block() as blk:
            pss = [pst.enter_context(nc.psum_tensor(f"ps_p1_s{i}_b{b}", [P, D], F32))
                   for i in range(2)]
            pstr2 = [pst.enter_context(nc.psum_tensor(f"ps_p1_t{i}_b{b}", [P, 1024], BF16))
                     for i in range(2)]
            psyp = [pst.enter_context(nc.psum_tensor(f"ps_p1_y{i}_b{b}", [P, 512], F32))
                    for i in range(2)]
            t0, v0, s0 = sems.val(ST), sems.val(SV), sems.val(SS)
            NI = 64
            NSLOT = NI + 3
            _S, _T, _PV = {}, {}, {}
            _EX, _PT, _YPC = {}, {}, {}
            _MK, _NM = {}, {}
            tc, vc, sc_ = t0, v0, s0
            for j in range(3):
                tc += 1; _S[j] = tc
            for i in range(NSLOT):
                if i + 3 < NI:
                    tc += 1; _S[i + 3] = tc
                if 0 <= i - 1 < NI:
                    tc += 1; _T[i - 1] = tc
                if 0 <= i - 2 < NI:
                    tc += 1; _PV[i - 2] = tc
            for j in range(2):
                sc_ += 1; _EX[j] = sc_
            for i in range(NSLOT):
                if i + 2 < NI:
                    sc_ += 1; _EX[i + 2] = sc_
                if 0 <= i - 1 < NI:
                    sc_ += 1; _PT[i - 1] = sc_
                if i >= 10 and (i - 10) % 8 == 0 and (i - 10) // 8 < 8:
                    sc_ += 1; _YPC[(i - 10) // 8] = sc_
            vc += 1; _MK[0] = vc
            for i in range(NSLOT):
                if i + 1 < NI:
                    vc += 1; _MK[i + 1] = vc
                if 0 <= i < NI:
                    vc += 1; _NM[i] = vc

            def emit_S(te, j):
                h, g = j // 4, j % 4
                par, hp = h % 2, h // 2
                if j >= 2:
                    sems.wait(te, SS, _EX[j - 2])
                lt = qT[64 * par:64 * par + 64, hp, g * 128:(g + 1) * 128]
                kv = kT[64 * par:64 * par + 64, hp, :]
                te.matmul(pss[j % 2][:, 0:512], lt, kv[:, g * 128:g * 128 + 512],
                          start=True, stop=True)
                instr = te.matmul(pss[j % 2][:, 512:640], lt,
                                  kv[:, g * 128 + 512:g * 128 + 640],
                                  start=True, stop=True)
                sems.inc(instr, ST)

            @blk.tensor
            def _(te):
                for j in range(3):
                    emit_S(te, j)
                for i in range(NSLOT):
                    if i + 3 < NI:
                        emit_S(te, i + 3)
                    if 0 <= i - 1 < NI:
                        j = i - 1
                        sems.wait(te, SV, _NM[j])
                        if j >= 2:
                            sems.wait(te, SS, _PT[j - 2])
                        for kc in range(5):
                            instr = te.transpose(pstr2[j % 2][:, kc * 128:(kc + 1) * 128],
                                                 Pn[:, j % 4, kc * 128:(kc + 1) * 128],
                                                 id_bf[:, :])
                        sems.inc(instr, ST)
                    if 0 <= i - 2 < NI:
                        j = i - 2
                        h, g = j // 4, j % 4
                        par, hp = h % 2, h // 2
                        sems.wait(te, SS, _PT[j])
                        if j % 8 == 0 and hp >= 2:
                            sems.wait(te, SS, _YPC[hp - 2])
                        for kc in range(5):
                            instr = te.matmul(
                                psyp[hp % 2][64 * par:64 * par + 64, g * 128:(g + 1) * 128],
                                vbv[:, g + kc, h * 64:(h + 1) * 64],
                                PTs[:, j % 2, kc * 128:(kc + 1) * 128],
                                start=(kc == 0), stop=(kc == 4))
                        sems.inc(instr, ST)

            @blk.vector
            def _(v):
                v = VP(v)

                def emit_stt(j):
                    g = j % 4
                    sems.wait(v, SS, _EX[j])
                    instr = v.scalar_tensor_tensor(
                        Pn[:, j % 4, :], Pn[:, j % 4, :], 1.0, mask_sb[:, g, :],
                        Op.mult, Op.mult,
                        accum_out=cols[:, 2 + (j % 2):3 + (j % 2)])
                    sems.inc(instr, SV)

                emit_stt(0)
                for i in range(NSLOT):
                    if i + 1 < NI:
                        emit_stt(i + 1)
                    if 0 <= i < NI:
                        v.reciprocal(cols[:, 4 + (i % 2):5 + (i % 2)],
                                     cols[:, 2 + (i % 2):3 + (i % 2)])
                        instr = v.tensor_scalar(Pn[:, i % 4, :], Pn[:, i % 4, :],
                                                cols[:, 4 + (i % 2):5 + (i % 2)],
                                                None, Op.mult)
                        sems.inc(instr, SV)

            @blk.scalar
            def _(sc):
                def emit_exp(j):
                    sems.wait(sc, ST, _S[j])
                    if j >= 4:
                        sems.wait(sc, ST, _T[j - 4])
                    instr = sc.activation(Pn[:, j % 4, :], pss[j % 2][:, 0:640],
                                          AF.Exp, bias=cols[:, 6:7])
                    sems.inc(instr, SS)

                emit_exp(0)
                emit_exp(1)
                for i in range(NSLOT):
                    if i + 2 < NI:
                        emit_exp(i + 2)
                    if 0 <= i - 1 < NI:
                        j = i - 1
                        sems.wait(sc, ST, _T[j])
                        if j >= 2:
                            sems.wait(sc, ST, _PV[j - 2])
                        instr = sc.copy(PTs[:, j % 2, :], pstr2[j % 2][:, 0:640])
                        sems.inc(instr, SS)
                    if i >= 10 and (i - 10) % 8 == 0 and (i - 10) // 8 < 8:
                        hp = (i - 10) // 8
                        sems.wait(sc, ST, _PV[8 * hp + 7])
                        instr = sc.copy(yfm[:, hp, :], psyp[hp % 2][:, :])
                        sems.inc(instr, SS)

        with ExitStack() as pst, nc.Block() as blk:
            pso = [pst.enter_context(nc.psum_tensor(f"ps_p1_o{i}_b{b}", [P, D], F32))
                   for i in range(2)]
            t0, w0 = sems.val(ST), sems.val("p1w")
            mm_vals = {g: t0 + g + 1 for g in range(4)}
            scrOv = bufA[:, 10:16, :].rearrange("p a b -> p (a b)").bitcast(
                F32).rearrange("p (a b) -> p a b", b=D)
            _EV = emit_postln_pipe(nc, blk, sems, "p1", mm_vals, pso,
                                   [4 * b + g for g in range(4)],
                                   xbuf, scrOv, bufA[:, 8, :], stats, epsc, 4)

            @blk.gpsimd
            def _(g):
                for half in range(2):
                    src = w["WprojT"][:, half * 512:(half + 1) * 512]
                    sems.inc(g.dma_start(wq2[half][:, :, :],
                                         src.rearrange("(c p) n -> p c n", p=P)),
                             "p1w", 16)

            @blk.tensor
            def _(te):
                sems.wait(te, "p1w", w0 + 32)
                for g in range(4):
                    if g >= 2:
                        sems.wait(te, SV, _EV[g - 2])
                    for half in range(2):
                        for hp in range(8):
                            te.matmul(pso[g % 2][:, half * 512:(half + 1) * 512],
                                      yfm[:, hp, g * 128:(g + 1) * 128],
                                      wq2[half][:, hp, :],
                                      start=(hp == 0), stop=False)
                        instr = te.matmul(pso[g % 2][:, half * 512:(half + 1) * 512],
                                          ones_row[:, :],
                                          bprojr[:, half * 512:(half + 1) * 512],
                                          start=False, stop=True)
                    sems.inc(instr, ST)

    return sems



def emit_phase2(nc, st, env):
    xbuf, bufA, bufB = env["xbuf"], env["bufA"], env["bufB"]
    ones_row, id_bf = env["ones_row"], env["id_bf"]
    epsc = env["epsc"][:, :]
    w = env["w2"]
    sems = Sems(nc, st, ["p2dma", "p2ds", "v_p2", "s_p2", "t_p2"])
    SV, SS, ST, SD = "v_p2", "s_p2", "t_p2", "p2dma"

    Win_bf = st.enter_context(nc.sbuf_tensor("sb_p2_Win", [P, 8, 64], BF16))
    binr = st.enter_context(nc.sbuf_tensor("sb_p2_binr", [1, 64], BF16))
    Wout_bf = st.enter_context(nc.sbuf_tensor("sb_p2_Wout", [64, D], BF16))
    boutr = st.enter_context(nc.sbuf_tensor("sb_p2_boutr", [1, D], BF16))
    W1b = st.enter_context(nc.sbuf_tensor("sb_p2_W1b", [P, D], F32))
    b1b = st.enter_context(nc.sbuf_tensor("sb_p2_b1b", [P, D], F32))
    W2b = st.enter_context(nc.sbuf_tensor("sb_p2_W2b", [P, D], F32))
    b2b = st.enter_context(nc.sbuf_tensor("sb_p2_b2b", [P, 64], F32))
    s_f32 = st.enter_context(nc.sbuf_tensor("sb_p2_s", [P, NT, 64], F32))
    gbuf = st.enter_context(nc.sbuf_tensor("sb_p2_g", [P, NT, 64], BF16))
    gfm = st.enter_context(nc.sbuf_tensor("sb_p2_gfm", [64, NT, P], BF16))
    scrE1 = st.enter_context(nc.sbuf_tensor("sb_p2_scr1", [P, 2, D], F32))
    scrE2 = st.enter_context(nc.sbuf_tensor("sb_p2_scr2", [P, 2, D], BF16))
    stats = st.enter_context(nc.sbuf_tensor("sb_p2_stats", [P, 5, NT], F32))
    stats2 = st.enter_context(nc.sbuf_tensor("sb_p2_stats2", [P, 5, NT], F32))


    with nc.Block() as blk:
        @blk.gpsimd
        def _(g):
            sems.inc(g.dma_start(Win_bf[:, :, :], w["Win"].rearrange("(c p) n -> p c n", p=P)), SD, 16)
            sems.inc(g.dma_start(binr[:, :], w["binr"]), SD, 16)
            sems.inc(g.dma_start(Wout_bf[:, :], w["Wout"]), SD, 16)
            sems.inc(g.dma_start(boutr[:, :], w["boutr"]), SD, 16)

        @blk.sync
        def _(s):
            sems.inc(s.dma_start(W1b[:, :], w["W1b"]), "p2ds", 16)
            sems.inc(s.dma_start(b1b[:, :], w["b1b"]), "p2ds", 16)
            sems.inc(s.dma_start(W2b[:, :], w["W2b"]), "p2ds", 16)
            sems.inc(s.dma_start(b2b[:, :], w["b2b"]), "p2ds", 16)
            s.wait_ge(sems.h[SD], sems.val(SD))
            s.wait_ge(sems.h["p2ds"], sems.val("p2ds"))

        @blk.gpsimd
        def _(g):
            g.wait_ge(sems.h[SD], sems.val(SD))
            g.wait_ge(sems.h["p2ds"], sems.val("p2ds"))

    with nc.Block() as blk:
        mean, rstd = emit_ln_stats(nc, blk, sems,
                                   [xbuf[:, t, :] for t in range(NT)], D,
                                   stats, bufA[:, 0, :], epsc, "p2")

    with nc.Block() as blk:
        @blk.vector
        def _(v):
            v = VP(v)
            for t in range(NT):
                instr = v.tensor_scalar(bufA[:, t, :], xbuf[:, t, :],
                                        mean[:, t:t + 1], rstd[:, t:t + 1],
                                        Op.subtract, Op.mult)

    with ExitStack() as pst, nc.Block() as blk:
        psT = [pst.enter_context(nc.psum_tensor(f"sb_p2_psT{i}", [P, D], BF16)) for i in range(2)]
        t0, v0 = sems.val(ST), sems.val(SV)

        @blk.tensor
        def _(te):
            for t in range(NT):
                if t >= 2:
                    sems.wait(te, SV, v0 + t - 1)
                for c in range(8):
                    instr = te.transpose(psT[t % 2][:, c * P:(c + 1) * P],
                                         bufA[:, t, c * P:(c + 1) * P], id_bf[:, :])
                sems.inc(instr, ST)

        @blk.vector
        def _(v):
            v = VP(v)
            for t in range(NT):
                sems.wait(v, ST, t0 + t + 1)
                instr = v.tensor_copy(
                    bufB[:, :, t * P:(t + 1) * P],
                    psT[t % 2][:, :].rearrange("p (c n) -> p c n", n=P))
                sems.inc(instr, SV)

    with ExitStack() as pst, nc.Block() as blk:
        psS = [pst.enter_context(nc.psum_tensor(f"sb_p2_psS{i}", [P, 512], F32)) for i in range(2)]
        t0, v0 = sems.val(ST), sems.val(SV)

        @blk.tensor
        def _(te):
            for t in range(NT):
                if t >= 2:
                    sems.wait(te, SV, v0 + t - 1)
                for c in range(8):
                    te.matmul(psS[t % 2][:, 0:64], bufB[:, c, t * P:(t + 1) * P],
                              Win_bf[:, c, :], start=(c == 0), stop=False)
                instr = te.matmul(psS[t % 2][:, 0:64], ones_row[:, :], binr[:, :],
                                  start=False, stop=True)
                sems.inc(instr, ST)

        @blk.vector
        def _(v):
            v = VP(v)
            for t in range(NT):
                sems.wait(v, ST, t0 + t + 1)
                instr = v.tensor_copy(s_f32[:, t, :], psS[t % 2][:, 0:64])
                sems.inc(instr, SV)

    with nc.Block() as blk:
        v0, s0 = sems.val(SV), sems.val(SS)
        W1v = W1b[:, :].rearrange("p (m i) -> p m i", i=16)
        _M1, _M2, _RL = {}, {}, {}
        vc, sc_ = v0, s0
        for i in range(NT + 2):
            if 0 <= i - 2 < NT:
                vc += 1; _M2[i - 2] = vc
            if 0 <= i < NT:
                vc += 1; _M1[i] = vc
        for i in range(NT + 2):
            if 0 <= i - 1 < NT:
                sc_ += 1; _RL[i - 1] = sc_

        @blk.vector
        def _(v):
            v = VP(v)
            for i in range(NT + 2):
                if 0 <= i - 2 < NT:
                    t = i - 2
                    pa = t % 2
                    sems.wait(v, SS, _RL[t])
                    v.tensor_tensor(scrE1[:, pa, :], scrE2[:, pa, :], W2b[:, :], Op.mult)
                    v.tensor_reduce(s_f32[:, t, :], scrE1[:, pa, :].rearrange(
                        "p (m i) -> p m i", i=16), AX.X, Op.add)
                    instr = v.tensor_tensor(gbuf[:, t, :], s_f32[:, t, :], b2b[:, :], Op.add)
                    sems.inc(instr, SV)
                if 0 <= i < NT:
                    t = i
                    pa = t % 2
                    sv = s_f32[:, t, :].rearrange("p (m o) -> p m o", o=1)
                    v.tensor_tensor(scrE1[:, pa, :].rearrange("p (m i) -> p m i", i=16),
                                    sv.broadcast_to([P, 64, 16]), W1v, Op.mult)
                    instr = v.tensor_tensor(scrE1[:, pa, :], scrE1[:, pa, :],
                                            b1b[:, :], Op.add)
                    sems.inc(instr, SV)

        @blk.scalar
        def _(sc):
            for i in range(NT + 2):
                if 0 <= i - 1 < NT:
                    t = i - 1
                    pa = t % 2
                    sems.wait(sc, SV, _M1[t])
                    instr = sc.activation(scrE2[:, pa, :], scrE1[:, pa, :], AF.Relu)
                    sems.inc(instr, SS)


    with ExitStack() as pst, nc.Block() as blk:
        psG = [pst.enter_context(nc.psum_tensor(f"sb_p2_psG{i}", [64, 1024], BF16)) for i in range(2)]
        t0, v0 = sems.val(ST), sems.val(SV)

        @blk.tensor
        def _(te):
            for bank in range(2):
                for slot in range(8):
                    t = bank * 8 + slot
                    instr = te.transpose(psG[bank][:, slot * P:(slot + 1) * P],
                                         gbuf[:, t, :], id_bf[:, :])
                sems.inc(instr, ST)

        @blk.vector
        def _(v):
            v = VP(v)
            for bank in range(2):
                sems.wait(v, ST, t0 + bank + 1)
                instr = v.tensor_copy(
                    gfm[:, bank * 8:(bank + 1) * 8, :],
                    psG[bank][:, :].rearrange("p (s n) -> p s n", n=P))
                sems.inc(instr, SV)

    with ExitStack() as pst, nc.Block() as blk:
        psO = [pst.enter_context(nc.psum_tensor(f"sb_p2_psO{i}", [P, D], F32)) for i in range(2)]
        scr3 = pst.enter_context(nc.sbuf_tensor("sb_p2_scr3", [P, 3, D], F32))
        t0 = sems.val(ST)
        mm_vals = {t: t0 + t + 1 for t in range(NT)}
        _EV = emit_postln_pipe(nc, blk, sems, "p2", mm_vals, psO, list(range(NT)),
                               xbuf, scr3, bufA[:, 0, :], stats2, epsc, NT)

        @blk.tensor
        def _(te):
            for t in range(NT):
                if t >= 2:
                    sems.wait(te, SV, _EV[t - 2])
                for half in range(2):
                    te.matmul(psO[t % 2][:, half * 512:(half + 1) * 512],
                              gfm[:, t, :], Wout_bf[:, half * 512:(half + 1) * 512],
                              start=True, stop=False)
                    instr = te.matmul(psO[t % 2][:, half * 512:(half + 1) * 512],
                                      ones_row[:, :], boutr[:, half * 512:(half + 1) * 512],
                                      start=False, stop=True)
                sems.inc(instr, ST)



def emit_ln_to_fm(nc, env, sems, stats, tag):
    xbuf, bufA, bufB = env["xbuf"], env["bufA"], env["bufB"]
    id_bf = env["id_bf"]
    epsc = env["epsc"][:, :]
    SV, SS, ST = f"v_{tag}", f"s_{tag}", f"t_{tag}"
    sums, sumsq = stats[:, 0, :], stats[:, 1, :]
    mean, var, rstd = stats[:, 2, :], stats[:, 3, :], stats[:, 4, :]

    with ExitStack() as pst, nc.Block() as blk:
        psT = [pst.enter_context(nc.psum_tensor(f"ps_{tag}_lnT{i}", [P, D], BF16))
               for i in range(2)]
        v0, s0, t0 = sems.val(SV), sems.val(SS), sems.val(ST)
        NS = NT + 7
        _RD, _VA, _APL, _CP, _SQ, _RS, _TR = {}, {}, {}, {}, {}, {}, {}
        vc, sc_, tc = v0, s0, t0
        for i in range(NS):
            if 0 <= i < NT:
                vc += 1; _RD[i] = vc
            if 0 <= i - 2 < NT:
                vc += 1; _VA[i - 2] = vc
            if 0 <= i - 4 < NT:
                vc += 1; _APL[i - 4] = vc
            if 0 <= i - 6 < NT:
                vc += 1; _CP[i - 6] = vc
        for i in range(NS):
            if 0 <= i - 1 < NT:
                sc_ += 1; _SQ[i - 1] = sc_
            if 0 <= i - 3 < NT:
                sc_ += 1; _RS[i - 3] = sc_
        for i in range(NS):
            if 0 <= i - 5 < NT:
                tc += 1; _TR[i - 5] = tc

        @blk.vector
        def _(v):
            v = VP(v)
            for i in range(NS):
                if 0 <= i < NT:
                    t = i
                    instr = v.tensor_reduce(sums[:, t:t + 1], xbuf[:, t, :], AX.X, Op.add)
                    sems.inc(instr, SV)
                if 0 <= i - 2 < NT:
                    t = i - 2
                    sems.wait(v, SS, _SQ[t])
                    v.tensor_scalar(mean[:, t:t + 1], sums[:, t:t + 1], 1.0 / D, None, Op.mult)
                    v.tensor_scalar(var[:, t:t + 1], sumsq[:, t:t + 1], 1.0 / D, None, Op.mult)
                    v.tensor_tensor(sums[:, t:t + 1], mean[:, t:t + 1], mean[:, t:t + 1], Op.mult)
                    instr = v.tensor_tensor(var[:, t:t + 1], var[:, t:t + 1],
                                            sums[:, t:t + 1], Op.subtract)
                    sems.inc(instr, SV)
                if 0 <= i - 4 < NT:
                    t = i - 4
                    sems.wait(v, SS, _RS[t])
                    instr = v.tensor_scalar(bufA[:, t, :], xbuf[:, t, :],
                                            mean[:, t:t + 1], rstd[:, t:t + 1],
                                            Op.subtract, Op.mult)
                    sems.inc(instr, SV)
                if 0 <= i - 6 < NT:
                    t = i - 6
                    sems.wait(v, ST, _TR[t])
                    instr = v.tensor_copy(
                        bufB[:, :, t * P:(t + 1) * P],
                        psT[t % 2][:, :].rearrange("p (c n) -> p c n", n=P))
                    sems.inc(instr, SV)

        @blk.scalar
        def _(sc):
            for i in range(NS):
                if 0 <= i - 1 < NT:
                    t = i - 1
                    instr = sc.activation(bufA[:, t, :], xbuf[:, t, :], AF.Square,
                                          accum_out=sumsq[:, t:t + 1])
                    sc.drain()
                    sems.inc(instr, SS)
                if 0 <= i - 3 < NT:
                    t = i - 3
                    sems.wait(sc, SV, _VA[t])
                    sc.activation(rstd[:, t:t + 1], var[:, t:t + 1], AF.Ln, bias=epsc)
                    instr = sc.activation(rstd[:, t:t + 1], rstd[:, t:t + 1],
                                          AF.Exp, scale=-0.5)
                    sems.inc(instr, SS)

        @blk.tensor
        def _(te):
            for i in range(NS):
                if 0 <= i - 5 < NT:
                    t = i - 5
                    sems.wait(te, SV, _APL[t])
                    if t >= 2:
                        sems.wait(te, SV, _CP[t - 2])
                    for c in range(8):
                        instr = te.transpose(psT[t % 2][:, c * P:(c + 1) * P],
                                             bufA[:, t, c * P:(c + 1) * P], id_bf[:, :])
                    sems.inc(instr, ST)
    return mean, rstd



_TMFM_N = [0]


def emit_tm_to_fm(nc, sems, tag, src, dst, id_bf):
    SV, ST = f"v_{tag}", f"t_{tag}"
    _TMFM_N[0] += 1
    with ExitStack() as pst, nc.Block() as blk:
        psT = [pst.enter_context(nc.psum_tensor(f"ps_{tag}_T{i}_{_TMFM_N[0]}",
                                                [P, D], BF16)) for i in range(2)]
        t0, v0 = sems.val(ST), sems.val(SV)

        @blk.tensor
        def _(te):
            for t in range(NT):
                if t >= 2:
                    sems.wait(te, SV, v0 + t - 1)
                for c in range(8):
                    instr = te.transpose(psT[t % 2][:, c * P:(c + 1) * P],
                                         src[:, t, c * P:(c + 1) * P], id_bf[:, :])
                sems.inc(instr, ST)

        @blk.vector
        def _(v):
            v = VP(v)
            for t in range(NT):
                sems.wait(v, ST, t0 + t + 1)
                instr = v.tensor_copy(
                    dst[:, :, t * P:(t + 1) * P],
                    psT[t % 2][:, :].rearrange("p (c n) -> p c n", n=P))
                sems.inc(instr, SV)



def emit_phase3(nc, st, env):
    xbuf, bufA, bufB = env["xbuf"], env["bufA"], env["bufB"]
    ones_row, id_bf = env["ones_row"], env["id_bf"]
    epsc = env["epsc"][:, :]
    w = env["w3"]
    sems = Sems(nc, st, ["p3dma", "p3ds", "v_p3", "s_p3", "t_p3"])
    SV, SS, ST, SD = "v_p3", "s_p3", "t_p3", "p3dma"

    b1r = st.enter_context(nc.sbuf_tensor("sb_p3_b1r", [1, D], BF16))
    bwr = st.enter_context(nc.sbuf_tensor("sb_p3_bwr", [1, 496], BF16))
    b2r = st.enter_context(nc.sbuf_tensor("sb_p3_b2r", [1, D], BF16))
    h1buf = st.enter_context(nc.sbuf_tensor("sb_p3_h1", [P, NT, D], BF16))
    wsel = st.enter_context(nc.sbuf_tensor("sb_p3_wsel", [P, NT, 64], F32))
    wlscr = st.enter_context(nc.sbuf_tensor("sb_p3_wl", [P, 496], F32))
    escr = st.enter_context(nc.sbuf_tensor("sb_p3_e", [P, 496], F32))
    hstat = st.enter_context(nc.sbuf_tensor("sb_p3_hs", [P, 3, 16], F32))
    scr1 = st.enter_context(nc.sbuf_tensor("sb_p3_s1", [P, D], F32))
    scr2 = st.enter_context(nc.sbuf_tensor("sb_p3_s2", [P, D], F32))
    gb = env["bufB"][:, 2, :].bitcast(F32)
    bb = env["bufB"][:, 4, :].bitcast(F32)
    sqd = env["bufB"][:, 0, 0:D]
    stats = st.enter_context(nc.sbuf_tensor("sb_p3_stats", [P, 5, NT], F32))
    statsR = st.enter_context(nc.sbuf_tensor("sb_p3_statsR", [P, 5, NT], F32))
    statsA = st.enter_context(nc.sbuf_tensor("sb_p3_statsA", [P, 5, NT], F32))
    wst = ExitStack()
    w1T = wst.enter_context(nc.sbuf_tensor("sb_p3_w1T", [P, 8, D], BF16))
    wwT = wst.enter_context(nc.sbuf_tensor("sb_p3_wwT", [P, 8, 496], BF16))

    with nc.Block() as blk:
        @blk.gpsimd
        def _(g):
            sems.inc(g.dma_start(w1T[:, :, :], w["w1T"].rearrange("(c p) n -> p c n", p=P)), SD, 16)
            sems.inc(g.dma_start(wwT[:, :, :], w["wwT"].rearrange("(c p) n -> p c n", p=P)), SD, 16)
            sems.inc(g.dma_start(b1r[:, :], w["b1r"]), SD, 16)
            sems.inc(g.dma_start(bwr[:, :], w["bwr"]), SD, 16)
            sems.inc(g.dma_start(b2r[:, :], w["b2r"]), SD, 16)
            g.wait_ge(sems.h[SD], sems.val(SD))

        @blk.sync
        def _(s):
            s.wait_ge(sems.h[SD], sems.val(SD))

    mean3, rstd3 = emit_ln_to_fm(nc, env, sems, stats, "p3")

    with ExitStack() as pst, nc.Block() as blk:
        psH = [pst.enter_context(nc.psum_tensor(f"ps_p3_h{i}", [P, D], F32))
               for i in range(2)]
        t0, v0 = sems.val(ST), sems.val(SV)

        @blk.tensor
        def _(te):
            for t in range(NT):
                if t >= 2:
                    sems.wait(te, SV, v0 + t - 1)
                for half in range(2):
                    for c in range(8):
                        te.matmul(psH[t % 2][:, half * 512:(half + 1) * 512],
                                  bufB[:, c, t * P:(t + 1) * P],
                                  w1T[:, c, half * 512:(half + 1) * 512],
                                  start=(c == 0), stop=False)
                    instr = te.matmul(psH[t % 2][:, half * 512:(half + 1) * 512],
                                      ones_row[:, :], b1r[:, half * 512:(half + 1) * 512],
                                      start=False, stop=True)
                sems.inc(instr, ST)

        @blk.vector
        def _(v):
            v = VP(v)
            for t in range(NT):
                sems.wait(v, ST, t0 + t + 1)
                instr = v.tensor_copy(h1buf[:, t, :], psH[t % 2][:, :])
                sems.inc(instr, SV)

    emit_tm_to_fm(nc, sems, "p3", h1buf, bufB, id_bf)

    with ExitStack() as pst, nc.Block() as blk:
        psW = [pst.enter_context(nc.psum_tensor(f"ps_p3_w{i}", [P, 512], F32))
               for i in range(2)]
        t0, v0, s0 = sems.val(ST), sems.val(SV), sems.val(SS)

        @blk.tensor
        def _(te):
            for t in range(NT):
                if t >= 2:
                    sems.wait(te, SV, v0 + 3 * (t - 2) + 1)
                for c in range(8):
                    te.matmul(psW[t % 2][:, 0:496], bufB[:, c, t * P:(t + 1) * P],
                              wwT[:, c, :], start=(c == 0), stop=False)
                instr = te.matmul(psW[t % 2][:, 0:496], ones_row[:, :], bwr[:, :],
                                  start=False, stop=True)
                sems.inc(instr, ST)

        @blk.vector
        def _(v):
            v = VP(v)
            for t in range(NT):
                sems.wait(v, ST, t0 + t + 1)
                v.tensor_copy(wlscr[:, :], psW[t % 2][:, :496])
                instr = v.tensor_reduce(
                    hstat[:, 0, :], wlscr[:, :].rearrange("p (h k) -> p h k", k=31),
                    AX.X, Op.max, negate=True)
                sems.inc(instr, SV)
                nm = hstat[:, 0, :].rearrange("p (h o) -> p h o", o=1)
                instr = v.tensor_tensor(
                    wlscr[:, :].rearrange("p (h k) -> p h k", k=31),
                    wlscr[:, :].rearrange("p (h k) -> p h k", k=31),
                    nm.broadcast_to([P, 16, 31]), Op.add)
                sems.inc(instr, SV)
                sems.wait(v, SS, s0 + t + 1)
                v.tensor_reduce(hstat[:, 1, :],
                                escr[:, :].rearrange("p (h k) -> p h k", k=31),
                                AX.X, Op.add)
                v.reciprocal(hstat[:, 2, :], hstat[:, 1, :])
                rs = hstat[:, 2, :].rearrange("p (h o) -> p h o", o=1)
                ev = escr[:, :].rearrange("p (h k) -> p h k", k=31)[:, :, 27:31]
                instr = v.tensor_tensor(
                    wsel[:, t, :].rearrange("p (h k) -> p h k", k=4),
                    ev, rs.broadcast_to([P, 16, 4]), Op.mult)
                sems.inc(instr, SV)
                tb, gph = t // 4, t % 4
                acc = scr1[:, :]
                for j in range(tb + 1):
                    wv = wsel[:, t, :].rearrange("p (h k) -> p h k", k=4)[
                        :, :, 3 - tb + j:4 - tb + j]
                    hv = h1buf[:, j * 4 + gph, :].rearrange("p (h r) -> p h r", r=64)
                    if j == 0:
                        v.tensor_tensor(acc.rearrange("p (h r) -> p h r", r=64),
                                        hv, wv.broadcast_to([P, 16, 64]), Op.mult)
                    else:
                        v.tensor_tensor(scr2[:, :].rearrange("p (h r) -> p h r", r=64),
                                        hv, wv.broadcast_to([P, 16, 64]), Op.mult)
                        v.tensor_tensor(acc, acc, scr2[:, :], Op.add)
                v.tensor_copy(bufA[:, t, :], acc)

        @blk.scalar
        def _(sc):
            for t in range(NT):
                sems.wait(sc, SV, v0 + 3 * t + 2)
                instr = sc.activation(escr[:, :], wlscr[:, :], AF.Exp)
                sems.inc(instr, SS)

    wst.close()
    w2T = st.enter_context(nc.sbuf_tensor("sb_p3_w2T", [P, 8, D], BF16))
    with nc.Block() as blk:
        @blk.gpsimd
        def _(g):
            sems.inc(g.dma_start(w2T[:, :, :], w["w2T"].rearrange("(c p) n -> p c n", p=P)), SD, 16)
            g.wait_ge(sems.h[SD], sems.val(SD))

    emit_tm_to_fm(nc, sems, "p3", bufA, bufB, id_bf)

    gb = h1buf[:, 0:2, :].rearrange("p a b -> p (a b)").bitcast(F32)
    bb = h1buf[:, 2:4, :].rearrange("p a b -> p (a b)").bitcast(F32)
    scrH = h1buf[:, 4:8, :].rearrange("p a b -> p (a b)").bitcast(F32).rearrange(
        "p (a b) -> p a b", b=D)
    scrR = h1buf[:, 8:12, :].rearrange("p a b -> p (a b)").bitcast(F32).rearrange(
        "p (a b) -> p a b", b=D)
    scrF = h1buf[:, 12:14, :].rearrange("p a b -> p (a b)").bitcast(F32)
    with nc.Block() as blk:
        @blk.sync
        def _(s):
            sems.inc(s.dma_start(gb, w["gb"]), "p3ds", 16)
            sems.inc(s.dma_start(bb, w["bb"]), "p3ds", 16)
            s.wait_ge(sems.h["p3ds"], sems.val("p3ds"))

        @blk.vector
        def _(v):
            v = VP(v)
            v.tensor_tensor(statsR[:, 0, :], mean3[:, 0:NT], rstd3[:, 0:NT], Op.mult)
            v.tensor_scalar(statsR[:, 0, :], statsR[:, 0, :], -1.0, None, Op.mult)

    with ExitStack() as pst, nc.Block() as blk:
        psH = [pst.enter_context(nc.psum_tensor(f"ps_p3_o{i}", [P, D], F32))
               for i in range(2)]
        t0, v0, s0 = sems.val(ST), sems.val(SV), sems.val(SS)
        negmr3 = statsR[:, 0, :]
        sumsR, sumsqR = statsR[:, 1, :], statsR[:, 2, :]
        varR, rstdR, meanR = statsR[:, 3, :], statsR[:, 4, :], stats[:, 0, :]
        sumsA, sumsqA = statsA[:, 0, :], statsA[:, 1, :]
        meanA, varA, rstdA = statsA[:, 2, :], statsA[:, 3, :], statsA[:, 4, :]
        NS = NT + 6
        _MM, _EVH, _RS_T, _VAR, _T12, _VAA, _APF = {}, {}, {}, {}, {}, {}, {}
        _SQR, _RSR, _SQA, _RSA = {}, {}, {}, {}
        tc, vc, sc_ = t0, v0, s0
        for i in range(NS):
            if 0 <= i < NT:
                tc += 1; _MM[i] = tc
        for i in range(NS):
            if 0 <= i < NT:
                vc += 1; _EVH[i] = vc
            if 0 <= i - 3 < NT:
                vc += 1; _T12[i - 3] = vc
            if 0 <= i - 1 < NT:
                vc += 1; _RS_T[i - 1] = vc
            if 0 <= i - 2 < NT:
                vc += 1; _VAR[i - 2] = vc
            if 0 <= i - 4 < NT:
                vc += 1; _VAA[i - 4] = vc
            if 0 <= i - 5 < NT:
                vc += 1; _APF[i - 5] = vc
        for i in range(NS):
            if 0 <= i - 2 < NT:
                sc_ += 1; _SQR[i - 2] = sc_
                sc_ += 1; _RSR[i - 2] = sc_
            if 0 <= i - 4 < NT:
                sc_ += 1; _SQA[i - 4] = sc_
                sc_ += 1; _RSA[i - 4] = sc_

        @blk.tensor
        def _(te):
            for t in range(NT):
                if t >= 2:
                    sems.wait(te, SV, _EVH[t - 2])
                for half in range(2):
                    for c in range(8):
                        te.matmul(psH[t % 2][:, half * 512:(half + 1) * 512],
                                  bufB[:, c, t * P:(t + 1) * P],
                                  w2T[:, c, half * 512:(half + 1) * 512],
                                  start=(c == 0), stop=False)
                    instr = te.matmul(psH[t % 2][:, half * 512:(half + 1) * 512],
                                      ones_row[:, :], b2r[:, half * 512:(half + 1) * 512],
                                      start=False, stop=True)
                sems.inc(instr, ST)

        @blk.vector
        def _(v):
            v = VP(v)
            for i in range(NS):
                if 0 <= i < NT:
                    t = i
                    sems.wait(v, ST, _MM[t])
                    instr = v.tensor_scalar(scrH[:, t % 2, :], psH[t % 2][:, :],
                                            negmr3[:, t:t + 1], None, Op.add)
                    sems.inc(instr, SV)
                if 0 <= i - 3 < NT:
                    t = i - 3
                    sems.wait(v, SS, _RSR[t])
                    v.tensor_scalar(scrR[:, t % 2, :], scrR[:, t % 2, :],
                                    meanR[:, t:t + 1], rstdR[:, t:t + 1],
                                    Op.subtract, Op.mult)
                    v.tensor_tensor(scrR[:, t % 2, :], scrR[:, t % 2, :], gb, Op.mult)
                    instr = v.scalar_tensor_tensor(
                        bufA[:, t, :], scrR[:, t % 2, :], 1.0, bb,
                        Op.mult, Op.add, accum_out=sumsA[:, t:t + 1])
                    sems.inc(instr, SV)
                if 0 <= i - 1 < NT:
                    t = i - 1
                    instr = v.scalar_tensor_tensor(
                        scrR[:, t % 2, :], xbuf[:, t, :], rstd3[:, t:t + 1],
                        scrH[:, t % 2, :], Op.mult, Op.add,
                        accum_out=sumsR[:, t:t + 1])
                    sems.inc(instr, SV)
                if 0 <= i - 2 < NT:
                    t = i - 2
                    sems.wait(v, SS, _SQR[t])
                    v.tensor_scalar(meanR[:, t:t + 1], sumsR[:, t:t + 1], 1.0 / D, None, Op.mult)
                    v.tensor_scalar(varR[:, t:t + 1], sumsqR[:, t:t + 1], 1.0 / D, None, Op.mult)
                    v.tensor_tensor(sumsR[:, t:t + 1], meanR[:, t:t + 1],
                                    meanR[:, t:t + 1], Op.mult)
                    instr = v.tensor_tensor(varR[:, t:t + 1], varR[:, t:t + 1],
                                            sumsR[:, t:t + 1], Op.subtract)
                    sems.inc(instr, SV)
                if 0 <= i - 4 < NT:
                    t = i - 4
                    sems.wait(v, SS, _SQA[t])
                    v.tensor_scalar(meanA[:, t:t + 1], sumsA[:, t:t + 1], 1.0 / D, None, Op.mult)
                    v.tensor_scalar(varA[:, t:t + 1], sumsqA[:, t:t + 1], 1.0 / D, None, Op.mult)
                    v.tensor_tensor(sumsA[:, t:t + 1], meanA[:, t:t + 1],
                                    meanA[:, t:t + 1], Op.mult)
                    instr = v.tensor_tensor(varA[:, t:t + 1], varA[:, t:t + 1],
                                            sumsA[:, t:t + 1], Op.subtract)
                    sems.inc(instr, SV)
                if 0 <= i - 5 < NT:
                    t = i - 5
                    sems.wait(v, SS, _RSA[t])
                    v.tensor_scalar(scrF, bufA[:, t, :], meanA[:, t:t + 1],
                                    rstdA[:, t:t + 1], Op.subtract, Op.mult)
                    instr = v.tensor_tensor(xbuf[:, t, :], xbuf[:, t, :], scrF, Op.add)
                    sems.inc(instr, SV)

        @blk.scalar
        def _(sc):
            for i in range(NS):
                if 0 <= i - 2 < NT:
                    t = i - 2
                    sems.wait(sc, SV, _RS_T[t])
                    instr = sc.activation(h1buf[:, 15, :], scrR[:, t % 2, :], AF.Square,
                                          accum_out=sumsqR[:, t:t + 1])
                    sc.drain()
                    sems.inc(instr, SS)
                    sems.wait(sc, SV, _VAR[t])
                    sc.activation(rstdR[:, t:t + 1], varR[:, t:t + 1], AF.Ln, bias=epsc)
                    instr = sc.activation(rstdR[:, t:t + 1], rstdR[:, t:t + 1],
                                          AF.Exp, scale=-0.5)
                    sems.inc(instr, SS)
                if 0 <= i - 4 < NT:
                    t = i - 4
                    sems.wait(sc, SV, _T12[t])
                    instr = sc.activation(h1buf[:, 15, :], bufA[:, t, :], AF.Square,
                                          accum_out=sumsqA[:, t:t + 1])
                    sc.drain()
                    sems.inc(instr, SS)
                    sems.wait(sc, SV, _VAA[t])
                    sc.activation(rstdA[:, t:t + 1], varA[:, t:t + 1], AF.Ln, bias=epsc)
                    instr = sc.activation(rstdA[:, t:t + 1], rstdA[:, t:t + 1],
                                          AF.Exp, scale=-0.5)
                    sems.inc(instr, SS)

    return sems



def emit_phase4(nc, st, env, x_reload_src, x_out_dst, spill_dst=None):
    xbuf, bufA, bufB = env["xbuf"], env["bufA"], env["bufB"]
    ones_row, id_bf = env["ones_row"], env["id_bf"]
    id_f32 = env["id_f32"]
    epsc = env["epsc"][:, :]
    w = env["w4"]
    sems = Sems(nc, st, ["p4w", "p4x", "p4y", "p4b", "v_p4", "s_p4", "t_p4"])
    SV, SS, ST = "v_p4", "s_p4", "t_p4"
    y_acc = xbuf

    W2h = st.enter_context(nc.sbuf_tensor("sb_p4_W2h", [P, 16, D], BF16))
    hbuf = st.enter_context(nc.sbuf_tensor("sb_p4_h", [P, 16, 512], BF16))
    Wg_bf = st.enter_context(nc.sbuf_tensor("sb_p4_Wg", [P, 8, 8], BF16))
    bgr = st.enter_context(nc.sbuf_tensor("sb_p4_bgr", [1, 8], BF16))
    b2s = st.enter_context(nc.sbuf_tensor("sb_p4_b2s", [8, D], BF16))
    Gt = st.enter_context(nc.sbuf_tensor("sb_p4_G", [P, NT, 8], F32))
    wgt = st.enter_context(nc.sbuf_tensor("sb_p4_wgt", [P, NT, 8], F32))
    wgtb = st.enter_context(nc.sbuf_tensor("sb_p4_wgtb", [P, NT, 8], BF16))
    wT = st.enter_context(nc.sbuf_tensor("sb_p4_wT", [8, NT, P], BF16))
    b1s = st.enter_context(nc.sbuf_tensor("sb_p4_b1s", [32, P], F32))
    b1T = st.enter_context(nc.sbuf_tensor("sb_p4_b1T", [P, 32], F32))
    stats = st.enter_context(nc.sbuf_tensor("sb_p4_stats", [P, 5, NT], F32))
    sm = st.enter_context(nc.sbuf_tensor("sb_p4_sm", [P, 2, 8], F32))
    sm2 = st.enter_context(nc.sbuf_tensor("sb_p4_sm2", [P, 2, 8], F32))
    Ebuf = st.enter_context(nc.sbuf_tensor("sb_p4_E", [P, 2, 8], F32))
    cols = st.enter_context(nc.sbuf_tensor("sb_p4_cols", [P, 8], F32))
    xstage = st.enter_context(nc.sbuf_tensor("sb_p4_xst", [P, 2, D], F32))
    ostage = st.enter_context(nc.sbuf_tensor("sb_p4_ost", [P, 2, D], F32))

    import os
    STAGES = int(os.environ.get("P4_STAGES", "9"))

    emit_ln_to_fm(nc, env, sems, stats, "p4")

    if spill_dst is not None:
        with nc.Block() as blk:
            @blk.sync
            def _(s):
                sems.inc(s.dma_start(spill_dst.rearrange("t p d -> p t d"),
                                     xbuf[:, :, :]), "p4x", 16)
                s.wait_ge(sems.h["p4x"], sems.val("p4x"))

    if STAGES < 2:
        return sems
    with ExitStack() as pst, nc.Block() as blk:
        psg = [pst.enter_context(nc.psum_tensor(f"ps_p4_g{i}", [P, 512], F32))
               for i in range(2)]
        pswT = [pst.enter_context(nc.psum_tensor(f"ps_p4_wT{i}", [8, 1024], BF16))
                for i in range(2)]
        st0, sv0, ss0, w0 = sems.val(ST), sems.val(SV), sems.val(SS), sems.val("p4w")

        @blk.gpsimd
        def _(g):
            sems.inc(g.dma_start(Wg_bf[:, :, :],
                                 w["Wg"].rearrange("(c p) e -> p c e", p=P)), "p4w", 16)
            sems.inc(g.dma_start(bgr[:, :], w["bgr"]), "p4w", 16)
            sems.inc(g.dma_start(b2s[:, :], w["b2s"]), "p4w", 16)
            sems.inc(g.dma_start(
                b1s[:, :], w["b1"][0, :].rearrange("(a p) -> a p", p=P)), "p4w", 16)
            for grp in range(4):
                srcw = w["W1"][0, :, grp * 512:(grp + 1) * 512]
                dst = bufA[:, (grp // 2)::2, (grp % 2) * 512:(grp % 2 + 1) * 512]
                sems.inc(g.dma_start(dst, srcw.rearrange("(c p) i -> p c i", p=P)),
                         "p4w", 16)
            for grp in range(4):
                srcw = w["W2"][0, grp * 512:(grp + 1) * 512, :]
                sems.inc(g.dma_start(W2h[:, 4 * grp:4 * grp + 4, :],
                                     srcw.rearrange("(c p) o -> p c o", p=P)),
                         "p4w", 16)

        @blk.tensor
        def _(te):
            sems.wait(te, "p4w", w0 + 48)
            for t in range(NT):
                if t >= 2:
                    sems.wait(te, SV, sv0 + 2 * (t - 2) + 1)
                for c in range(8):
                    te.matmul(psg[t % 2][:, 0:8], bufB[:, c, t * P:(t + 1) * P],
                              Wg_bf[:, c, :], start=(c == 0), stop=False)
                instr = te.matmul(psg[t % 2][:, 0:8], ones_row[:, :], bgr[:, :],
                                  start=False, stop=True)
                sems.inc(instr, ST)
            for t in range(NT):
                sems.wait(te, SV, sv0 + 2 * t + 2)
                instr = te.transpose(pswT[t // 8][:, (t % 8) * P:(t % 8 + 1) * P],
                                     wgtb[:, t, :], id_bf[:, :])
                sems.inc(instr, ST)

        @blk.vector
        def _(v):
            v = VP(v)
            negm1, negm2p = cols[:, 0:1], cols[:, 1:2]
            den, rden = cols[:, 2:3], cols[:, 3:4]
            for t in range(NT):
                G = Gt[:, t, :]
                sems.wait(v, ST, st0 + t + 1)
                v.tensor_copy(G, psg[t % 2][:, 0:8])
                v.tensor_reduce(negm1, G, AX.X, Op.max, negate=True)
                v.tensor_scalar(sm[:, 0, :], G, negm1, None, Op.add)
                v.tensor_scalar(sm2[:, 0, :], sm[:, 0, :], 0.0, None, Op.is_ge)
                v.scalar_tensor_tensor(sm2[:, 0, :], sm2[:, 0, :], -1e30,
                                       sm[:, 0, :], Op.mult, Op.add)
                instr = v.tensor_reduce(negm2p, sm2[:, 0, :], AX.X, Op.max, negate=True)
                sems.inc(instr, SV)
                sems.wait(v, SS, ss0 + t + 1)
                v.tensor_scalar(sm2[:, 0, :], sm[:, 0, :], negm2p, 0.0, Op.add, Op.is_ge)
                v.tensor_scalar(den, den, 1.0, None, Op.add)
                v.reciprocal(rden, den)
                v.tensor_tensor(sm[:, 0, :], Ebuf[:, 0, :], sm2[:, 0, :], Op.mult)
                v.tensor_scalar(wgt[:, t, :], sm[:, 0, :], rden, None, Op.mult)
                instr = v.tensor_copy(wgtb[:, t, :], wgt[:, t, :])
                sems.inc(instr, SV)
            for bank in range(2):
                sems.wait(v, ST, st0 + NT + bank * 8 + 8)
                v.tensor_copy(wT[:, bank * 8:(bank + 1) * 8, :],
                              pswT[bank][:, :].rearrange("p (s n) -> p s n", n=P))

        @blk.scalar
        def _(sc):
            negm1, negm2p, den = cols[:, 0:1], cols[:, 1:2], cols[:, 2:3]
            for t in range(NT):
                sems.wait(sc, SV, sv0 + 2 * t + 1)
                sc.activation(Ebuf[:, 0, :], Gt[:, t, :], AF.Exp, bias=negm1)
                instr = sc.activation(den, negm2p, AF.Exp, scale=-1.0)
                sems.inc(instr, SS)

    if STAGES < 3:
        return sems
    with ExitStack() as pst, nc.Block() as blk:
        psb = [pst.enter_context(nc.psum_tensor(f"ps_p4_binit{i}", [P, D], F32))
               for i in range(2)]
        st0, sv0 = sems.val(ST), sems.val(SV)
        spill_done = sems.val("p4x")

        @blk.tensor
        def _(te):
            for t in range(NT):
                if t >= 2:
                    sems.wait(te, SV, sv0 + t - 1)
                for half in range(2):
                    instr = te.matmul(psb[t % 2][:, half * 512:(half + 1) * 512],
                                      wT[:, t, :], b2s[:, half * 512:(half + 1) * 512],
                                      start=True, stop=True)
                sems.inc(instr, ST)

        @blk.vector
        def _(v):
            v = VP(v)
            if spill_done:
                sems.wait(v, "p4x", spill_done)
            for t in range(NT):
                sems.wait(v, ST, st0 + t + 1)
                instr = v.tensor_copy(y_acc[:, t, :], psb[t % 2][:, :])
                sems.inc(instr, SV)

    if STAGES < 4:
        return sems
    with ExitStack() as pst, nc.Block() as blk:
        psh = [pst.enter_context(nc.psum_tensor(f"ps_p4_h{i}", [P, 512], F32))
               for i in range(2)]
        psy = [pst.enter_context(nc.psum_tensor(f"ps_p4_y{i}", [P, D], F32))
               for i in range(2)]
        psb1 = pst.enter_context(nc.psum_tensor("ps_p4_b1", [P, 512], F32))
        st0, sv0, ss0, w0 = sems.val(ST), sems.val(SV), sems.val(SS), sems.val("p4w")
        NIDX = int(os.environ.get("P4_NIDX", "16"))
        DSUB = int(os.environ.get("P4_DSUB", "3"))
        NTAU = int(os.environ.get("P4_NTAU", "4"))

        TSTRIDE = (16 if DSUB >= 2 else 0) + (4 if DSUB >= 3 else 0)
        H64 = 16 * NTAU if DSUB >= 2 else 0
        Y16 = 4 * NTAU if DSUB >= 3 else 0

        def tbase(idx):
            return st0 + idx * (H64 + Y16)

        def sbase(idx):
            return ss0 + H64 * idx

        NDMA = 8 if DSUB >= 1 else 0

        def wvals(idx):
            base = w0 - 16 * (NDMA + 1) + 16 * sum(
                (NDMA + (1 if i % 2 == 0 else 0)) for i in range(idx))
            n = (NDMA + 1) if idx % 2 == 0 else max(NDMA, 1)
            return [base + 16 * (j + 1) for j in range(n)]

        @blk.gpsimd
        def _(g):
            for idx in range(NIDX):
                if idx == 0:
                    continue
                e, half = idx // 2, idx % 2
                if half == 0:
                    if e > 0:
                        sems.wait(g, "p4b", 2 * e - 1)
                    sems.inc(g.dma_start(
                        b1s[:, :], w["b1"][e, :].rearrange("(a p) -> a p", p=P)),
                        "p4w", 16)
                if DSUB < 1:
                    continue
                if idx >= 1:
                    if DSUB >= 2:
                        sems.wait(g, ST, tbase(idx - 1) + (NTAU - 1) * TSTRIDE + 16)
                    else:
                        sems.wait(g, ST, tbase(idx - 1))
                for grp in range(4):
                    src = w["W1"][e, :, half * 2048 + grp * 512:
                                  half * 2048 + (grp + 1) * 512]
                    dst = bufA[:, (grp // 2)::2, (grp % 2) * 512:(grp % 2 + 1) * 512]
                    sems.inc(g.dma_start(dst, src.rearrange("(c p) i -> p c i", p=P)),
                             "p4w", 16)
                if idx >= 1:
                    sems.wait(g, ST, tbase(idx))
                for grp in range(4):
                    src = w["W2"][e, half * 2048 + grp * 512:
                                  half * 2048 + (grp + 1) * 512, :]
                    sems.inc(g.dma_start(W2h[:, 4 * grp:4 * grp + 4, :],
                                         src.rearrange("(c p) o -> p c o", p=P)),
                             "p4w", 16)

        @blk.tensor
        def _(te):
            for idx in range(NIDX):
                e, half = idx // 2, idx % 2
                wv = wvals(idx)
                b1off = 1 if half == 0 else 0
                if half == 0:
                    sems.wait(te, "p4w", wv[0])
                    if e > 0:
                        sems.wait(te, "p4b", 2 * e)
                    instr = te.transpose(psb1[:, 0:32], b1s[:, :], id_f32[0:32, 0:32])
                    sems.inc(instr, "p4b", 1)
                for tau in range(NTAU):
                    if DSUB < 2:
                        break
                    if int(os.environ.get("P4_SER", "0")) and DSUB >= 3:
                        sems.wait(te, SV, sv0 + idx * 4 * NTAU + tau * 4)
                    for ic in range(16):
                        glob = tau * 16 + ic
                        sems.wait(te, "p4w", wv[b1off + ic // 4])
                        if sbase(idx) + glob - 1 >= 1:
                            sems.wait(te, SS, sbase(idx) + glob - 1)
                        for c in range(8):
                            instr = te.matmul(
                                psh[ic % 2][:, :],
                                bufA[:, 2 * c + ic // 8, (ic % 8) * P:(ic % 8 + 1) * P],
                                bufB[:, c, tau * 512:(tau + 1) * 512],
                                start=(c == 0), stop=(c == 7))
                        sems.inc(instr, ST)
                    if DSUB < 3:
                        continue
                    for tt in range(4):
                        r = idx * 16 + tau * 4 + tt
                        if r >= 2:
                            sems.wait(te, SV, sv0 + r - 1)
                        for half_o in range(2):
                            for ic in range(16):
                                if tt == 0 and half_o == 0:
                                    sems.wait(te, SS, sbase(idx) + tau * 16 + ic + 1)
                                    sems.wait(te, "p4w", wv[b1off + 4 + ic // 4])
                                instr = te.matmul(
                                    psy[r % 2][:, half_o * 512:(half_o + 1) * 512],
                                    hbuf[:, ic, tt * P:(tt + 1) * P],
                                    W2h[:, ic, half_o * 512:(half_o + 1) * 512],
                                    start=(ic == 0), stop=(ic == 15))
                        sems.inc(instr, ST)

        @blk.scalar
        def _(sc):
            for idx in range(NIDX):
                e, half = idx // 2, idx % 2
                b1off = 1 if half == 0 else 0
                for tau in range(NTAU):
                    if DSUB < 2:
                        break
                    for ic in range(16):
                        glob = tau * 16 + ic
                        sems.wait(sc, ST, tbase(idx) + tau * TSTRIDE + ic + 1)
                        if half == 0 and glob == 0:
                            sems.wait(sc, "p4b", 2 * e + 2)
                        instr = sc.activation(hbuf[:, ic, :], psh[ic % 2][:, :],
                                              AF.Gelu,
                                              bias=b1T[:, half * 16 + ic:half * 16 + ic + 1])
                        sems.inc(instr, SS)

        @blk.vector
        def _(v):
            v = VP(v)
            for idx in range(NIDX):
                e, half = idx // 2, idx % 2
                b1off = 1 if half == 0 else 0
                if half == 0:
                    sems.wait(v, "p4b", 2 * e + 1)
                    instr = v.tensor_copy(b1T[:, :], psb1[:, 0:32])
                    sems.inc(instr, "p4b", 1)
                for tau in range(NTAU):
                    if DSUB < 3:
                        break
                    for tt in range(4):
                        r = idx * 16 + tau * 4 + tt
                        t = tau * 4 + tt
                        sems.wait(v, ST, tbase(idx) + tau * TSTRIDE + 16 + tt + 1)
                        if int(os.environ.get("P4_STT", "1")):
                            instr = v.scalar_tensor_tensor(
                                y_acc[:, t, :], psy[r % 2][:, :], wgt[:, t, e:e + 1],
                                y_acc[:, t, :], Op.mult, Op.add)
                        else:
                            instr = v.tensor_copy(hbuf[:, 0, :].bitcast(F32)[:, 0:256], psy[r % 2][:, 0:256])
                        sems.inc(instr, SV)

    if STAGES < 5:
        return sems
    with nc.Block() as blk:
        v0, s0 = sems.val(SV), sems.val(SS)
        x0, y0 = sems.val("p4x"), sems.val("p4y")
        sums, sumsq = stats[:, 0, :], stats[:, 1, :]
        meanE, varE, rstdE = stats[:, 2, :], stats[:, 3, :], stats[:, 4, :]
        NS = NT + 6
        _RD, _VA, _AP2, _SQ, _RS2 = {}, {}, {}, {}, {}
        vc, sc_ = v0, s0
        for i in range(NS):
            if 0 <= i < NT:
                vc += 1; _RD[i] = vc
            if 0 <= i - 2 < NT:
                vc += 1; _VA[i - 2] = vc
            if 0 <= i - 4 < NT:
                vc += 1; _AP2[i - 4] = vc
        for i in range(NS):
            if 0 <= i - 1 < NT:
                sc_ += 1; _SQ[i - 1] = sc_
            if 0 <= i - 3 < NT:
                sc_ += 1; _RS2[i - 3] = sc_

        @blk.sync
        def _(s):
            for t in range(NT):
                if t >= 2:
                    sems.wait(s, SV, _AP2[t - 2])
                sems.inc(s.dma_start(xstage[:, t % 2, :],
                                     x_reload_src[t, :, :]), "p4x", 16)

        @blk.vector
        def _(v):
            v = VP(v)
            for i in range(NS):
                if 0 <= i < NT:
                    t = i
                    instr = v.tensor_reduce(sums[:, t:t + 1], y_acc[:, t, :], AX.X, Op.add)
                    sems.inc(instr, SV)
                if 0 <= i - 2 < NT:
                    t = i - 2
                    sems.wait(v, SS, _SQ[t])
                    v.tensor_scalar(meanE[:, t:t + 1], sums[:, t:t + 1], 1.0 / D, None, Op.mult)
                    v.tensor_scalar(varE[:, t:t + 1], sumsq[:, t:t + 1], 1.0 / D, None, Op.mult)
                    v.tensor_tensor(sums[:, t:t + 1], meanE[:, t:t + 1],
                                    meanE[:, t:t + 1], Op.mult)
                    instr = v.tensor_tensor(varE[:, t:t + 1], varE[:, t:t + 1],
                                            sums[:, t:t + 1], Op.subtract)
                    sems.inc(instr, SV)
                if 0 <= i - 4 < NT:
                    t = i - 4
                    sems.wait(v, SS, _RS2[t])
                    sems.wait(v, "p4x", x0 + 16 * (t + 1))
                    if t >= 2:
                        sems.wait(v, "p4y", y0 + 16 * (t - 1))
                    v.tensor_scalar(ostage[:, t % 2, :], y_acc[:, t, :],
                                    meanE[:, t:t + 1], rstdE[:, t:t + 1],
                                    Op.subtract, Op.mult)
                    instr = v.tensor_tensor(ostage[:, t % 2, :], ostage[:, t % 2, :],
                                            xstage[:, t % 2, :], Op.add)
                    sems.inc(instr, SV)

        @blk.scalar
        def _(sc):
            for i in range(NS):
                if 0 <= i - 1 < NT:
                    t = i - 1
                    instr = sc.activation(bufA[:, 0, :], y_acc[:, t, :], AF.Square,
                                          accum_out=sumsq[:, t:t + 1])
                    sc.drain()
                    sems.inc(instr, SS)
                if 0 <= i - 3 < NT:
                    t = i - 3
                    sems.wait(sc, SV, _VA[t])
                    sc.activation(rstdE[:, t:t + 1], varE[:, t:t + 1], AF.Ln, bias=epsc)
                    instr = sc.activation(rstdE[:, t:t + 1], rstdE[:, t:t + 1],
                                          AF.Exp, scale=-0.5)
                    sems.inc(instr, SS)

        @blk.gpsimd
        def _(g):
            for t in range(NT):
                sems.wait(g, SV, _AP2[t])
                sems.inc(g.dma_start(x_out_dst[t, :, :], ostage[:, t % 2, :]),
                         "p4y", 16)
            g.wait_ge(sems.h["p4y"], sems.val("p4y"))

    return sems

def emit_postln_pipe(nc, blk, sems, tag, mm_vals, psum, tile_ids, xbuf, scr3,
                     sqdump, stats, epsc, nt):
    SV, SS = f"v_{tag}", f"s_{tag}"
    v0, s0 = sems.val(SV), sems.val(SS)
    sums, sumsq = stats[:, 0, :], stats[:, 1, :]
    mean, var, rstd = stats[:, 2, :], stats[:, 3, :], stats[:, 4, :]
    _EV, _VA, _AP, _SQ, _RS = {}, {}, {}, {}, {}
    vc, sc_ = v0, s0
    for i in range(nt + 3):
        if 0 <= i - 1 < nt:
            vc += 1; _EV[i - 1] = vc
        if 0 <= i - 2 < nt:
            vc += 1; _VA[i - 2] = vc
        if 0 <= i - 3 < nt:
            vc += 1; _AP[i - 3] = vc
    for i in range(nt + 3):
        if 0 <= i - 2 < nt:
            sc_ += 1; _SQ[i - 2] = sc_
        if 0 <= i - 3 < nt:
            sc_ += 1; _RS[i - 3] = sc_

    @blk.vector
    def _(v):
        v = VP(v)
        for i in range(nt + 3):
            if 0 <= i - 1 < nt:
                t = i - 1
                sems.wait(v, f"t_{tag}", mm_vals[t])
                instr = v.tensor_scalar(scr3[:, t % 3, :], psum[t % 2][:, :], 1.0,
                                        None, Op.mult, Op.add,
                                        accum_out=sums[:, t:t + 1])
                sems.inc(instr, SV)
            if 0 <= i - 2 < nt:
                t = i - 2
                sems.wait(v, SS, _SQ[t])
                v.tensor_scalar(mean[:, t:t + 1], sums[:, t:t + 1], 1.0 / D, None, Op.mult)
                v.tensor_scalar(var[:, t:t + 1], sumsq[:, t:t + 1], 1.0 / D, None, Op.mult)
                v.tensor_tensor(sums[:, t:t + 1], mean[:, t:t + 1], mean[:, t:t + 1], Op.mult)
                instr = v.tensor_tensor(var[:, t:t + 1], var[:, t:t + 1],
                                        sums[:, t:t + 1], Op.subtract)
                sems.inc(instr, SV)
            if 0 <= i - 3 < nt:
                t = i - 3
                sems.wait(v, SS, _RS[t])
                v.tensor_scalar(scr3[:, t % 3, :], scr3[:, t % 3, :],
                                mean[:, t:t + 1], rstd[:, t:t + 1],
                                Op.subtract, Op.mult)
                instr = v.tensor_tensor(xbuf[:, tile_ids[t], :], xbuf[:, tile_ids[t], :],
                                        scr3[:, t % 3, :], Op.add)
                sems.inc(instr, SV)

    @blk.scalar
    def _(sc):
        for i in range(nt + 3):
            if 0 <= i - 2 < nt:
                t = i - 2
                sems.wait(sc, SV, _EV[t])
                instr = sc.activation(sqdump, scr3[:, t % 3, :], AF.Square,
                                      accum_out=sumsq[:, t:t + 1])
                sc.drain()
                sems.inc(instr, SS)
            if 0 <= i - 3 < nt:
                t = i - 3
                sems.wait(sc, SV, _VA[t])
                sc.activation(rstd[:, t:t + 1], var[:, t:t + 1], AF.Ln, bias=epsc)
                instr = sc.activation(rstd[:, t:t + 1], rstd[:, t:t + 1], AF.Exp, scale=-0.5)
                sems.inc(instr, SS)
    return _EV


def _prep_shared(inputs):
    t = {}
    t["p2_Win"] = np.ascontiguousarray(inputs["L0_ffn_Win"])
    t["p2_binr"] = np.ascontiguousarray(inputs["L0_ffn_bin_"][None, :])
    t["p2_W1b"] = np.ascontiguousarray(
        np.tile(inputs["L0_ffn_W1"].reshape(1, D), (P, 1)))
    t["p2_b1b"] = np.ascontiguousarray(
        np.tile(inputs["L0_ffn_b1"].reshape(1, D), (P, 1)))
    t["p2_W2b"] = np.ascontiguousarray(
        np.tile(inputs["L0_ffn_W2"].reshape(1, D), (P, 1)))
    t["p2_b2b"] = np.ascontiguousarray(
        np.tile(inputs["L0_ffn_b2"].reshape(1, 64), (P, 1)))
    t["p2_Wout"] = np.ascontiguousarray(inputs["L0_ffn_Wout"])
    t["p2_boutr"] = np.ascontiguousarray(inputs["L0_ffn_bout"][None, :])
    t["p1_WqkvT"] = np.ascontiguousarray(inputs["L0_attn_Wqkv"].T)
    t["p1_WprojT"] = np.ascontiguousarray(inputs["L0_attn_Wproj"].T)
    t["p1_bprojr"] = np.ascontiguousarray(inputs["L0_attn_bproj"][None, :])
    t["p3_w1T"] = np.ascontiguousarray(inputs["L1_attn_w1"].T)
    t["p3_wwT"] = np.ascontiguousarray(inputs["L1_attn_ww"].T)
    t["p3_w2T"] = np.ascontiguousarray(inputs["L1_attn_w2"].T)
    t["p3_b1r"] = np.ascontiguousarray(inputs["L1_attn_b1"][None, :])
    t["p3_bwr"] = np.ascontiguousarray(inputs["L1_attn_bw"][None, :])
    t["p3_b2r"] = np.ascontiguousarray(inputs["L1_attn_b2"][None, :])
    t["p3_gb"] = np.ascontiguousarray(np.tile(inputs["L1_attn_gamma"][None, :], (P, 1)))
    t["p3_bb"] = np.ascontiguousarray(np.tile(inputs["L1_attn_beta"][None, :], (P, 1)))
    t["p4_Wg"] = np.ascontiguousarray(inputs["L1_ffn_Wg"])
    t["p4_bgr"] = np.ascontiguousarray(inputs["L1_ffn_bg"][None, :])
    t["p4_b2s"] = np.ascontiguousarray(inputs["L1_ffn_b2"])
    t["p4_W1"] = np.ascontiguousarray(inputs["L1_ffn_W1"])
    t["p4_W2"] = np.ascontiguousarray(inputs["L1_ffn_W2"])
    t["p4_b1"] = np.ascontiguousarray(inputs["L1_ffn_b1"])
    t["ones_row"] = np.ones((1, P), dtype=BF)
    t["id_bf"] = np.eye(P, dtype=BF)
    t["id_f32"] = np.eye(P, dtype=np.float32)
    return t


PHASE_KEYS = {
    1: ["p1_WqkvT", "p1_WprojT", "p1_bprojr"],
    2: ["p2_Win", "p2_binr", "p2_W1b", "p2_b1b", "p2_W2b", "p2_b2b",
        "p2_Wout", "p2_boutr"],
    3: ["p3_w1T", "p3_wwT", "p3_w2T", "p3_b1r", "p3_bwr", "p3_b2r",
        "p3_gb", "p3_bb"],
    4: ["p4_Wg", "p4_bgr", "p4_b2s", "p4_W1", "p4_W2", "p4_b1"],
}
COMMON_KEYS = ["ones_row", "id_bf", "id_f32"]


def _phase_weight_aps(nc, t):
    aps = {}
    for k, v in t.items():
        dt = BF16 if v.dtype == BF else F32
        aps[k] = nc.dram_tensor(k, list(v.shape), dt, kind="ExternalInput").ap()
    return aps


def _common_env(nc, st, aps):
    env = {}
    env["xbuf"] = st.enter_context(nc.sbuf_tensor("xbuf", [P, NT, D], F32))
    env["bufA"] = st.enter_context(nc.sbuf_tensor("bufA", [P, NT, D], BF16))
    env["bufB"] = st.enter_context(nc.sbuf_tensor("bufB", [P, 8, 2048], BF16))
    env["ones_row"] = st.enter_context(nc.sbuf_tensor("c_ones", [1, P], BF16))
    env["id_bf"] = st.enter_context(nc.sbuf_tensor("c_idbf", [P, P], BF16))
    env["id_f32"] = st.enter_context(nc.sbuf_tensor("c_idf32", [P, P], F32))
    env["epsc"] = st.enter_context(nc.sbuf_tensor("c_eps", [P, 1], F32))
    return env


def _wire_weights(aps, phase):
    if phase == 1:
        return {"WqkvT": aps["p1_WqkvT"], "WprojT": aps["p1_WprojT"],
                "bprojr": aps["p1_bprojr"], "mask1": aps["mask1"],
                "ropeq": aps["ropeq"], "ropek": aps["ropek"]}
    if phase == 2:
        return {"Win": aps["p2_Win"], "binr": aps["p2_binr"],
                "W1b": aps["p2_W1b"], "b1b": aps["p2_b1b"],
                "W2b": aps["p2_W2b"], "b2b": aps["p2_b2b"],
                "Wout": aps["p2_Wout"], "boutr": aps["p2_boutr"]}
    if phase == 3:
        return {"w1T": aps["p3_w1T"], "wwT": aps["p3_wwT"], "w2T": aps["p3_w2T"],
                "b1r": aps["p3_b1r"], "bwr": aps["p3_bwr"], "b2r": aps["p3_b2r"],
                "gb": aps["p3_gb"], "bb": aps["p3_bb"]}
    if phase == 4:
        return {"Wg": aps["p4_Wg"], "bgr": aps["p4_bgr"], "b2s": aps["p4_b2s"],
                "W1": aps["p4_W1"], "W2": aps["p4_W2"], "b1": aps["p4_b1"]}
    raise NotImplementedError(phase)


def build_phase_program(phase, shared):
    nc = bass.Bass("TRN2", target_bir_lowering=False)
    xin = nc.dram_tensor("xin", [NT, P, D], F32, kind="ExternalInput").ap()
    xout = nc.dram_tensor("xout", [NT, P, D], F32, kind="ExternalOutput").ap()
    keys = COMMON_KEYS + PHASE_KEYS[phase]
    aps = _phase_weight_aps(nc, {k: shared[k] for k in keys})
    if phase == 1:
        aps["mask1"] = nc.dram_tensor("mask1", [4, P, 640], F32, kind="ExternalInput").ap()
        aps["ropeq"] = nc.dram_tensor("ropeq", [2, 4, P, 64], F32, kind="ExternalInput").ap()
        aps["ropek"] = nc.dram_tensor("ropek", [2, 8, P, 64], F32, kind="ExternalInput").ap()
        xh = nc.dram_tensor("xh", [NT, P, D], F32, kind="ExternalInput").ap()

    st = ExitStack()
    with st:
        env = _common_env(nc, st, aps)
        env[f"w{phase}"] = _wire_weights(aps, phase)
        gsems = Sems(nc, st, ["io"])
        with nc.Block() as blk:
            @blk.sync
            def _(s):
                gsems.inc(s.dma_start(env["xbuf"][:, :, :], xin.rearrange("t p d -> p t d")), "io", 16)
                gsems.inc(s.dma_start(env["ones_row"][:, :], aps["ones_row"]), "io", 16)
                gsems.inc(s.dma_start(env["id_bf"][:, :], aps["id_bf"]), "io", 16)
                gsems.inc(s.dma_start(env["id_f32"][:, :], aps["id_f32"]), "io", 16)
                s.wait_ge(gsems.h["io"], gsems.val("io"))

            @blk.vector
            def _(v):
                v = VP(v)
                v.memset(env["epsc"][:, :], EPS)

        if phase == 1:
            emit_phase1(nc, st, env, xh)
        elif phase == 2:
            emit_phase2(nc, st, env)
        elif phase == 3:
            emit_phase3(nc, st, env)
        elif phase == 4:
            emit_phase4(nc, st, env, xin, xout, spill_dst=None)
        else:
            raise NotImplementedError(phase)

        if phase != 4:
            with nc.Block() as blk:
                @blk.sync
                def _(s):
                    instr = s.dma_start(xout.rearrange("t p d -> p t d"),
                                        env["xbuf"][:, :, :])
                    gsems.inc(instr, "io", 16)
                    s.wait_ge(gsems.h["io"], gsems.val("io"))
    return nc


def _shard_x(x_full):
    shards = []
    for k in range(NCORES):
        xc = x_full[:, k * SLOC:(k + 1) * SLOC, :]
        shards.append(np.ascontiguousarray(
            xc.reshape(B, SLOC // P, P, D).reshape(NT, P, D)))
    return shards


def _unshard_x(shards):
    out = np.empty((B, SEQ, D), dtype=np.float32)
    for k, sh in enumerate(shards):
        out[:, k * SLOC:(k + 1) * SLOC, :] = sh.reshape(B, SLOC // P, P, D).reshape(
            B, SLOC, D)
    return out


def _percore_p1(k, x_full):
    out = {}
    if k == 0:
        xh = np.zeros((B, SLOC, D), dtype=np.float32)
    else:
        xh = np.ascontiguousarray(x_full[:, (k - 1) * SLOC:k * SLOC, :])
    out["xh"] = np.ascontiguousarray(
        xh.reshape(B, SLOC // P, P, D).reshape(NT, P, D))
    ql = np.arange(P)[:, None]
    cc = np.arange(640)[None, :]
    mask = np.empty((4, P, 640), dtype=np.float32)
    for g in range(4):
        allow = (cc >= ql) & (cc <= ql + 512)
        jglob = 512 * (k - 1) + 128 * g + cc
        allow = allow & (jglob >= 0)
        mask[g] = np.where(allow, 1.0, 0.0).astype(np.float32)
    out["mask1"] = mask
    half = HD // 2
    inv_freq = (1.0 / 10000.0 ** (np.arange(half, dtype=np.float32) / half)).astype(np.float32)
    sgn = np.empty(HD, dtype=np.float32)
    sgn[0::2], sgn[1::2] = -1.0, 1.0

    def tables(pos):
        ang = pos[:, :, None].astype(np.float32) * inv_freq[None, None, :]
        cosr = np.repeat(np.cos(ang), 2, axis=-1).astype(np.float32)
        sinr = np.repeat(np.sin(ang), 2, axis=-1).astype(np.float32) * sgn
        return cosr, sinr

    posq = 512 * k + 128 * np.arange(4)[:, None] + np.arange(P)[None, :]
    cosr, sinr = tables(posq)
    out["ropeq"] = np.ascontiguousarray(
        np.stack([cosr, sinr]) * np.float32(0.125))
    posk = 512 * (k - 1) + 128 * np.arange(8)[:, None] + np.arange(P)[None, :]
    cosr, sinr = tables(posk)
    out["ropek"] = np.ascontiguousarray(np.stack([cosr, sinr]))
    return out


def run_phase(phase, x_full, inputs):
    shared = _prep_shared(inputs)
    nc = build_phase_program(phase, shared)
    shards = _shard_x(x_full)
    keys = COMMON_KEYS + PHASE_KEYS[phase]
    in_maps = []
    for k in range(NCORES):
        m = {kk: shared[kk] for kk in keys}
        m["xin"] = shards[k]
        if phase == 1:
            m.update(_percore_p1(k, x_full))
        in_maps.append(m)
    res = run_bass_kernel_spmd(nc, in_maps, core_ids=list(range(NCORES)))
    return _unshard_x([r["xout"] for r in res.results])


def build_fused_program(shared):
    nc = bass.Bass("TRN2", target_bir_lowering=False)
    xin = nc.dram_tensor("xin", [NT, P, D], F32, kind="ExternalInput").ap()
    xout = nc.dram_tensor("xout", [NT, P, D], F32, kind="ExternalOutput").ap()
    spill = nc.dram_tensor("spill", [NT, P, D], F32, kind="Internal").ap()
    keys = COMMON_KEYS + sum((PHASE_KEYS[p] for p in (1, 2, 3, 4)), [])
    aps = _phase_weight_aps(nc, {k: shared[k] for k in keys})
    aps["mask1"] = nc.dram_tensor("mask1", [4, P, 640], F32, kind="ExternalInput").ap()
    aps["ropeq"] = nc.dram_tensor("ropeq", [2, 4, P, 64], F32, kind="ExternalInput").ap()
    aps["ropek"] = nc.dram_tensor("ropek", [2, 8, P, 64], F32, kind="ExternalInput").ap()
    xh = nc.dram_tensor("xh", [NT, P, D], F32, kind="ExternalInput").ap()

    st = ExitStack()
    with st:
        env = _common_env(nc, st, aps)
        for p in (1, 2, 3, 4):
            env[f"w{p}"] = _wire_weights(aps, p)
        gsems = Sems(nc, st, ["io"])
        with nc.Block() as blk:
            @blk.sync
            def _(s):
                gsems.inc(s.dma_start(env["xbuf"][:, :, :], xin.rearrange("t p d -> p t d")), "io", 16)
                gsems.inc(s.dma_start(env["ones_row"][:, :], aps["ones_row"]), "io", 16)
                gsems.inc(s.dma_start(env["id_bf"][:, :], aps["id_bf"]), "io", 16)
                gsems.inc(s.dma_start(env["id_f32"][:, :], aps["id_f32"]), "io", 16)
                s.wait_ge(gsems.h["io"], gsems.val("io"))

            @blk.vector
            def _(v):
                v = VP(v)
                v.memset(env["epsc"][:, :], EPS)

        with ExitStack() as pstk:
            emit_phase1(nc, pstk, env, xh)
        with ExitStack() as pstk:
            emit_phase2(nc, pstk, env)
        with ExitStack() as pstk:
            emit_phase3(nc, pstk, env)
        with ExitStack() as pstk:
            emit_phase4(nc, pstk, env, spill, xout, spill_dst=spill)
    return nc


def run_fused(x_full, inputs):
    shared = _prep_shared(inputs)
    nc = build_fused_program(shared)
    shards = _shard_x(x_full)
    keys = COMMON_KEYS + sum((PHASE_KEYS[p] for p in (1, 2, 3, 4)), [])
    in_maps = []
    for k in range(NCORES):
        m = {kk: shared[kk] for kk in keys}
        m["xin"] = shards[k]
        m.update(_percore_p1(k, x_full))
        in_maps.append(m)
    res = run_bass_kernel_spmd(nc, in_maps, core_ids=list(range(NCORES)))
    return _unshard_x([r["xout"] for r in res.results])


def kernel(**inputs):
    x = np.asarray(inputs["L0_attn_x"], dtype=np.float32)
    return run_fused(x, inputs)
